# version 04
# baseline (speedup 1.0000x reference)
import math
import numpy as np
import ml_dtypes
import concourse.bass as bass
import concourse.mybir as mybir
from concourse.bass_utils import run_bass_kernel_spmd
from contextlib import ExitStack

F32 = mybir.dt.float32
BF16 = mybir.dt.bfloat16
AF = mybir.ActivationFunctionType
ALU = mybir.AluOpType
AX = mybir.AxisListType

MAGIC = float(2 ** 23 + 2 ** 22)
NEGBIG = -8.0e30
LN_EPS = 1e-5

STAGE = 4
DBG = {}
DBGHW = False
DBG_SIM = False
DBG_NO_M = False
DBG_NO_D = False
RACECHECK = True


def _bucket(d):
    if d < 16:
        return d
    v = 16 + int(np.float32(np.log(np.float32(d) / 16 + 1e-10) / math.log(8.0) * 16))
    return min(v, 31)


def _make_hh():
    hh0 = np.zeros((33, 128 * 128), dtype=np.float32)
    hh1 = np.zeros((33, 128 * 128), dtype=np.float32)
    for k in range(128):
        for q in range(128):
            i = k * 128 + q
            d0 = q - k
            if d0 >= 0:
                hh0[_bucket(d0), i] = 1.0
            else:
                hh0[32, i] = 1.0
            hh1[_bucket(d0 + 128), i] = 1.0
    return hh0.astype(ml_dtypes.bfloat16), hh1.astype(ml_dtypes.bfloat16)


def _col(v, n):
    return np.ascontiguousarray(np.asarray(v, np.float32).reshape(n, 128).T)


ENGS = ("T", "V", "S", "Y", "G")


class EM:

    def __init__(self):
        self.steps = []
        self.cnt = {}

    def emit(self, eng, fn, waits=(), inc=False, tag=None):
        t = tag if tag is not None else eng
        if eng in ("Y", "G"):
            assert t not in ("Y", "G"), "DMA steps need an explicit tag"
        n = (16 if eng in ("Y", "G") else 1) if inc else 0
        self.cnt[t] = self.cnt.get(t, 0) + n
        self.steps.append((eng, fn, tuple(waits), n, t))
        return self.cnt[t]

    def now(self, tag):
        return self.cnt.get(tag, 0)


def build(nc, stage=4):
    D, T = 1024, 2048

    def din(name, shape, dt=F32):
        return nc.dram_tensor(name, list(shape), dt, kind="ExternalInput").ap()

    dr = {k: din(k, s, d) for k, s, d in [
        ("xT", (D, T), F32),
        ("wqkvT0", (D, 3 * D), F32), ("wprojT0", (D, D), F32),
        ("bproj0", (128, 8), F32), ("gamma0", (128, 8), F32), ("beta0", (128, 8), F32),
        ("csT", (2, 1152), F32),
        ("w1T0", (D, 4096), F32), ("cw0", (128, 96), F32), ("cb0", (128, 32), F32),
        ("w2T0", (4096, D), F32),
        ("wqkvT1", (D, 3 * D), F32), ("battn1", (128, 16), F32), ("bv1", (1, D), F32),
        ("wprojT1", (D, D), F32), ("bproj1", (128, 8), F32), ("relemb", (32, 16), F32),
        ("w1T1", (D, 4096), F32), ("cw1", (128, 96), F32), ("cb1", (128, 32), F32),
        ("w2T1", (4096, D), F32),
        ("ident", (128, 128), BF16), ("identf", (128, 128), F32),
        ("hh0", (33, 16384), BF16), ("hh1", (33, 16384), BF16),
    ]}

    yT = nc.dram_tensor("yT", [D, T], F32, kind="ExternalOutput").ap()
    xh = [nc.dram_tensor(f"xh{i}", [D, T], F32, kind="Internal").ap() for i in range(2)]
    hsil_d = nc.dram_tensor("hsil_d", [32, 128, T], BF16, kind="Internal").ap()
    w2q_d = nc.dram_tensor("w2q_d", [32, 128, D], BF16, kind="Internal").ap()
    tflat_d = nc.dram_tensor("tflat_d", [2, 8, 16, 2048], BF16, kind="Internal").ap()
    row_d = nc.dram_tensor("row_d", [8, T], F32, kind="Internal").ap()

    es = ExitStack()

    def sb(name, shape, dt):
        return es.enter_context(nc.sbuf_tensor(name, list(shape), dt))

    PX = sb("PX", (128, 16384), F32)
    PA = sb("PA", (128, 16384), BF16)
    PB = sb("PB", (128, 16640), BF16)
    PO = sb("PO", (128, 16384), BF16)
    PD = sb("PD", (128, 9252), BF16)
    WS = sb("WS", (128, 2048), F32)
    SCR = sb("SCR", (128, 2048), F32)
    SMB = sb("SMB", (128, 1024), BF16)
    SMF = sb("SMF", (128, 768), F32)
    SM2 = sb("SM2", (33, 64), F32)
    ROWS = sb("ROWS", (128, 2048), F32)

    ps0 = es.enter_context(nc.psum_tensor("ps0", [128, 2048], F32))
    ps1 = es.enter_context(nc.psum_tensor("ps1", [128, 2048], F32))

    PXb = PX.ap().bitcast(BF16)
    PBf = PB.ap().bitcast(F32)
    POf = PO.ap().bitcast(F32)
    PAf = PA.ap().bitcast(F32)
    PDf = PD.ap().bitcast(F32)
    WSb = WS.ap().bitcast(BF16)
    SCRb = SCR.ap().bitcast(BF16)
    FLT = SCRb[0:16, 0:2048]
    W1Q = PB.ap()[:, 4096:6144]

    IDENT = SMB.ap()[:, 0:128]
    ONESB = SMB.ap()[:, 128:129]
    ONER = SMB.ap()[0:1, 130:258]
    DIAG = [SMB.ap()[:, 258 + i * 128: 386 + i * 128] for i in range(3)]
    GAMB = SMB.ap()[:, 672:680]
    IDF = SMF.ap()[:, 640:768]

    ONERF = SMF.ap()[0:1, 0:128]
    BPB0 = SMF.ap()[:, 128:136]
    GAM0 = SMF.ap()[:, 136:144]
    NGAM0 = SMF.ap()[:, 144:152]
    BT1 = SMF.ap()[:, 152:168]
    BP1 = SMF.ap()[:, 176:184]
    CB = [SMF.ap()[:, 184 + i * 32: 216 + i * 32] for i in range(2)]
    CW = [SMF.ap()[:, 248 + i * 96: 344 + i * 96] for i in range(2)]
    NEGM = SMF.ap()[:, 440:456]
    AMC = SMF.ap()[:, 456:472]
    WSR = [SMF.ap()[:, 472 + i: 473 + i] for i in range(2)]
    STK = SMF.ap()[:, 474:490]
    NORC = SMF.ap()[:, 490:506]
    FREP = SMF.ap()[:, 506:507]
    BETA0 = SMF.ap()[:, 507:515]
    EPSR = SMF.ap()[0:1, 515:516]
    C30R = SMF.ap()[0:1, 516:517]
    ONESFC = SMF.ap()[:, 522:523]
    C1 = SMF.ap()[:, 520:521]
    C2 = SMF.ap()[:, 521:522]

    CST = POf[0:2, 4864:6016]
    CSN2 = POf[0:2, 1280:2432]
    CS2 = POf[0:2, 2560:3712]
    NORR = POf[0:1, 3712:4864]
    R0 = ROWS.ap()[0:1, :]
    R_RS0 = ROWS.ap()[32:33, :]
    R_M0R0 = ROWS.ap()[64:65, :]
    R96 = ROWS.ap()[96:97, :]

    em = EM()
    E = em.emit

    from concourse.hw_specs import get_activation_tables
    _set_ids = {name: i for i, name in enumerate(get_activation_tables(nc.m.arch))}

    def load_actset(name, waits=()):
        sid = _set_ids[name]
        return E("S", lambda e, sid=sid: e.add_instruction(
            mybir.InstLoadActFuncSet(
                name=nc.get_next_instruction_name(), act_func_set_id=sid,
                ins=[], outs=[])), waits, inc=False)

    def dump(name, ap, sim_ap="same"):
        if not DBGHW:
            return
        if DBG_SIM:
            if sim_ap is None:
                return
            if sim_ap != "same":
                ap = sim_ap
        d = nc.dram_tensor("dbg_" + name, [ap.shape[0], ap.shape[1]], ap.dtype,
                           kind="ExternalOutput").ap()
        w = tuple((k, v) for k, v in em.cnt.items() if v > 0)
        E("Y", lambda e, d=d, ap=ap: e.dma_start(d, ap), w, inc=True, tag="ydbg")

    def ph_init():
        for dst, src in [(IDENT, dr["ident"]), (IDF, dr["identf"]),
                         (BPB0, dr["bproj0"]), (GAM0, dr["gamma0"]),
                         (BETA0, dr["beta0"]), (BT1, dr["battn1"]),
                         (BP1, dr["bproj1"]), (CB[0], dr["cb0"]), (CB[1], dr["cb1"]),
                         (CW[0], dr["cw0"]), (CW[1], dr["cw1"]), (CST, dr["csT"]),
                         (SM2.ap()[0:32, 0:16], dr["relemb"])]:
            E("Y", lambda e, dst=dst, src=src: e.dma_start(dst, src), inc=True,
              tag="yinit")
        yw = em.now("yinit")
        E("V", lambda e: e.memset(ONESB, 1.0))
        E("V", lambda e: e.memset(ONER, 1.0))
        E("V", lambda e: e.memset(ONERF, 1.0))
        E("V", lambda e: e.memset(EPSR, LN_EPS))
        E("V", lambda e: e.memset(C30R, 1e-30))
        E("V", lambda e: e.memset(ONESFC, 1.0))
        E("V", lambda e: e.tensor_tensor(BPB0, BPB0, BETA0, ALU.add),
          (("yinit", yw),))
        E("V", lambda e: e.tensor_copy(GAMB, GAM0))
        E("V", lambda e: e.tensor_scalar(NGAM0, GAM0, -1.0, None, ALU.mult), inc=True)
        return {"init_v": em.now("V"), "init_y": yw}

    def ph_dbuild(ev):
        E("V", lambda e: e.tensor_scalar(CSN2, CST, -2.0, None, ALU.mult),
          (("yinit", ev["init_y"]),))
        vcs = E("V", lambda e: e.tensor_tensor(CS2, CST, CST, ALU.mult), inc=True)
        for j, w_ in ((0, 512), (512, 512), (1024, 128)):
            E("T", lambda e, j=j, w_=w_: e.matmul(
                ps1[0:1, j: j + w_], ONESFC[0:2, :], CS2[:, j:j + w_],
                start=True, stop=True), (("V", vcs),) if j == 0 else (),
              inc=(j == 1024))
        tno = em.now("T")
        E("V", lambda e: e.tensor_copy(NORR, ps1[0:1, 0:1152]), (("T", tno),),
          inc=True)
        vnorm = em.now("V")
        ynorr = E("Y", lambda e: e.dma_start(row_d[0:1, 0:1152], NORR),
                  (("V", vnorm),), inc=True, tag="yrow")
        ynorc = E("Y", lambda e: e.dma_start(
            NORC[:, 0:9],
            row_d[0:1, 0:1152].rearrange("o (c p) -> (o p) c", p=128),
            allow_slow_non_contiguous=True), (("yrow", ynorr),), inc=True,
            tag="yrow")
        E("V", lambda e: e.memset(STK[:, 0:9], -3.0e38), inc=True)
        vstk = em.now("V")
        E("S", lambda e: e.preload_act_table(AF.Sqrt))
        vprev = 0
        for kc in range(9):
            qn = [(0, 512), (512, 512), (1024, 1)]
            for qi, (q0, qw) in enumerate(qn):
                w = []
                if qi == 0:
                    w = [("V", vnorm)] if kc == 0 else [("V", vprev)]
                E("T", lambda e, kc=kc, q0=q0, qw=qw, qi=qi: e.matmul(
                    ps0[:, q0:q0 + qw], CST[:, kc * 128:(kc + 1) * 128],
                    CSN2[:, q0:q0 + qw], start=(True if qi == 0 else True), stop=False),
                  tuple(w))
                E("T", lambda e, q0=q0, qw=qw, qi=qi: e.matmul(
                    ps0[:, q0:q0 + qw], ONERF, NORR[:, q0:q0 + qw],
                    start=False, stop=True), inc=(qi == len(qn) - 1))
            tmm = em.now("T")
            E("V", lambda e, kc=kc: e.tensor_scalar(
                POf[:, 0:1025], ps0[:, 0:1025], NORC[:, kc:kc + 1], 1e-30,
                ALU.add, ALU.max),
              (("T", tmm), ("yrow", ynorc), ("S", em.now("S"))), inc=True)
            vd2 = em.now("V")
            E("V", lambda e, kc=kc: e.tensor_reduce(
                STK[:, kc:kc + 1], POf[:, 0:1025], axis=AX.X, op=ALU.max),
              (("V", vstk), ("V", vd2)), inc=True)
            vprev = em.now("V")
            E("S", lambda e, kc=kc: e.activation(
                PD.ap()[:, kc * 1028: kc * 1028 + 1025], POf[:, 0:1025], AF.Sqrt),
              (("V", vd2),), inc=True)
        sqd = em.now("S")
        vred = E("V", lambda e: e.tensor_reduce(
            STK[:, 15:16], STK[:, 0:9], axis=AX.X, op=ALU.max),
            (("V", em.now("V")),), inc=True)
        tmx = E("T", lambda e: e.transpose(ps1[0:1, 0:128], STK[:, 15:16], IDF),
                (("V", vred),), inc=True)
        E("V", lambda e: e.tensor_reduce(STK[0:1, 14:15], ps1[0:1, 0:128], axis=AX.X,
                                         op=ALU.max), (("T", tmx),), inc=True)
        smx = E("S", lambda e: e.activation(STK[0:1, 13:14], STK[0:1, 14:15], AF.Sqrt),
                (("V", em.now("V")),), inc=True)
        E("V", lambda e: e.tensor_scalar(STK[0:1, 12:13], STK[0:1, 13:14], 1e-6, None,
                                         ALU.max), (("S", smx),), inc=True)
        E("V", lambda e: e.reciprocal(STK[0:1, 11:12], STK[0:1, 12:13]),
          (("V", em.now("V")),), inc=True)
        E("V", lambda e: e.tensor_scalar(STK[0:1, 10:11], STK[0:1, 13:14], 1e-6, None,
                                         ALU.is_gt), inc=True)
        E("V", lambda e: e.tensor_tensor(STK[0:1, 9:10], STK[0:1, 11:12],
                                         STK[0:1, 10:11], ALU.mult),
          (("V", em.now("V")),), inc=True)
        vf = E("V", lambda e: e.tensor_scalar(STK[0:1, 9:10], STK[0:1, 9:10], -0.8,
                                              None, ALU.mult),
               (("V", em.now("V")),), inc=True)
        tfr = E("T", lambda e: e.matmul(ps1[:, 128:129], ONERF, STK[0:1, 9:10],
                                        start=True, stop=True), (("V", vf),), inc=True)
        vfr = E("V", lambda e: e.tensor_copy(FREP, ps1[:, 128:129]), (("T", tfr),),
                inc=True)
        for kc in range(9):
            E("V", lambda e, kc=kc: e.tensor_scalar(
                PD.ap()[:, kc * 1028: kc * 1028 + 1025],
                PD.ap()[:, kc * 1028: kc * 1028 + 1025], FREP, None, ALU.mult),
              (("S", sqd), ("V", vfr)), inc=(kc == 8))
        E("V", lambda e: e.tensor_copy(SCRb[0:1, 0:1025],
                                       PD.ap()[0:1, 8 * 1028: 8 * 1028 + 1025]),
          (("V", em.now("V")),), inc=True)
        E("V", lambda e: e.memset(PD.ap()[:, 8 * 1028: 8 * 1028 + 1025], 0.0),
          (("V", em.now("V")),), inc=True)
        E("V", lambda e: e.tensor_copy(PD.ap()[0:1, 8 * 1028: 8 * 1028 + 1025],
                                       SCRb[0:1, 0:1025]),
          (("V", em.now("V")),), inc=True)
        for kc in range(9):
            E("V", lambda e, kc=kc: e.memset(PD.ap()[:, kc * 1028: kc * 1028 + 1], 0.0),
              (("V", em.now("V")),) if kc == 0 else (), inc=(kc == 8))
        E("V", lambda e: e.memset(PD.ap()[0:1, 0:1025], 0.0),
          (("V", em.now("V")),), inc=True)
        return {"bias_ready": [("V", em.now("V"))],
                "ps_free": [("V", em.now("V")), ("S", em.now("S"))],
                "po_free": [("V", em.now("V")), ("S", em.now("S")),
                            ("T", em.now("T"))]}

    def ph_tbuild(ev, pre=()):
        relb = SM2.ap().bitcast(BF16)
        y31 = E("Y", lambda e: e.dma_start(POf[0:1, 3072:3088],
                                           SM2.ap()[31:32, 0:16]),
                tuple(pre) + (("yinit", ev["init_y"]), ("V", ev["init_v"])),
                inc=True, tag="y31")
        t31 = E("T", lambda e: e.matmul(ps1[0:32, 0:16], ONERF[0:1, 0:32],
                                        POf[0:1, 3072:3088], start=True,
                                        stop=True), (("y31", y31),), inc=True)
        E("V", lambda e: e.tensor_tensor(SM2.ap()[0:32, 16:32], SM2.ap()[0:32, 0:16],
                                         ps1[0:32, 0:16], ALU.subtract), (("T", t31),),
          inc=True)
        E("V", lambda e: e.memset(SMB.ap()[32:64, 680:696], 0.0))
        E("V", lambda e: e.tensor_scalar(SMB.ap()[0:32, 680:696],
                                         SM2.ap()[0:32, 16:32],
                                         8.0, None, ALU.mult),
          (("V", em.now("V")),), inc=True)
        E("V", lambda e: e.memset(SMB.ap()[32:33, 680:696], NEGBIG),
          (("V", em.now("V")),), inc=True)
        vrel = em.now("V")
        hhsrc = [dr["hh0"], dr["hh1"]]
        E("V", lambda e: e.memset(PO.ap()[32:64, 0:4096], 0.0), tuple(pre), inc=True)
        vz = em.now("V")
        prev_t = [0, 0]
        sprev = 0
        yfprev = 0
        for t in range(2):
            for g in range(8):
                hb = PO.ap()[0:33, (g % 2) * 2048:(g % 2) * 2048 + 2048]
                hb64 = PO.ap()[0:64, (g % 2) * 2048:(g % 2) * 2048 + 2048]
                ghh = "ghh%d" % (g % 2)
                gw = E("G", lambda e, t=t, g=g, hb=hb: e.dma_start(
                    hb, hhsrc[t][:, g * 2048:(g + 1) * 2048]),
                    tuple(pre) + (("T", prev_t[g % 2]), ("V", vz)), inc=True,
                    tag=ghh)
                for c in range(4):
                    E("T", lambda e, hb=hb, hb64=hb64, c=c: e.matmul(
                        ps0[0:16, c * 512:(c + 1) * 512], SMB.ap()[0:64, 680:696],
                        hb64[:, c * 512:(c + 1) * 512], start=True, stop=True),
                      ((ghh, gw), ("V", vrel), ("S", sprev)) if c == 0 else (),
                      inc=(c == 3))
                tm = em.now("T")
                prev_t[g % 2] = tm
                sf = E("S", lambda e: e.copy(FLT, ps0[0:16, 0:2048]),
                       (("T", tm), ("yflt", yfprev)), inc=True)
                sprev = sf
                yf = E("Y", lambda e, t=t, g=g: e.dma_start(tflat_d[t, g], FLT),
                       (("S", sf),), inc=True, tag="yflt")
                yfprev = yf
                E("Y", lambda e, t=t, g=g: e.dma_start(
                    PD.ap()[g * 16:(g + 1) * 16, t * 4096: t * 4096 + 2048]
                    .rearrange("kp (h j) -> kp h j", j=128),
                    tflat_d[t, g].rearrange("h (kp j) -> kp h j", j=128)),
                  (("yflt", yf),), inc=True, tag="ypd")
        return {"bias_ready": [("ypd", em.now("ypd"))],
                "ps_free": [("V", em.now("V")), ("S", em.now("S"))],
                "po_free": [("T", em.now("T")), ("S", em.now("S"))]}

    def ph_prep(L, xsrc, ev, pa_free=(), ps_free=()):
        for c in range(4):
            E("G", lambda e, c=c: e.dma_start(
                PA.ap()[:, c * 4096:(c + 1) * 4096].rearrange("p (kc t) -> p kc t", t=2048),
                xsrc.rearrange("(kc p) t -> p kc t", p=128)[:, 2 * c: 2 * c + 2, :]),
              tuple(pa_free), inc=True, tag="gxb")
        gxb = em.now("gxb")
        out = {"xb_g": gxb}
        if L == 0:
            for tb in range(4):
                for kc in range(8):
                    E("T", lambda e, tb=tb, kc=kc: e.matmul(
                        ps1[0:1, tb * 512:(tb + 1) * 512], ONESB,
                        PA.ap()[:, kc * 2048 + tb * 512: kc * 2048 + (tb + 1) * 512],
                        start=(kc == 0), stop=(kc == 7)),
                      (("gxb", gxb), ("V", ev["init_v"])) + tuple(ps_free)
                      if (kc == 0 and tb == 0) else (),
                      inc=(kc == 7))
            tsum = em.now("T")
            E("V", lambda e: e.tensor_scalar(R0, ps1[0:1, 0:2048], 1.0 / 1024, None,
                                             ALU.mult), (("T", tsum),), inc=True)
            vmu = em.now("V")
            sq = SCRb[:, 0:2048]
            for kc in range(8):
                E("V", lambda e, kc=kc: e.tensor_tensor(
                    sq, PA.ap()[:, kc * 2048:(kc + 1) * 2048],
                    PA.ap()[:, kc * 2048:(kc + 1) * 2048], ALU.mult),
                  (("T", em.now("T")),), inc=True)
                vsq = em.now("V")
                for tb in range(4):
                    E("T", lambda e, tb=tb, kc=kc: e.matmul(
                        ps1[0:1, tb * 512:(tb + 1) * 512], ONESB,
                        sq[:, tb * 512:(tb + 1) * 512],
                        start=(kc == 0), stop=(kc == 7)),
                      (("V", vsq), ("V", vmu)) if tb == 0 else (),
                      inc=(tb == 3))
            tsq = em.now("T")
            T0 = PX.ap()[0:1, 0:2048]
            T1 = PX.ap()[0:1, 2048:4096]
            T2 = PX.ap()[0:1, 4096:6144]
            vt0 = E("V", lambda e: e.tensor_tensor(T0, R0, R0, ALU.mult),
                    (("T", tsq),), inc=True)
            E("V", lambda e: e.scalar_tensor_tensor(T1, ps1[0:1, 0:2048], 1.0 / 1024,
                                                    T0, ALU.mult, ALU.subtract),
              (("V", vt0),), inc=True)
            vvar = em.now("V")
            load_actset("natural_log_exp_and_others")
            s1 = E("S", lambda e: e.activation(T2, T1, AF.Ln, bias=EPSR),
                   (("V", vvar),), inc=True)
            s2 = E("S", lambda e: e.activation(T1, T2, AF.Exp, scale=-0.5),
                   (("S", s1),), inc=True)
            E("V", lambda e: e.tensor_copy(R_RS0, T1), (("S", s2),))
            vt0 = E("V", lambda e: e.tensor_tensor(T0, R0, T1, ALU.mult), inc=True)
            E("V", lambda e: e.tensor_scalar(R_M0R0, T0, -1.0, None, ALU.mult),
              (("V", vt0),), inc=True)
            out["rows_v"] = em.now("V")
        return out

    def ph_qkv(L, ev, pxb_free=(), po_free=(), ps_free=()):
        wq = dr["wqkvT0"] if L == 0 else dr["wqkvT1"]
        gxb = ev["xb_g"]
        t_after_fmt = {}
        last_evict_v = em.now("V")
        for fmt in range(16):
            buf = WSb[:, (fmt % 2) * 1024:(fmt % 2) * 1024 + 1024]
            gwq = "gwq%d" % (fmt % 2)
            w = [("T", t_after_fmt[fmt - 2])] if fmt >= 2 else []
            gw = E("G", lambda e, fmt=fmt, buf=buf: e.dma_start(
                buf.rearrange("p (kc f) -> p kc f", f=128),
                wq[:, fmt * 128:(fmt + 1) * 128]
                .rearrange("(kc p) f -> p kc f", p=128)), tuple(w), inc=True,
                tag=gwq)
            for kc in range(8):
                for qb in range(4):
                    w = ()
                    if kc == 0 and qb == 0:
                        w = ((gwq, gw), ("gxb", gxb), ("V", last_evict_v)) + \
                            tuple((tuple(pxb_free) + tuple(ps_free)) if fmt == 0
                                  else ())
                    E("T", lambda e, kc=kc, qb=qb, buf=buf: e.matmul(
                        ps0[:, qb * 512:(qb + 1) * 512], buf[:, kc * 128:(kc + 1) * 128],
                        PA.ap()[:, kc * 2048 + qb * 512: kc * 2048 + (qb + 1) * 512],
                        start=(kc == 0), stop=(kc == 7)), w,
                      inc=(kc == 7 and qb == 3))
            tmm = em.now("T")
            t_after_fmt[fmt] = tmm
            base = (fmt % 8) * 2048 + (0 if fmt < 8 else 16384)
            for qb in range(4):
                dst = PXb[:, base + qb * 512: base + (qb + 1) * 512]
                if L == 1:
                    E("V", lambda e, dst=dst, qb=qb, fmt=fmt: e.tensor_scalar(
                        dst, ps0[:, qb * 512:(qb + 1) * 512], BT1[:, fmt:fmt + 1],
                        None, ALU.add), (("T", tmm),) if qb == 0 else (), inc=(qb == 3))
                else:
                    E("V", lambda e, dst=dst, qb=qb: e.tensor_copy(
                        dst, ps0[:, qb * 512:(qb + 1) * 512]),
                      (("T", tmm),) if qb == 0 else (), inc=(qb == 3))
            last_evict_v = em.now("V")
        qk_v = last_evict_v
        gv = E("G", lambda e: e.dma_start(
            PO.ap()[:, 0:8192].rearrange("p (kc f) -> p kc f", f=1024),
            wq[:, 2048:3072].rearrange("(kc p) f -> p kc f", p=128)),
            tuple(po_free), inc=True, tag="gwv")
        vbias_v = 0
        if L == 1:
            ybv = E("Y", lambda e: e.dma_start(SCR.ap()[0:1, 1024:2048], dr["bv1"]),
                    inc=True, tag="ybv")
            E("T", lambda e: e.matmul(ps1[:, 0:512], ONERF, SCR.ap()[0:1, 1024:1536],
                                      start=True, stop=True), (("ybv", ybv),))
            tb2 = E("T", lambda e: e.matmul(ps1[:, 512:1024], ONERF,
                                            SCR.ap()[0:1, 1536:2048],
                                            start=True, stop=True), inc=True)
            vbias_v = E("V", lambda e: e.tensor_copy(PDf[:, 3072:4096],
                                                     ps1[:, 0:1024]),
                        (("T", tb2),), inc=True)
        ev_t = {}
        for tt in range(16):
            for fc in range(2):
                bank = (tt % 2) * 2 + fc
                w0 = [("gwv", gv)]
                if tt >= 2:
                    w0.append(("V", ev_t[(tt - 2, fc)]))
                if L == 1:
                    w0.append(("V", vbias_v))
                for kc in range(8):
                    E("T", lambda e, tt=tt, fc=fc, kc=kc, bank=bank: e.matmul(
                        ps1[:, bank * 512:(bank + 1) * 512],
                        PA.ap()[:, kc * 2048 + tt * 128: kc * 2048 + (tt + 1) * 128],
                        PO.ap()[:, kc * 1024 + fc * 512: kc * 1024 + (fc + 1) * 512],
                        start=(kc == 0), stop=(kc == 7)),
                      tuple(w0) if kc == 0 else (), inc=(kc == 7))
                tvm = em.now("T")
                base = tt * 1040 + fc * 8 * 65
                dst = PB.ap()[:, base: base + 520].rearrange("p (h e) -> p h e", e=65)[:, :, 0:64]
                src = ps1[:, bank * 512:(bank + 1) * 512].rearrange(
                    "p (h e) -> p h e", e=64)
                if L == 1:
                    bsl = PDf[:, 3072 + fc * 512: 3072 + (fc + 1) * 512] \
                        .rearrange("p (h e) -> p h e", e=64)
                    E("V", lambda e, dst=dst, src=src, bsl=bsl: e.tensor_tensor(
                        dst, src, bsl, ALU.add), (("T", tvm),), inc=True)
                else:
                    E("V", lambda e, dst=dst, src=src: e.tensor_copy(dst, src),
                      (("T", tvm),), inc=True)
                ev_t[(tt, fc)] = em.now("V")
            E("V", lambda e, tt=tt: e.memset(
                PB.ap()[:, tt * 1040: tt * 1040 + 1040].rearrange(
                    "p (h e) -> p h e", e=65)[:, :, 64:65], 1.0), inc=True)
        return {"qk_v": qk_v, "v_v": em.now("V"), "qkv_t": em.now("T")}

    ROWSB = ROWS.ap().bitcast(BF16)

    def ph_bounds(L, ev):
        qkv = ev["qk_v"]
        SA = PAf[0:1, 0:2048]
        SB = PAf[0:1, 2048:4096]
        SC = PAf[0:1, 4096:6144]
        KP = PAf[0:1, 6144:6152]
        ydrain = em.now("yms2")
        yms = []
        for hp in range(8):
            vsq = E("V", lambda e, hp=hp: e.tensor_tensor(
                SCRb[:, 0:2048], PXb[:, hp * 2048:(hp + 1) * 2048],
                PXb[:, hp * 2048:(hp + 1) * 2048], ALU.mult),
                ((("V", qkv),) + tuple(ev.get("bias_ready", ())) if hp == 0
                 else (("T", em.now("T")),)))
            E("V", lambda e, hp=hp: e.tensor_tensor(
                SCRb[:, 2048:4096], PXb[:, 16384 + hp * 2048: 16384 + (hp + 1) * 2048],
                PXb[:, 16384 + hp * 2048: 16384 + (hp + 1) * 2048], ALU.mult),
              inc=True)
            vsq = em.now("V")
            for half in range(2):
                h = hp * 2 + half
                p0 = half * 64
                for tb in range(4):
                    E("T", lambda e, p0=p0, tb=tb: e.matmul(
                        ps1[0:1, tb * 512:(tb + 1) * 512], ONESB[p0:p0 + 64, :],
                        SCRb[p0:p0 + 64, tb * 512:(tb + 1) * 512], start=True,
                        stop=True, tile_position=(p0, 0)),
                      (("V", vsq), ("V", em.now("V"))) if tb == 0 else ())
                    E("T", lambda e, p0=p0, tb=tb: e.matmul(
                        ps0[0:1, tb * 512:(tb + 1) * 512], ONESB[p0:p0 + 64, :],
                        SCRb[p0:p0 + 64, 2048 + tb * 512: 2048 + (tb + 1) * 512],
                        start=True, stop=True, tile_position=(p0, 0)),
                      inc=(tb == 3))
                tq = em.now("T")
                E("V", lambda e: e.tensor_copy(SA, ps1[0:1, 0:2048]),
                  (("T", tq), ("yms", yms[-1] if yms else 0)))
                vcp = E("V", lambda e: e.tensor_copy(SB, ps0[0:1, 0:2048]), inc=True)
                for qb in range(4):
                    E("V", lambda e, qb=qb: e.tensor_reduce(
                        KP[:, qb:qb + 1], SB[:, qb * 512:(qb + 1) * 512], axis=AX.X,
                        op=ALU.max), (("V", vcp),) if qb == 0 else (),
                      inc=(qb == 3))
                vkp = em.now("V")
                if L == 1:
                    for qb in range(1, 4):
                        vkp = E("V", lambda e, qb=qb: e.tensor_tensor(
                            KP[:, qb:qb + 1], KP[:, qb:qb + 1], KP[:, qb - 1:qb],
                            ALU.max), (("V", vkp),), inc=True)
                else:
                    for qb in range(3):
                        vkp = E("V", lambda e, qb=qb: e.tensor_tensor(
                            KP[:, 3:4], KP[:, 3:4], KP[:, qb:qb + 1], ALU.max),
                            (("V", vkp),), inc=True)
                    for qb in range(3):
                        E("V", lambda e, qb=qb: e.tensor_scalar(
                            KP[:, qb:qb + 1], KP[:, 3:4], 1.0, None, ALU.mult),
                          (("V", vkp),) if qb == 0 else (), inc=(qb == 2))
                    vkp = em.now("V")
                for qb in range(4):
                    E("V", lambda e, qb=qb: e.tensor_scalar(
                        SC[:, qb * 512:(qb + 1) * 512], SA[:, qb * 512:(qb + 1) * 512],
                        KP[:, qb:qb + 1], None, ALU.mult),
                      (("V", vkp),) if qb == 0 else (), inc=(qb == 3))
                vpr = em.now("V")
                sl = E("S", lambda e: e.activation(SC, SC, AF.Ln, bias=C30R),
                       (("V", vpr),), inc=True)
                sl = E("S", lambda e: e.activation(SC, SC, AF.Exp, scale=0.5),
                       (("S", sl),), inc=True)
                vm2 = E("V", lambda e: e.tensor_scalar(
                    PA.ap()[0:1, 12352:14400], SC, -1.0, -4.0, ALU.mult, ALU.add),
                    (("S", sl),), inc=True)
                yms.append(E("Y", lambda e, h=h: e.dma_start(
                    ROWSB[8 + h: 9 + h, 0:2048], PA.ap()[0:1, 12352:14400]),
                    (("V", vm2), ("yms2", ydrain)), inc=True, tag="yms"))
        return {"mrows_y": em.now("yms"), "mrows_v": em.now("V")}

    def ph_attncore(L, ev, bg=None):
        load_actset("natural_log_exp_and_others")
        prev_exp = em.now("S")
        prev_ev_v = em.now("V")
        prev_mT = 0
        first_all = True
        MSA = SCRb[0:1, 2048:2560]
        MSB = SCRb[0:1, 3072:3584]
        for hp in range(8):
            for qb in range(4):
                kc_hi = 16 if L == 0 else 4 * qb + 4
                yms = E("Y", lambda e, hp=hp, qb=qb: e.dma_start(
                    MSA, ROWSB[8 + 2 * hp: 9 + 2 * hp, qb * 512:(qb + 1) * 512]),
                    (("yms", ev["mrows_y"]), ("T", prev_mT)), inc=True, tag="yms2")
                yms = E("Y", lambda e, hp=hp, qb=qb: e.dma_start(
                    MSB, ROWSB[9 + 2 * hp: 10 + 2 * hp, qb * 512:(qb + 1) * 512]),
                    inc=True, tag="yms2")
                n_k = 0
                for k0 in range(0, kc_hi, 2):
                    steps = []
                    for half in range(2):
                        p0 = half * 64
                        for ki, kc in enumerate((k0, k0 + 1)):
                            qoff = max(0, kc * 128 - qb * 512) if L == 1 else 0
                            col = half * 1024 + ki * 512
                            lhs = PXb[p0:p0 + 64, 16384 + hp * 2048 + kc * 128:
                                      16384 + hp * 2048 + (kc + 1) * 128]
                            rhs = PXb[p0:p0 + 64, hp * 2048 + qb * 512 + qoff:
                                      hp * 2048 + (qb + 1) * 512]
                            badds = []
                            if L == 0 and qb <= 1 and kc <= 8:
                                badds.append((col, qb * 512, 512,
                                              PD.ap()[:, kc * 1028 + qb * 512:
                                                      kc * 1028 + (qb + 1) * 512]))
                            elif L == 0 and qb == 2 and kc <= 8:
                                badds.append((col, 1024, 1,
                                              PD.ap()[:, kc * 1028 + 1024:
                                                      kc * 1028 + 1025]))
                            elif L == 1:
                                h = hp * 2 + half
                                j0 = kc - 4 * qb
                                if 0 <= j0 <= 3:
                                    badds.append((col + j0 * 128, 0, 128,
                                                  PD.ap()[:, h * 128:(h + 1) * 128]))
                                if 0 <= j0 + 1 <= 3:
                                    badds.append((col + (j0 + 1) * 128, 0, 128,
                                                  PD.ap()[:, 4096 + h * 128:
                                                          4096 + (h + 1) * 128]))
                                if j0 == -1:
                                    badds.append((col, 0, 128,
                                                  PD.ap()[:, 4096 + h * 128:
                                                          4096 + (h + 1) * 128]))
                            if DBG_NO_D:
                                badds = []
                            last_sc = DBG_NO_M and not badds
                            steps.append((lambda e, col=col, lhs=lhs, rhs=rhs,
                                          qoff=qoff, p0=p0, sp=last_sc: e.matmul(
                                ps0[:, col + qoff: col + 512], lhs, rhs,
                                start=True, stop=sp, tile_position=(p0, 0)),
                                False))
                            for bi, (c_, _, w_, tsl) in enumerate(badds):
                                sp = DBG_NO_M and bi == len(badds) - 1
                                steps.append((lambda e, c_=c_, w_=w_, tsl=tsl,
                                              sp=sp: e.matmul(
                                    ps0[:, c_: c_ + w_], IDENT, tsl,
                                    start=False, stop=sp), False))
                            ms = MSA if half == 0 else MSB
                            if not DBG_NO_M:
                                steps.append((lambda e, col=col, ms=ms, qoff=qoff:
                                              e.matmul(
                                    ps0[:, col + qoff: col + 512], ONER,
                                    ms[:, qoff:512], start=False, stop=True), True))
                    for i, (fn, is_m) in enumerate(steps):
                        w = []
                        if i == 0:
                            w = [("S", prev_exp)]
                            if first_all:
                                w += [("V", ev["mrows_v"]), ("V", ev["v_v"])] + \
                                    list(ev.get("bias_ready", []))
                        if is_m:
                            w.append(("yms2", yms))
                        E("T", fn, tuple(w), inc=(i == len(steps) - 1))
                    tsc = em.now("T")
                    first_all = False
                    for half in range(2):
                        h = hp * 2 + half
                        dst = PA.ap()[:, half * 8192 + k0 * 512:
                                      half * 8192 + (k0 + 2) * 512]
                        E("S", lambda e, dst=dst, half=half, h=h: e.activation(
                            dst, ps0[:, half * 1024:(half + 1) * 1024], AF.Exp,
                            scale=0.125),
                          (("T", tsc),) if half == 0 else (), inc=(half == 1))
                    cur_exp = em.now("S")
                    av = []
                    for half in range(2):
                        h = hp * 2 + half
                        for ki, kc in enumerate((k0, k0 + 1)):
                            qoff = max(0, kc * 128 - qb * 512) if L == 1 else 0
                            vsl = PB.ap()[:, kc * 1040 + h * 65: kc * 1040 + h * 65 + 65]
                            psl = PA.ap()[:, half * 8192 + kc * 512 + qoff:
                                          half * 8192 + (kc + 1) * 512]
                            av.append((lambda e, half=half, vsl=vsl, psl=psl,
                                       qoff=qoff, st=(kc == 0),
                                       sp=(kc == kc_hi - 1): e.matmul(
                                ps1[0:65, half * 512 + qoff:(half + 1) * 512],
                                vsl, psl, start=st, stop=sp),))
                    for i, (fn,) in enumerate(av):
                        w = ()
                        if i == 0:
                            w = (("S", cur_exp), ("V", prev_ev_v))
                        E("T", fn, w, inc=(i == len(av) - 1))
                    n_k += 1
                    prev_exp = cur_exp
                tav = em.now("T")
                for half in range(2):
                    h = hp * 2 + half
                    vrec = E("V", lambda e, half=half: e.reciprocal(
                        SCR.ap()[0:1, 512:1024],
                        ps1[64:65, half * 512:(half + 1) * 512]), (("T", tav),),
                        inc=True)
                    trb = E("T", lambda e: e.matmul(
                        ps1[0:64, 1024 + 0:1024 + 512], ONERF[0:1, 0:64],
                        SCR.ap()[0:1, 512:1024], start=True, stop=True), (("V", vrec),),
                        inc=True)
                    vcb = E("V", lambda e: e.tensor_copy(SCR.ap()[0:64, 0:512],
                                                         ps1[0:64, 1024:1536]),
                            (("T", trb),), inc=True)
                    dst = PO.ap()[(h % 2) * 64:(h % 2) * 64 + 64,
                                  (h // 2) * 2048 + qb * 512:
                                  (h // 2) * 2048 + (qb + 1) * 512]
                    E("V", lambda e, dst=dst, half=half: e.tensor_tensor(
                        dst, ps1[0:64, half * 512:(half + 1) * 512],
                        SCR.ap()[0:64, 0:512], ALU.mult), (("V", vcb),), inc=True)
                prev_ev_v = em.now("V")
                prev_mT = em.now("T")
                if bg is not None:
                    next(bg)
        return {"attn_t": em.now("T"), "attn_v": em.now("V"), "attn_s": em.now("S")}

    def ph_proj(L, ev, xsrc, xdst):
        wp = dr["wprojT0"] if L == 0 else dr["wprojT1"]
        bcol = BPB0 if L == 0 else BP1
        gw = E("G", lambda e: e.dma_start(
            PB.ap()[:, 0:8192].rearrange("p (kc f) -> p kc f", f=1024),
            wp.rearrange("(kc p) f -> p kc f", p=128)),
            (("T", ev["attn_t"]),), inc=True, tag="gwp")
        tmp = PBf[:, 4096:8192]
        CPB = PA.ap()[:, 0:512]
        SQB = PA.ap()[:, 512:1024]
        vprev_tb = em.now("V")
        yout = 0
        for tb in range(4):
            if L == 0:
                PR1 = PAf[0:1, 512:1024]
                PR2 = PAf[0:1, 1024:1536]
                E("V", lambda e, tb=tb: e.tensor_copy(
                    PR1, R_RS0[0:1, tb * 512:(tb + 1) * 512]),
                  (("V", vprev_tb), ("V", ev["attn_v"])))
                vcp = E("V", lambda e, tb=tb: e.tensor_copy(
                    PR2, R_M0R0[0:1, tb * 512:(tb + 1) * 512]), inc=True)
                E("T", lambda e: e.matmul(
                    ps1[:, 1024:1536], ONERF, PR1,
                    start=True, stop=True), (("V", vcp),))
                tbc = E("T", lambda e: e.matmul(
                    ps1[:, 1536:2048], ONERF, PR2,
                    start=True, stop=True), inc=True)
                E("V", lambda e: e.tensor_copy(SCR.ap()[:, 512:1024], ps1[:, 1024:1536]),
                  (("T", tbc),))
                E("V", lambda e: e.tensor_copy(SCR.ap()[:, 1024:1536],
                                               ps1[:, 1536:2048]), inc=True)
            vbc = em.now("V")
            v_evict = {}
            for Mt in range(8):
                bank = Mt % 4
                yxr = "yxr%d" % (Mt % 2)
                yx = E("Y", lambda e, Mt=Mt, tb=tb: e.dma_start(
                    WS.ap()[:, (Mt % 2) * 512:(Mt % 2) * 512 + 512],
                    xsrc[Mt * 128:(Mt + 1) * 128, tb * 512:(tb + 1) * 512]),
                    ((("V", v_evict[Mt - 2]),) if Mt >= 2 else (("V", vprev_tb),)) +
                    (tuple(ev.get("xland", ())) if tb == 0 and Mt == 0 else ()),
                    inc=True, tag=yxr)
                w0 = [("gwp", gw), ("V", ev["attn_v"])]
                if Mt >= 4:
                    w0.append(("V", v_evict[Mt - 4]))
                for kc in range(8):
                    E("T", lambda e, tb=tb, Mt=Mt, kc=kc, bank=bank: e.matmul(
                        ps0[:, bank * 512:(bank + 1) * 512],
                        PB.ap()[:, kc * 1024 + Mt * 128: kc * 1024 + (Mt + 1) * 128],
                        PO.ap()[:, kc * 2048 + tb * 512: kc * 2048 + (tb + 1) * 512],
                        start=(kc == 0), stop=(kc == 7)),
                      tuple(w0) if kc == 0 else (), inc=(kc == 7))
                tmm = em.now("T")
                xs = WS.ap()[:, (Mt % 2) * 512:(Mt % 2) * 512 + 512]
                tmps = tmp[:, Mt * 512:(Mt + 1) * 512]
                wprev = (("V", v_evict[Mt - 1]),) if Mt >= 1 else ()
                if L == 0:
                    E("V", lambda e, xs=xs, Mt=Mt: e.scalar_tensor_tensor(
                        SCR.ap()[:, 1536:2048], xs, GAM0[:, Mt:Mt + 1],
                        SCR.ap()[:, 512:1024], ALU.mult, ALU.mult),
                      (("T", tmm), (yxr, yx), ("V", vbc)) + wprev, inc=True)
                    E("V", lambda e, tmps=tmps, xs=xs, bank=bank, Mt=Mt:
                      e.scalar_tensor_tensor(
                        tmps, ps0[:, bank * 512:(bank + 1) * 512], bcol[:, Mt:Mt + 1],
                        xs, ALU.add, ALU.add), inc=True)
                    E("V", lambda e, tmps=tmps: e.tensor_tensor(
                        tmps, tmps, SCR.ap()[:, 1536:2048], ALU.add),
                      (("V", em.now("V")),), inc=True)
                    E("V", lambda e, tmps=tmps, Mt=Mt: e.scalar_tensor_tensor(
                        tmps, SCR.ap()[:, 1024:1536], GAM0[:, Mt:Mt + 1], tmps,
                        ALU.mult, ALU.add), (("V", em.now("V")),), inc=True)
                else:
                    E("V", lambda e, tmps=tmps, xs=xs, bank=bank, Mt=Mt:
                      e.scalar_tensor_tensor(
                        tmps, ps0[:, bank * 512:(bank + 1) * 512], bcol[:, Mt:Mt + 1],
                        xs, ALU.add, ALU.add),
                      (("T", tmm), (yxr, yx)) + wprev, inc=True)
                v_evict[Mt] = em.now("V")
            for Mt in range(8):
                tmps = tmp[:, Mt * 512:(Mt + 1) * 512]
                E("V", lambda e, tmps=tmps: e.tensor_copy(CPB, tmps),
                  (("V", v_evict[7]), ("T", em.now("T"))) if Mt == 0
                  else (("T", em.now("T")),))
                E("V", lambda e, tmps=tmps: e.tensor_tensor(SQB, tmps, tmps, ALU.mult),
                  inc=True)
                vst = em.now("V")
                E("T", lambda e, Mt=Mt: e.matmul(
                    ps1[0:1, 0:512], ONESB, CPB, start=(Mt == 0), stop=(Mt == 7)),
                  (("V", vst),))
                E("T", lambda e, Mt=Mt: e.matmul(
                    ps1[0:1, 512:1024], ONESB, SQB, start=(Mt == 0), stop=(Mt == 7)),
                  inc=True)
            tst = em.now("T")
            rmu = PAf[0:1, 1536:2048]
            rsd = PAf[0:1, 2048:2560]
            vmu1 = E("V", lambda e: e.tensor_scalar(rmu, ps1[0:1, 0:512],
                                                    1.0 / 1024, None,
                                                    ALU.mult), (("T", tst),),
                     inc=True)
            E("V", lambda e: e.tensor_tensor(rsd, rmu, rmu, ALU.mult),
              (("V", vmu1),), inc=True)
            E("V", lambda e: e.scalar_tensor_tensor(rsd, ps1[0:1, 512:1024],
                                                    1.0 / 1024, rsd, ALU.mult,
                                                    ALU.subtract),
              (("V", em.now("V")),), inc=True)
            vvar = em.now("V")
            sl = E("S", lambda e: e.activation(rsd, rsd, AF.Ln, bias=EPSR),
                   (("V", vvar),), inc=True)
            sl = E("S", lambda e: e.activation(rsd, rsd, AF.Exp, scale=-0.5),
                   (("S", sl),), inc=True)
            E("T", lambda e: e.matmul(ps1[:, 1024:1536], ONERF, rmu, start=True,
                                      stop=True), (("S", sl),))
            tbb = E("T", lambda e: e.matmul(ps1[:, 1536:2048], ONERF, rsd, start=True,
                                            stop=True), inc=True)
            E("V", lambda e: e.tensor_copy(SCR.ap()[:, 0:512], ps1[:, 1024:1536]),
              (("T", tbb),))
            E("V", lambda e: e.tensor_copy(SCR.ap()[:, 512:1024], ps1[:, 1536:2048]),
              inc=True)
            vbb = em.now("V")
            for Mt in range(8):
                tmps = tmp[:, Mt * 512:(Mt + 1) * 512]
                E("V", lambda e, tmps=tmps: e.tensor_tensor(
                    tmps, tmps, SCR.ap()[:, 0:512], ALU.subtract),
                  (("V", vbb),) if Mt == 0 else (), inc=True)
                dst = PX.ap()[:, Mt * 2048 + tb * 512: Mt * 2048 + (tb + 1) * 512]
                E("V", lambda e, tmps=tmps, dst=dst: e.tensor_tensor(
                    dst, tmps, SCR.ap()[:, 512:1024], ALU.mult),
                  (("V", em.now("V")),), inc=True)
            vln = em.now("V")
            yout = E("Y", lambda e, tb=tb: e.dma_start(
                xdst[:, tb * 512:(tb + 1) * 512]
                .rearrange("(mt p) t -> p mt t", p=128),
                PX.ap().rearrange("p (mt t) -> p mt t", t=2048)[:, :, tb * 512:(tb + 1) * 512]),
                (("V", vln),), inc=True, tag="yo0")
            vprev_tb = vln
        return {"proj_v": em.now("V"), "proj_y": yout, "proj_t": em.now("T"),
                "proj_s": em.now("S"), "xland": (("yo0", yout),)}

    def ph_wmean_gen(L, ev):
        srcs = [dr["w1T0" if L == 0 else "w1T1"], dr["w2T0" if L == 0 else "w2T1"]]
        ACC = SMF.ap()[:, 523:587]
        vch = {}
        n = 0
        for wi in range(2):
            w = srcs[wi]
            flat = w.rearrange("a b -> (a b)").rearrange("(c p n) -> c p n", p=128,
                                                         n=1024)
            for c in range(32):
                buf = WS.ap()[:, (c % 2) * 1024:(c % 2) * 1024 + 1024]
                gwm = "gwm%d" % (c % 2)
                gd = E("G", lambda e, c=c, buf=buf, flat=flat: e.dma_start(
                    buf, flat[c]),
                    (("V", vch[(wi, c - 2)]),) if c >= 2 else
                    ((("V", ev["qk_v"]),) if wi == 0 else
                     (("V", vch[(0, 30 + c)]),)), inc=True, tag=gwm)
                E("V", lambda e, wi=wi, c=c, buf=buf: e.tensor_reduce(
                    ACC[:, wi * 32 + c: wi * 32 + c + 1], buf, axis=AX.X, op=ALU.add,
                    apply_absolute_value=True), ((gwm, gd),), inc=True)
                vch[(wi, c)] = em.now("V")
                n += 1
                if n % 2 == 0:
                    yield
        while True:
            yield

    def ph_wmean_fin(L):
        ACC = SMF.ap()[:, 523:587]
        out = {}
        for wi in range(2):
            E("V", lambda e, wi=wi: e.tensor_reduce(
                SMF.ap()[:, 587 + wi: 588 + wi], ACC[:, wi * 32: wi * 32 + 32],
                axis=AX.X, op=ALU.add), (("V", em.now("V")),), inc=True)
            vsum = em.now("V")
            tcp = E("T", lambda e, wi=wi: e.matmul(
                ps1[0:1, 1792 + wi * 8: 1793 + wi * 8], ONESFC,
                SMF.ap()[:, 587 + wi: 588 + wi], start=True, stop=True),
                (("V", vsum),), inc=True)
            MCF = SMF.ap()[0:1, 592 + wi: 593 + wi]
            MCQ = SMF.ap()[0:1, 594 + wi: 595 + wi]
            WSS = SMF.ap()[0:1, 596 + wi: 597 + wi]
            vmc = E("V", lambda e, wi=wi, MCF=MCF: e.tensor_scalar(
                MCF, ps1[0:1, 1792 + wi * 8: 1793 + wi * 8], 1.0 / (4096 * 1024),
                1e-5, ALU.mult, ALU.max), (("T", tcp),), inc=True)
            E("V", lambda e, MCF=MCF, MCQ=MCQ: e.tensor_scalar(
                MCQ, MCF, 1.0 / 127, None, ALU.mult), (("V", vmc),))
            E("V", lambda e, MCF=MCF, WSS=WSS: e.reciprocal(WSS, MCF), inc=True)
            vws = em.now("V")
            trp = E("T", lambda e, WSS=WSS: e.matmul(
                ps1[:, 1536:1537], ONERF, WSS, start=True, stop=True),
                (("V", vws),), inc=True)
            E("V", lambda e, wi=wi: e.tensor_copy(WSR[wi], ps1[:, 1536:1537]),
              (("T", trp),), inc=True)
            out[f"mcq{wi}"] = MCQ
        out["wmean_v"] = em.now("V")
        return out

    def ph_quantx(L, ev):
        pv = ev["proj_v"]
        Mp = POf[:, 0:2048]
        Mn = POf[:, 2048:4096]
        E("V", lambda e: e.tensor_copy(Mp, PX.ap()[:, 0:2048]), (("V", pv),))
        E("V", lambda e: e.tensor_copy(Mn, PX.ap()[:, 0:2048]), inc=True)
        for kc in range(1, 8):
            E("V", lambda e, kc=kc: e.tensor_tensor(
                Mp, Mp, PX.ap()[:, kc * 2048:(kc + 1) * 2048], ALU.max),
              (("V", em.now("V")),))
            E("V", lambda e, kc=kc: e.tensor_tensor(
                Mn, Mn, PX.ap()[:, kc * 2048:(kc + 1) * 2048], ALU.min), inc=True)
        E("V", lambda e: e.tensor_scalar(Mn, Mn, -1.0, None, ALU.mult),
          (("V", em.now("V")),), inc=True)
        E("V", lambda e: e.tensor_tensor(Mp, Mp, Mn, ALU.max),
          (("V", em.now("V")),), inc=True)
        vam = em.now("V")
        vred = 0
        for tt in range(16):
            ttr = E("T", lambda e, tt=tt: e.transpose(
                ps1[:, (tt % 2) * 512:(tt % 2) * 512 + 128],
                Mp[:, tt * 128:(tt + 1) * 128], IDF),
                (("V", vam),) if tt < 2 else (("V", vred),), inc=True)
            E("V", lambda e, tt=tt: e.tensor_reduce(
                AMC[:, tt:tt + 1], ps1[:, (tt % 2) * 512:(tt % 2) * 512 + 128],
                axis=AX.X, op=ALU.max), (("T", ttr),), inc=True)
            vred = em.now("V")
        E("V", lambda e: e.tensor_scalar(AMC, AMC, 1e-5, None, ALU.max),
          (("V", em.now("V")),), inc=True)
        vclip = em.now("V")
        y1 = E("Y", lambda e: e.dma_start(
            row_d[3:4, 0:2048].rearrange("o (c p) -> (o p) c", p=128), AMC[:, 0:16],
            allow_slow_non_contiguous=True), (("V", vclip),), inc=True, tag="yrow")
        AROW = POf[0:1, 4096:6144]
        y2 = E("Y", lambda e: e.dma_start(AROW, row_d[3:4, 0:2048]),
               (("yrow", y1),), inc=True, tag="yrow")
        SXR = POf[0:1, 6144:8192]
        vrc = E("V", lambda e: e.reciprocal(SXR, AROW), (("yrow", y2),), inc=True)
        E("V", lambda e: e.tensor_scalar(SXR, SXR, 127.0, None, ALU.mult),
          (("V", vrc),), inc=True)
        E("V", lambda e: e.tensor_scalar(R0[:, 0:2048], AROW, ev["mcq0"], None,
                                         ALU.mult), inc=True)
        vrows = em.now("V")
        for tb in range(4):
            tb1 = E("T", lambda e, tb=tb: e.matmul(
                ps1[:, 1024:1536], ONERF, R0[0:1, tb * 512:(tb + 1) * 512],
                start=True, stop=True), (("V", vrows), ("V", em.now("V"))),
                inc=True)
            E("V", lambda e, tb=tb: e.tensor_copy(
                PDf[:, tb * 512:(tb + 1) * 512], ps1[:, 1024:1536]),
              (("T", tb1),), inc=True)
        vxq = 0
        for tb in range(4):
            tsx = E("T", lambda e, tb=tb: e.matmul(
                ps1[:, 1536:2048], ONERF, SXR[:, tb * 512:(tb + 1) * 512],
                start=True, stop=True), (("V", em.now("V")),), inc=True)
            for kc in range(8):
                E("V", lambda e, tb=tb, kc=kc: e.tensor_tensor(
                    SCR.ap()[:, 0:512], PX.ap()[:, kc * 2048 + tb * 512:
                                                kc * 2048 + (tb + 1) * 512],
                    ps1[:, 1536:2048], ALU.mult),
                  (("T", tsx), ("V", em.now("V"))), inc=True)
                E("V", lambda e, tb=tb, kc=kc: e.tensor_scalar(
                    PA.ap()[:, kc * 2048 + tb * 512: kc * 2048 + (tb + 1) * 512],
                    SCR.ap()[:, 0:512], MAGIC, MAGIC, ALU.add, ALU.subtract),
                  (("V", em.now("V")),), inc=True)
        return {"xq_v": em.now("V"), "r1b_v": em.now("V")}

    def ph_ffnA(L, ev):
        cwL = CW[L]
        cbL = CB[L]
        w1 = dr["w1T0" if L == 0 else "w1T1"]
        Mp = PB.ap()[:, 0:2048]
        Mn = PB.ap()[:, 2048:4096]
        load_actset("silu_and_others")
        E("V", lambda e: e.memset(Mp, -3.0e38), (("T", ev.get("attn_t", 0)),))
        E("V", lambda e: e.memset(Mn, 3.0e38))
        E("V", lambda e: e.memset(PO.ap()[:, 0:1], 0.0))
        E("V", lambda e: e.memset(PO.ap()[:, 2049:2050], 0.0), inc=True)
        vinit = em.now("V")
        vq = {}
        vus = {}
        vamax = {}
        ysp = {}
        sse = {}
        tm1s = {}
        for ct in range(32):
            buf = WS.ap()[:, (ct % 2) * 1024:(ct % 2) * 1024 + 1024]
            qb_ = W1Q[:, (ct % 2) * 1024:(ct % 2) * 1024 + 1024]
            yw1 = "yw1%d" % (ct % 2)
            gw = E("Y", lambda e, ct=ct, buf=buf: e.dma_start(
                buf.rearrange("p (kc j) -> p kc j", j=128),
                w1[:, ct * 128:(ct + 1) * 128].rearrange("(kc p) j -> p kc j", p=128)),
                (("V", vq[ct - 2]),) if ct >= 2 else (("V", ev["wmean_v"]),),
                inc=True, tag=yw1)
            E("V", lambda e, buf=buf: e.tensor_scalar(
                SCR.ap()[:, 1024:2048], buf, WSR[0], MAGIC, ALU.mult, ALU.add),
              ((yw1, gw), ("T", tm1s.get(ct - 2, 0)), ("V", em.now("V"))), inc=True)
            E("V", lambda e: e.tensor_scalar(
                SCR.ap()[:, 1024:2048], SCR.ap()[:, 1024:2048], MAGIC, -1.0,
                ALU.subtract, ALU.max), (("V", em.now("V")),), inc=True)
            E("V", lambda e, qb_=qb_: e.tensor_scalar(
                qb_, SCR.ap()[:, 1024:2048], 1.0, None, ALU.min),
              (("V", em.now("V")),), inc=True)
            vq[ct] = em.now("V")
            for kc in range(8):
                for tb in range(4):
                    w = []
                    if kc == 0 and tb == 0:
                        w = [("V", vq[ct]), ("V", ev["xq_v"])]
                        if ct >= 1:
                            w.append(("V", vus[ct - 1]))
                    E("T", lambda e, kc=kc, tb=tb, qb_=qb_: e.matmul(
                        ps0[:, tb * 512:(tb + 1) * 512], qb_[:, kc * 128:(kc + 1) * 128],
                        PA.ap()[:, kc * 2048 + tb * 512: kc * 2048 + (tb + 1) * 512],
                        start=(kc == 0), stop=(kc == 7)), tuple(w),
                      inc=(kc == 7 and tb == 3))
            tm1 = em.now("T")
            tm1s[ct] = tm1
            E("V", lambda e: e.tensor_tensor(
                PO.ap()[:, 1:2049], ps0[:, 0:2048], PDf[:, 0:2048], ALU.mult),
              (("T", tm1), ("V", ev["r1b_v"])), inc=True)
            vus[ct] = em.now("V")
            for t in range(3):
                E("V", lambda e, t=t, ct=ct: e.tensor_scalar(
                    DIAG[t], IDENT, cwL[:, t * 32 + ct: t * 32 + ct + 1], None,
                    ALU.mult), (("T", em.now("T")),) if t == 0 else (),
                  inc=(t == 2))
            vdg = em.now("V")
            for tb in range(4):
                for t in range(3):
                    E("T", lambda e, tb=tb, t=t: e.matmul(
                        ps1[:, (tb % 2) * 512:(tb % 2) * 512 + 512], DIAG[t],
                        PO.ap()[:, tb * 512 + t: tb * 512 + t + 512],
                        start=(t == 0), stop=(t == 2)),
                      (("V", vdg), ("S", sse.get(ct - 1, 0))) if tb == 0 and t == 0
                      else ((("S", sse.get((ct, tb - 2), 0)),) if t == 0 and tb >= 2
                            else ()), inc=(t == 2))
                tcv = em.now("T")
                stg = PO.ap()[:, 4096 + (ct % 2) * 2048: 4096 + (ct % 2) * 2048 + 2048]
                E("S", lambda e, tb=tb, stg=stg, ct=ct: e.activation(
                    stg[:, tb * 512:(tb + 1) * 512],
                    ps1[:, (tb % 2) * 512:(tb % 2) * 512 + 512], AF.Silu,
                    bias=cbL[:, ct:ct + 1]),
                  (("T", tcv),) + ((("yhs%d" % (ct % 2), ysp[ct - 2]),)
                                   if tb == 0 and ct >= 2 else ()), inc=True)
                sse[(ct, tb)] = em.now("S")
            sse[ct] = em.now("S")
            stg = PO.ap()[:, 4096 + (ct % 2) * 2048: 4096 + (ct % 2) * 2048 + 2048]
            E("V", lambda e, stg=stg: e.tensor_tensor(Mp, Mp, stg, ALU.max),
              (("S", sse[ct]), ("V", vinit)) +
              ((("V", vamax[ct - 1]),) if ct >= 1 else ()))
            E("V", lambda e, stg=stg: e.tensor_tensor(Mn, Mn, stg, ALU.min), inc=True)
            vamax[ct] = em.now("V")
            ysp[ct] = E("Y", lambda e, ct=ct, stg=stg: e.dma_start(hsil_d[ct], stg),
                        (("S", sse[ct]),), inc=True, tag="yhs%d" % (ct % 2))
        return {"ffnA_v": em.now("V"), "ffnA_t": em.now("T"),
                "ffnA_y": (("yhs0", em.now("yhs0")), ("yhs1", em.now("yhs1"))),
                "ffnA_s": em.now("S"), "amax_v": vamax[31]}

    def ph_ffnmid(L, ev):
        Mp = PB.ap()[:, 0:2048]
        Mn = PB.ap()[:, 2048:4096]
        vneg = E("V", lambda e: e.tensor_scalar(Mn, Mn, -1.0, None, ALU.mult),
                 (("V", ev["ffnA_v"]),), inc=True)
        E("V", lambda e: e.tensor_tensor(Mp, Mp, Mn, ALU.max), (("V", vneg),),
          inc=True)
        vam = em.now("V")
        A2C = SMF.ap()[:, 603:619]
        vred = 0
        for tt in range(16):
            ttr = E("T", lambda e, tt=tt: e.transpose(
                ps1.ap().bitcast(BF16)[:, (tt % 2) * 1024:(tt % 2) * 1024 + 128]
                if False else
                ps1[:, (tt % 2) * 512:(tt % 2) * 512 + 64].bitcast(BF16),
                Mp[:, tt * 128:(tt + 1) * 128], IDENT),
                (("V", vam),) if tt < 2 else (("V", vred),), inc=True)
            E("V", lambda e, tt=tt: e.tensor_reduce(
                A2C[:, tt:tt + 1],
                ps1[:, (tt % 2) * 512:(tt % 2) * 512 + 64].bitcast(BF16),
                axis=AX.X, op=ALU.max), (("T", ttr),), inc=True)
            vred = em.now("V")
        E("V", lambda e: e.tensor_scalar(A2C, A2C, 1e-5, None, ALU.max),
          (("V", em.now("V")),), inc=True)
        vclip = em.now("V")
        y1 = E("Y", lambda e: e.dma_start(
            row_d[4:5, 0:2048].rearrange("o (c p) -> (o p) c", p=128), A2C,
            allow_slow_non_contiguous=True),
            (("V", vclip)) + tuple(ev["ffnA_y"])
            if False else (("V", vclip),) + tuple(ev["ffnA_y"]),
            inc=True, tag="yrow")
        AROW = POf[0:1, 4096:6144]
        y2 = E("Y", lambda e: e.dma_start(AROW, row_d[4:5, 0:2048]),
               (("yrow", y1),), inc=True, tag="yrow")
        S2R = POf[0:1, 6144:8192]
        vrc = E("V", lambda e: e.reciprocal(S2R, AROW),
                (("yrow", y2), ("V", em.now("V"))), inc=True)
        E("V", lambda e: e.tensor_scalar(S2R, S2R, 127.0, None, ALU.mult),
          (("V", vrc),))
        E("V", lambda e: e.tensor_scalar(R0[:, 0:2048], AROW, ev["mcq1"], None,
                                         ALU.mult), inc=True)
        vrows = em.now("V")
        for tb in range(4):
            t1 = E("T", lambda e, tb=tb: e.matmul(
                ps1[:, 1024:1536], ONERF, S2R[:, tb * 512:(tb + 1) * 512],
                start=True, stop=True),
                (("V", vrows), ("V", ev["ffnA_v"])), inc=True)
            E("V", lambda e, tb=tb: e.tensor_copy(
                PDf[:, 2048 + tb * 512: 2048 + (tb + 1) * 512], ps1[:, 1024:1536]),
              (("T", t1),), inc=True)
            t2 = E("T", lambda e, tb=tb: e.matmul(
                ps1[:, 1536:2048], ONERF, R0[0:1, tb * 512:(tb + 1) * 512],
                start=True, stop=True), (("V", em.now("V")),), inc=True)
            E("V", lambda e, tb=tb: e.tensor_copy(
                PDf[:, tb * 512:(tb + 1) * 512], ps1[:, 1536:2048]),
              (("T", t2),), inc=True)
        return {"mid_v": em.now("V"), "mid_y": em.now("yrow")}

    def ph_ffnB(L, ev, xsrc, xdst):
        load_actset("natural_log_exp_and_others")
        w2 = dr["w2T0" if L == 0 else "w2T1"]
        vqw = {}
        for cc in range(32):
            buf = WS.ap()[:, (cc % 2) * 1024:(cc % 2) * 1024 + 1024]
            yw2 = "yw2%d" % (cc % 2)
            gw = E("Y", lambda e, cc=cc, buf=buf: e.dma_start(
                buf, w2[cc * 128:(cc + 1) * 128, :]),
                (("V", vqw[cc - 2]),) if cc >= 2 else
                (("V", ev["ffnA_v"]), ("T", ev["ffnA_t"]), ("S", ev["ffnA_s"])),
                inc=True, tag=yw2)
            dst = (PO.ap()[:, cc * 1024:(cc + 1) * 1024] if cc < 16 else
                   PA.ap()[:, (cc - 16) * 1024:(cc - 15) * 1024])
            wA = [(yw2, gw), ("V", em.now("V"))]
            if cc == 0:
                wA.append(("V", ev["mid_v"]))
                wA.append(("yrow", ev["mid_y"]))
            if cc >= 16 and cc - 16 < 1:
                wA.append(("T", ev["ffnA_t"]))
            E("V", lambda e, buf=buf: e.tensor_scalar(
                SCR.ap()[:, 1024:2048], buf, WSR[1], MAGIC, ALU.mult, ALU.add),
              tuple(wA), inc=True)
            E("V", lambda e: e.tensor_scalar(
                SCR.ap()[:, 1024:2048], SCR.ap()[:, 1024:2048], MAGIC, -1.0,
                ALU.subtract, ALU.max), (("V", em.now("V")),), inc=True)
            E("V", lambda e, dst=dst: e.tensor_scalar(
                dst, SCR.ap()[:, 1024:2048], 1.0, None, ALU.min),
              (("V", em.now("V")),), inc=True)
            vqw[cc] = em.now("V")
        vw2 = em.now("V")
        tmp = PBf[:, 0:4096]
        outb = PBf[:, 4096:8192]
        CPB = SCRb[:, 3072:3584]
        SQB = SCRb[:, 3584:4096]
        yhs = {}
        vhq = {}
        tmm = {}
        youts = 0
        vev_prev = 0
        for tb in range(4):
            hqb = PXb[:, (tb % 2) * 16384:(tb % 2) * 16384 + 16384]
            for cc in range(32):
                q_ = ("yhq%d" % (tb % 2), "ghq%d" % (tb % 2))[cc % 2]
                yhs[(tb, cc)] = E(("Y", "G")[cc % 2],
                                  lambda e, tb=tb, cc=cc, hqb=hqb: e.dma_start(
                    hqb[:, cc * 512:(cc + 1) * 512],
                    hsil_d[cc][:, tb * 512:(tb + 1) * 512]),
                    ((tuple(ev["ffnA_y"]) + tuple(ev["xland"])) if tb == 0
                     else ()) +
                    ((("T", tmm.get(tb - 2, 0)),) if tb >= 2 and cc <= 1 else ()),
                    inc=True, tag=q_)
            hq_full = (("yhq%d" % (tb % 2), em.now("yhq%d" % (tb % 2))),
                       ("ghq%d" % (tb % 2), em.now("ghq%d" % (tb % 2))))
            for cc in range(32):
                wq_ = hq_full + (("V", ev["mid_v"]), ("V", em.now("V")))
                E("V", lambda e, cc=cc, hqb=hqb, tb=tb: e.tensor_tensor(
                    SCR.ap()[:, 0:512], hqb[:, cc * 512:(cc + 1) * 512],
                    PDf[:, 2048 + tb * 512: 2048 + (tb + 1) * 512], ALU.mult),
                  wq_, inc=True)
                E("V", lambda e, cc=cc, hqb=hqb: e.tensor_scalar(
                    hqb[:, cc * 512:(cc + 1) * 512], SCR.ap()[:, 0:512], MAGIC,
                    MAGIC, ALU.add, ALU.subtract), (("V", em.now("V")),), inc=True)
                vhq[(tb, cc)] = em.now("V")
            v_evict = {}
            for Mt in range(8):
                yxr = "yxr%d" % (Mt % 2)
                yx = E("Y", lambda e, Mt=Mt, tb=tb: e.dma_start(
                    WS.ap()[:, (Mt % 2) * 512:(Mt % 2) * 512 + 512],
                    xsrc[Mt * 128:(Mt + 1) * 128, tb * 512:(tb + 1) * 512]),
                    (("V", v_evict[Mt - 2]),) if Mt >= 2 else
                    ((("V", vw2),) + tuple(ev["xland"]) if tb == 0 and Mt == 0 else
                     ((("V", vev_prev),) if Mt == 0 else ())), inc=True, tag=yxr)
                for cc in range(32):
                    lhs = (PO.ap()[:, cc * 1024 + Mt * 128: cc * 1024 + (Mt + 1) * 128]
                           if cc < 16 else
                           PA.ap()[:, (cc - 16) * 1024 + Mt * 128:
                                   (cc - 16) * 1024 + (Mt + 1) * 128])
                    w = []
                    if Mt == 0:
                        w = [("V", vhq[(tb, cc)]), ("V", vqw[cc])]
                    elif cc == 0:
                        w = [("V", vhq[(tb, 31)]), ("V", vw2)]
                    if cc == 0 and Mt >= 4:
                        w.append(("V", v_evict[Mt - 4]))
                    E("T", lambda e, cc=cc, Mt=Mt, lhs=lhs, hqb=hqb: e.matmul(
                        ps0[:, (Mt % 4) * 512:(Mt % 4) * 512 + 512], lhs,
                        hqb[:, cc * 512:(cc + 1) * 512],
                        start=(cc == 0), stop=(cc == 31)), tuple(w), inc=(cc == 31))
                ttm = em.now("T")
                E("V", lambda e, Mt=Mt, tb=tb: e.scalar_tensor_tensor(
                    SCR.ap()[:, 512:1024], ps0[:, (Mt % 4) * 512:(Mt % 4) * 512 + 512],
                    1.0, PDf[:, tb * 512:(tb + 1) * 512], ALU.mult, ALU.mult),
                  (("T", ttm), ("V", em.now("V"))), inc=True)
                E("V", lambda e, Mt=Mt: e.tensor_tensor(
                    tmp[:, Mt * 512:(Mt + 1) * 512], SCR.ap()[:, 512:1024],
                    WS.ap()[:, (Mt % 2) * 512:(Mt % 2) * 512 + 512], ALU.add),
                  ((yxr, yx), ("V", em.now("V"))), inc=True)
                v_evict[Mt] = em.now("V")
            tmm[tb] = em.now("T")
            for Mt in range(8):
                tmps = tmp[:, Mt * 512:(Mt + 1) * 512]
                E("V", lambda e, tmps=tmps: e.tensor_copy(CPB, tmps),
                  (("T", em.now("T")),))
                E("V", lambda e, tmps=tmps: e.tensor_tensor(SQB, tmps, tmps,
                                                            ALU.mult), inc=True)
                vst = em.now("V")
                E("T", lambda e, Mt=Mt: e.matmul(
                    ps1[0:1, 0:512], ONESB, CPB, start=(Mt == 0), stop=(Mt == 7)),
                  (("V", vst), ("S", ev["ffnA_s"])))
                E("T", lambda e, Mt=Mt: e.matmul(
                    ps1[0:1, 512:1024], ONESB, SQB, start=(Mt == 0), stop=(Mt == 7)),
                  inc=True)
            tst = em.now("T")
            rmu = ROWS.ap()[0:1, 0:512]
            rsd = ROWS.ap()[0:1, 512:1024]
            vmu1 = E("V", lambda e, rmu=rmu: e.tensor_scalar(
                rmu, ps1[0:1, 0:512], 1.0 / 1024, None, ALU.mult), (("T", tst),),
                inc=True)
            E("V", lambda e, rmu=rmu, rsd=rsd: e.tensor_tensor(rsd, rmu, rmu,
                                                               ALU.mult),
              (("V", vmu1),), inc=True)
            E("V", lambda e, rsd=rsd: e.scalar_tensor_tensor(
                rsd, ps1[0:1, 512:1024], 1.0 / 1024, rsd, ALU.mult, ALU.subtract),
              (("V", em.now("V")),), inc=True)
            vvar = em.now("V")
            sl = E("S", lambda e, rsd=rsd: e.activation(rsd, rsd, AF.Ln, bias=EPSR),
                   (("V", vvar),), inc=True)
            sl = E("S", lambda e, rsd=rsd: e.activation(rsd, rsd, AF.Exp, scale=-0.5),
                   (("S", sl),), inc=True)
            t1 = E("T", lambda e, rmu=rmu: e.matmul(ps1[:, 1024:1536], ONERF, rmu,
                                                    start=True, stop=True),
                   (("S", sl),))
            t2 = E("T", lambda e, rsd=rsd: e.matmul(ps1[:, 1536:2048], ONERF, rsd,
                                                    start=True, stop=True), inc=True)
            tbb = em.now("T")
            E("V", lambda e: e.tensor_copy(SCR.ap()[:, 1024:1536], ps1[:, 1024:1536]),
              (("T", tbb),))
            E("V", lambda e: e.tensor_copy(PDf[:, 4096:4608], ps1[:, 1536:2048]),
              inc=True)
            vbb = em.now("V")
            for Mt in range(8):
                tmps = tmp[:, Mt * 512:(Mt + 1) * 512]
                E("V", lambda e, tmps=tmps: e.tensor_tensor(
                    tmps, tmps, SCR.ap()[:, 1024:1536], ALU.subtract),
                  (("V", vbb), ("yo1", youts)) if Mt == 0 else (), inc=True)
                E("V", lambda e, tmps=tmps, Mt=Mt: e.tensor_tensor(
                    outb[:, Mt * 512:(Mt + 1) * 512], tmps, PDf[:, 4096:4608],
                    ALU.mult), (("V", em.now("V")),), inc=True)
            vln = em.now("V")
            vev_prev = vln
            youts = E("Y", lambda e, tb=tb: e.dma_start(
                xdst[:, tb * 512:(tb + 1) * 512].rearrange("(mt p) t -> p mt t",
                                                           p=128),
                outb.rearrange("p (mt t) -> p mt t", t=512)),
                (("V", vln),), inc=True, tag="yo1")
        return {"ffnB_y": youts, "ffnB_v": em.now("V"), "ffnB_t": em.now("T"),
                "ffnB_s": em.now("S"), "xland": (("yo1", youts),)}

    evs = {}
    evs.update(ph_init())
    for L in range(2):
        ev_pre = dict(evs)
        if L == 0:
            evs.update(ph_dbuild(evs))
            xsrc = dr["xT"]
        else:
            evs.update(ph_tbuild(evs, pre=(("T", evs["ffnB_t"]),
                                           ("V", evs["ffnB_v"]))))
            xsrc = xh[0]
        pa_free = () if L == 0 else (("T", evs["ffnB_t"]),) + tuple(evs["xland"])
        if L == 0:
            dump("init_smf", SMF.ap()[:, 0:768], None)
            dump("init_smb", SMB.ap()[:, 0:1024], None)
            dump("dbuild_pd", PD.ap()[:, 0:2056], PD.ap()[:, 0:1025])
            dump("dbuild_stk", SMF.ap()[:, 474:490], SMF.ap()[0:1, 474:490])
            dump("dbuild_norc", SMF.ap()[:, 490:499])
            dump("dbuild_frep", SMF.ap()[:, 506:507])
        evs.update(ph_prep(L, xsrc, evs, pa_free=pa_free,
                           ps_free=tuple(evs.get("ps_free", ()))))
        if L == 0:
            dump("prep_rows", ROWS.ap()[:, 0:2048], ROWS.ap()[0:1, 0:2048])
            dump("prep_pxs", PX.ap()[0:32, 0:6144], PX.ap()[0:1, 0:6144])
            dump("prep_pa", PA.ap()[:, 0:2048])
        pbf = () if L == 0 else (("yo1", evs["ffnB_y"]),)
        evs.update(ph_qkv(L, evs, po_free=tuple(evs.get("po_free", ())) + pbf,
                          pxb_free=pbf, ps_free=tuple(evs.get("ps_free", ()))))
        if L == 0:
            dump("qkv_px", PX.ap()[:, 0:1024])
            dump("qkv_pxk", PX.ap()[:, 8192:9216])
            dump("qkv_pb", PB.ap()[:, 0:260])
        evs.update(ph_bounds(L, evs))
        if L == 0:
            dump("bounds_rows", ROWS.ap()[:, 0:2048], ROWS.ap()[8:24, 0:1024])
        wgen = ph_wmean_gen(L, evs) if stage > 2 * L + 1 else None
        evs.update(ph_attncore(L, evs, bg=wgen))
        if L == 0:
            dump("attn_po", PO.ap()[:, 0:2048])
            dump("attn_pa", PAf[:, 0:2048])
            if DBGHW:
                E("V", lambda e: e.tensor_copy(WS.ap()[:, 0:2048],
                                               ps0.ap()[:, 0:2048]),
                  tuple((k, v) for k, v in em.cnt.items() if v > 0), inc=True)
                dump("attn_ps0", WS.ap()[:, 0:2048])
                ps1v = (ps1.ap()[0:65, 0:1024] if DBG_SIM
                        else ps1.ap()[:, 0:2048])
                wsv = (WS.ap()[0:65, 0:1024] if DBG_SIM
                       else WS.ap()[:, 0:2048])
                E("V", lambda e, ps1v=ps1v, wsv=wsv: e.tensor_copy(wsv, ps1v),
                  (("ydbg", em.now("ydbg")),), inc=True)
                dump("attn_ps1", WS.ap()[:, 0:2048], WS.ap()[0:65, 0:1024])
            dump("attn_scr", SCR.ap()[0:64, 0:1024], SCR.ap()[0:64, 0:512])
        last = (stage == 2 * L + 1)
        pr = ph_proj(L, evs, xsrc, yT if last else xh[1])
        evs.update(pr)
        if L == 0:
            dump("proj_pa", PAf[0:32, 0:2048])
        if last:
            break
        evs.update(ph_wmean_fin(L))
        evs.update(ph_quantx(L, evs))
        evs.update(ph_ffnA(L, evs))
        evs.update(ph_ffnmid(L, evs))
        last = (stage == 2 * L + 2)
        fb = ph_ffnB(L, evs, xh[1], yT if last else xh[0])
        evs.update(fb)
        if last:
            break

    dma_tags = sorted({t for (eng, _, _, _, t) in em.steps if eng in ("Y", "G")})
    em.emit("Y", lambda e: e.nop(),
            tuple((t, em.now(t)) for t in dma_tags), False, tag="ynop")

    DBG.clear()
    DBG['em'] = em
    DBG.update(dict(PX=PX.ap(), PXb=PXb, PA=PA.ap(), PB=PB.ap(), PO=PO.ap(),
                    PD=PD.ap(), SMF=SMF.ap(), SMB=SMB.ap(), SCR=SCR.ap(),
                    ROWS=ROWS.ap(), WS=WS.ap(), SM2=SM2.ap(), ps0=ps0.ap(), ps1=ps1.ap()))
    all_tags = sorted({t for (_, _, _, _, t) in em.steps} |
                      {we for (_, _, ws, _, _) in em.steps for (we, _) in ws})
    sems = {t: es.enter_context(nc.semaphore(f"sem{t}")) for t in all_tags}
    eng_map = {"T": "tensor", "V": "vector", "S": "scalar", "Y": "sync", "G": "gpsimd"}
    with nc.Block() as block:
        def mk(engkey):
            def body(e):
                hw = dict.fromkeys(all_tags, 0)
                for eng, fn, waits, n, t in em.steps:
                    if eng != engkey:
                        continue
                    for (we, wv) in waits:
                        if isinstance(wv, str):
                            raise RuntimeError("unresolved wait " + wv)
                        if wv > hw[we]:
                            e.wait_ge(sems[we], wv)
                            hw[we] = wv
                    inst = fn(e)
                    if n:
                        inst.then_inc(sems[t], n)
            return body
        for k, name in eng_map.items():
            getattr(block, name)(mk(k))

    es.close()
    return nc


def kernel(**inp):
    f32 = np.float32
    x = np.asarray(inp["L0_attn_x"], f32)
    hh0, hh1 = _make_hh()
    sc_ = np.asarray(inp["L0_attn_spatial_centers"], f32)
    csT = np.tile(sc_[0].reshape(2, 1), (1, 1152)).astype(f32)
    csT[:, 1:1025] = sc_.T
    shared = {
        "wqkvT0": np.ascontiguousarray(np.asarray(inp["L0_attn_W_qkv"], f32).T),
        "wprojT0": np.ascontiguousarray(np.asarray(inp["L0_attn_W_proj"], f32).T),
        "bproj0": _col(inp["L0_attn_b_proj"], 8),
        "gamma0": _col(inp["L0_attn_ln_gamma"], 8),
        "beta0": _col(inp["L0_attn_ln_beta"], 8),
        "csT": csT,
        "w1T0": np.ascontiguousarray(np.asarray(inp["L0_ffn_w1"], f32).T),
        "cw0": np.concatenate([_col(np.asarray(inp["L0_ffn_conv_w"], f32)[:, 0, t], 32)
                               for t in range(3)], axis=1),
        "cb0": _col(inp["L0_ffn_conv_b"], 32),
        "w2T0": np.ascontiguousarray(np.asarray(inp["L0_ffn_w2"], f32).T),
        "wqkvT1": np.ascontiguousarray(np.asarray(inp["L1_attn_W_attn"], f32).T),
        "battn1": _col(np.asarray(inp["L1_attn_b_attn"], f32)[0:2048], 16),
        "bv1": np.ascontiguousarray(np.asarray(inp["L1_attn_b_attn"], f32)[2048:3072]
                                    .reshape(1, 1024)),
        "wprojT1": np.ascontiguousarray(np.asarray(inp["L1_attn_W_proj"], f32).T),
        "bproj1": _col(inp["L1_attn_b_proj"], 8),
        "relemb": np.asarray(inp["L1_attn_rel_emb"], f32),
        "w1T1": np.ascontiguousarray(np.asarray(inp["L1_ffn_w1"], f32).T),
        "cw1": np.concatenate([_col(np.asarray(inp["L1_ffn_conv_w"], f32)[:, 0, t], 32)
                               for t in range(3)], axis=1),
        "cb1": _col(inp["L1_ffn_conv_b"], 32),
        "w2T1": np.ascontiguousarray(np.asarray(inp["L1_ffn_w2"], f32).T),
        "ident": np.eye(128, dtype=ml_dtypes.bfloat16),
        "identf": np.eye(128, dtype=np.float32),
        "hh0": hh0, "hh1": hh1,
    }
    nc = build(bass.Bass("TRN2", target_bir_lowering=False,
                         detect_race_conditions=RACECHECK), stage=STAGE)
    in_maps = []
    for b in range(8):
        m = dict(shared)
        m["xT"] = np.ascontiguousarray(x[b].T)
        in_maps.append(m)
    res = run_bass_kernel_spmd(nc, in_maps, core_ids=list(range(8)))
    return np.stack([np.asarray(r["yT"], f32).T for r in res.results], axis=0)


# version 05
# speedup vs baseline: 1.0916x; 1.0916x over previous
import math
import numpy as np
import ml_dtypes
import concourse.bass as bass
import concourse.mybir as mybir
from concourse.bass_utils import run_bass_kernel_spmd
from contextlib import ExitStack

F32 = mybir.dt.float32
BF16 = mybir.dt.bfloat16
AF = mybir.ActivationFunctionType
ALU = mybir.AluOpType
AX = mybir.AxisListType

MAGIC = float(2 ** 23 + 2 ** 22)
NEGBIG = -8.0e30
LN_EPS = 1e-5

STAGE = 4
DBG = {}
DBGHW = False
DBG_SIM = False
DBG_NO_M = False
DBG_NO_D = False
RACECHECK = True


def _bucket(d):
    if d < 16:
        return d
    v = 16 + int(np.float32(np.log(np.float32(d) / 16 + 1e-10) / math.log(8.0) * 16))
    return min(v, 31)


def _make_hh():
    hh0 = np.zeros((33, 128 * 128), dtype=np.float32)
    hh1 = np.zeros((33, 128 * 128), dtype=np.float32)
    for k in range(128):
        for q in range(128):
            i = k * 128 + q
            d0 = q - k
            if d0 >= 0:
                hh0[_bucket(d0), i] = 1.0
            else:
                hh0[32, i] = 1.0
            hh1[_bucket(d0 + 128), i] = 1.0
    return hh0.astype(ml_dtypes.bfloat16), hh1.astype(ml_dtypes.bfloat16)


def _col(v, n):
    return np.ascontiguousarray(np.asarray(v, np.float32).reshape(n, 128).T)


ENGS = ("T", "V", "S", "Y", "G")


class EM:

    def __init__(self):
        self.steps = []
        self.cnt = {}

    def emit(self, eng, fn, waits=(), inc=False, tag=None):
        t = tag if tag is not None else eng
        if eng in ("Y", "G"):
            assert t not in ("Y", "G"), "DMA steps need an explicit tag"
        n = (16 if eng in ("Y", "G") else 1) if inc else 0
        self.cnt[t] = self.cnt.get(t, 0) + n
        self.steps.append((eng, fn, tuple(waits), n, t))
        return self.cnt[t]

    def now(self, tag):
        return self.cnt.get(tag, 0)


def build(nc, stage=4):
    D, T = 1024, 2048

    def din(name, shape, dt=F32):
        return nc.dram_tensor(name, list(shape), dt, kind="ExternalInput").ap()

    dr = {k: din(k, s, d) for k, s, d in [
        ("xT", (D, T), F32),
        ("wqkvT0", (D, 3 * D), F32), ("wprojT0", (D, D), F32),
        ("bproj0", (128, 8), F32), ("gamma0", (128, 8), F32), ("beta0", (128, 8), F32),
        ("csT", (2, 1152), F32),
        ("w1T0", (D, 4096), F32), ("cw0", (128, 96), F32), ("cb0", (128, 32), F32),
        ("w2T0", (4096, D), F32),
        ("wqkvT1", (D, 3 * D), F32), ("battn1", (128, 16), F32), ("bv1", (1, D), F32),
        ("wprojT1", (D, D), F32), ("bproj1", (128, 8), F32), ("relemb", (32, 16), F32),
        ("w1T1", (D, 4096), F32), ("cw1", (128, 96), F32), ("cb1", (128, 32), F32),
        ("w2T1", (4096, D), F32),
        ("ident", (128, 128), BF16), ("identf", (128, 128), F32),
        ("hh0", (33, 16384), BF16), ("hh1", (33, 16384), BF16),
    ]}

    yT = nc.dram_tensor("yT", [D, T], F32, kind="ExternalOutput").ap()
    xh = [nc.dram_tensor(f"xh{i}", [D, T], F32, kind="Internal").ap() for i in range(2)]
    hsil_d = nc.dram_tensor("hsil_d", [32, 128, T], BF16, kind="Internal").ap()
    w2q_d = nc.dram_tensor("w2q_d", [32, 128, D], BF16, kind="Internal").ap()
    tflat_d = nc.dram_tensor("tflat_d", [2, 8, 16, 2048], BF16, kind="Internal").ap()
    row_d = nc.dram_tensor("row_d", [8, T], F32, kind="Internal").ap()

    es = ExitStack()

    def sb(name, shape, dt):
        return es.enter_context(nc.sbuf_tensor(name, list(shape), dt))

    PX = sb("PX", (128, 16384), F32)
    PA = sb("PA", (128, 16384), BF16)
    PB = sb("PB", (128, 16640), BF16)
    PO = sb("PO", (128, 16384), BF16)
    PD = sb("PD", (128, 9252), BF16)
    WS = sb("WS", (128, 2048), F32)
    SCR = sb("SCR", (128, 2048), F32)
    SMB = sb("SMB", (128, 1024), BF16)
    SMF = sb("SMF", (128, 768), F32)
    SM2 = sb("SM2", (33, 64), F32)
    ROWS = sb("ROWS", (128, 2048), F32)

    ps0 = es.enter_context(nc.psum_tensor("ps0", [128, 2048], F32))
    ps1 = es.enter_context(nc.psum_tensor("ps1", [128, 2048], F32))

    PXb = PX.ap().bitcast(BF16)
    PBf = PB.ap().bitcast(F32)
    POf = PO.ap().bitcast(F32)
    PAf = PA.ap().bitcast(F32)
    PDf = PD.ap().bitcast(F32)
    WSb = WS.ap().bitcast(BF16)
    SCRb = SCR.ap().bitcast(BF16)
    FLT = SCRb[0:16, 0:2048]
    W1Q = PB.ap()[:, 4096:6144]

    IDENT = SMB.ap()[:, 0:128]
    ONESB = SMB.ap()[:, 128:129]
    ONER = SMB.ap()[0:1, 130:258]
    DIAG = [SMB.ap()[:, 258 + i * 128: 386 + i * 128] for i in range(3)]
    GAMB = SMB.ap()[:, 672:680]
    IDF = SMF.ap()[:, 640:768]

    ONERF = SMF.ap()[0:1, 0:128]
    BPB0 = SMF.ap()[:, 128:136]
    GAM0 = SMF.ap()[:, 136:144]
    NGAM0 = SMF.ap()[:, 144:152]
    BT1 = SMF.ap()[:, 152:168]
    BP1 = SMF.ap()[:, 176:184]
    CB = [SMF.ap()[:, 184 + i * 32: 216 + i * 32] for i in range(2)]
    CW = [SMF.ap()[:, 248 + i * 96: 344 + i * 96] for i in range(2)]
    NEGM = SMF.ap()[:, 440:456]
    AMC = SMF.ap()[:, 456:472]
    WSR = [SMF.ap()[:, 472 + i: 473 + i] for i in range(2)]
    STK = SMF.ap()[:, 474:490]
    NORC = SMF.ap()[:, 490:506]
    FREP = SMF.ap()[:, 506:507]
    BETA0 = SMF.ap()[:, 507:515]
    EPSR = SMF.ap()[0:1, 515:516]
    C30R = SMF.ap()[0:1, 516:517]
    ONESFC = SMF.ap()[:, 522:523]
    C1 = SMF.ap()[:, 520:521]
    C2 = SMF.ap()[:, 521:522]

    CST = POf[0:2, 4864:6016]
    CSN2 = POf[0:2, 1280:2432]
    CS2 = POf[0:2, 2560:3712]
    NORR = POf[0:1, 3712:4864]
    R0 = ROWS.ap()[0:1, :]
    R_RS0 = ROWS.ap()[32:33, :]
    R_M0R0 = ROWS.ap()[64:65, :]
    R96 = ROWS.ap()[96:97, :]

    em = EM()
    E = em.emit

    from concourse.hw_specs import get_activation_tables
    _set_ids = {name: i for i, name in enumerate(get_activation_tables(nc.m.arch))}

    def load_actset(name, waits=()):
        sid = _set_ids[name]
        return E("S", lambda e, sid=sid: e.add_instruction(
            mybir.InstLoadActFuncSet(
                name=nc.get_next_instruction_name(), act_func_set_id=sid,
                ins=[], outs=[])), waits, inc=False)

    def dump(name, ap, sim_ap="same"):
        if not DBGHW:
            return
        if DBG_SIM:
            if sim_ap is None:
                return
            if sim_ap != "same":
                ap = sim_ap
        d = nc.dram_tensor("dbg_" + name, [ap.shape[0], ap.shape[1]], ap.dtype,
                           kind="ExternalOutput").ap()
        w = tuple((k, v) for k, v in em.cnt.items() if v > 0)
        E("Y", lambda e, d=d, ap=ap: e.dma_start(d, ap), w, inc=True, tag="ydbg")

    def ph_init():
        for dst, src in [(IDENT, dr["ident"]), (IDF, dr["identf"]),
                         (BPB0, dr["bproj0"]), (GAM0, dr["gamma0"]),
                         (BETA0, dr["beta0"]), (BT1, dr["battn1"]),
                         (BP1, dr["bproj1"]), (CB[0], dr["cb0"]), (CB[1], dr["cb1"]),
                         (CW[0], dr["cw0"]), (CW[1], dr["cw1"]), (CST, dr["csT"]),
                         (SM2.ap()[0:32, 0:16], dr["relemb"])]:
            E("Y", lambda e, dst=dst, src=src: e.dma_start(dst, src), inc=True,
              tag="yinit")
        yw = em.now("yinit")
        E("V", lambda e: e.memset(ONESB, 1.0))
        E("V", lambda e: e.memset(ONER, 1.0))
        E("V", lambda e: e.memset(ONERF, 1.0))
        E("V", lambda e: e.memset(EPSR, LN_EPS))
        E("V", lambda e: e.memset(C30R, 1e-30))
        E("V", lambda e: e.memset(ONESFC, 1.0))
        E("V", lambda e: e.tensor_tensor(BPB0, BPB0, BETA0, ALU.add),
          (("yinit", yw),))
        E("V", lambda e: e.tensor_copy(GAMB, GAM0))
        E("V", lambda e: e.tensor_scalar(NGAM0, GAM0, -1.0, None, ALU.mult), inc=True)
        return {"init_v": em.now("V"), "init_y": yw}

    def ph_dbuild(ev):
        E("V", lambda e: e.tensor_scalar(CSN2, CST, -2.0, None, ALU.mult),
          (("yinit", ev["init_y"]),))
        vcs = E("V", lambda e: e.tensor_tensor(CS2, CST, CST, ALU.mult), inc=True)
        for j, w_ in ((0, 512), (512, 512), (1024, 128)):
            E("T", lambda e, j=j, w_=w_: e.matmul(
                ps1[0:1, j: j + w_], ONESFC[0:2, :], CS2[:, j:j + w_],
                start=True, stop=True), (("V", vcs),) if j == 0 else (),
              inc=(j == 1024))
        tno = em.now("T")
        E("V", lambda e: e.tensor_copy(NORR, ps1[0:1, 0:1152]), (("T", tno),),
          inc=True)
        vnorm = em.now("V")
        ynorr = E("Y", lambda e: e.dma_start(row_d[0:1, 0:1152], NORR),
                  (("V", vnorm),), inc=True, tag="yrow")
        ynorc = E("Y", lambda e: e.dma_start(
            NORC[:, 0:9],
            row_d[0:1, 0:1152].rearrange("o (c p) -> (o p) c", p=128),
            allow_slow_non_contiguous=True), (("yrow", ynorr),), inc=True,
            tag="yrow")
        E("V", lambda e: e.memset(STK[:, 0:9], -3.0e38), inc=True)
        vstk = em.now("V")
        E("S", lambda e: e.preload_act_table(AF.Sqrt))
        vprev = 0
        for kc in range(9):
            qn = [(0, 512), (512, 512), (1024, 1)]
            for qi, (q0, qw) in enumerate(qn):
                w = []
                if qi == 0:
                    w = [("V", vnorm)] if kc == 0 else [("V", vprev)]
                E("T", lambda e, kc=kc, q0=q0, qw=qw, qi=qi: e.matmul(
                    ps0[:, q0:q0 + qw], CST[:, kc * 128:(kc + 1) * 128],
                    CSN2[:, q0:q0 + qw], start=(True if qi == 0 else True), stop=False),
                  tuple(w))
                E("T", lambda e, q0=q0, qw=qw, qi=qi: e.matmul(
                    ps0[:, q0:q0 + qw], ONERF, NORR[:, q0:q0 + qw],
                    start=False, stop=True), inc=(qi == len(qn) - 1))
            tmm = em.now("T")
            E("V", lambda e, kc=kc: e.tensor_scalar(
                POf[:, 0:1025], ps0[:, 0:1025], NORC[:, kc:kc + 1], 1e-30,
                ALU.add, ALU.max),
              (("T", tmm), ("yrow", ynorc), ("S", em.now("S"))), inc=True)
            vd2 = em.now("V")
            E("V", lambda e, kc=kc: e.tensor_reduce(
                STK[:, kc:kc + 1], POf[:, 0:1025], axis=AX.X, op=ALU.max),
              (("V", vstk), ("V", vd2)), inc=True)
            vprev = em.now("V")
            E("S", lambda e, kc=kc: e.activation(
                PD.ap()[:, kc * 1028: kc * 1028 + 1025], POf[:, 0:1025], AF.Sqrt),
              (("V", vd2),), inc=True)
        sqd = em.now("S")
        vred = E("V", lambda e: e.tensor_reduce(
            STK[:, 15:16], STK[:, 0:9], axis=AX.X, op=ALU.max),
            (("V", em.now("V")),), inc=True)
        tmx = E("T", lambda e: e.transpose(ps1[0:1, 0:128], STK[:, 15:16], IDF),
                (("V", vred),), inc=True)
        E("V", lambda e: e.tensor_reduce(STK[0:1, 14:15], ps1[0:1, 0:128], axis=AX.X,
                                         op=ALU.max), (("T", tmx),), inc=True)
        smx = E("S", lambda e: e.activation(STK[0:1, 13:14], STK[0:1, 14:15], AF.Sqrt),
                (("V", em.now("V")),), inc=True)
        E("V", lambda e: e.tensor_scalar(STK[0:1, 12:13], STK[0:1, 13:14], 1e-6, None,
                                         ALU.max), (("S", smx),), inc=True)
        E("V", lambda e: e.reciprocal(STK[0:1, 11:12], STK[0:1, 12:13]),
          (("V", em.now("V")),), inc=True)
        E("V", lambda e: e.tensor_scalar(STK[0:1, 10:11], STK[0:1, 13:14], 1e-6, None,
                                         ALU.is_gt), inc=True)
        E("V", lambda e: e.tensor_tensor(STK[0:1, 9:10], STK[0:1, 11:12],
                                         STK[0:1, 10:11], ALU.mult),
          (("V", em.now("V")),), inc=True)
        vf = E("V", lambda e: e.tensor_scalar(STK[0:1, 9:10], STK[0:1, 9:10], -0.8,
                                              None, ALU.mult),
               (("V", em.now("V")),), inc=True)
        tfr = E("T", lambda e: e.matmul(ps1[:, 128:129], ONERF, STK[0:1, 9:10],
                                        start=True, stop=True), (("V", vf),), inc=True)
        vfr = E("V", lambda e: e.tensor_copy(FREP, ps1[:, 128:129]), (("T", tfr),),
                inc=True)
        for kc in range(9):
            E("V", lambda e, kc=kc: e.tensor_scalar(
                PD.ap()[:, kc * 1028: kc * 1028 + 1025],
                PD.ap()[:, kc * 1028: kc * 1028 + 1025], FREP, None, ALU.mult),
              (("S", sqd), ("V", vfr)), inc=(kc == 8))
        E("V", lambda e: e.tensor_copy(SCRb[0:1, 0:1025],
                                       PD.ap()[0:1, 8 * 1028: 8 * 1028 + 1025]),
          (("V", em.now("V")),), inc=True)
        E("V", lambda e: e.memset(PD.ap()[:, 8 * 1028: 8 * 1028 + 1025], 0.0),
          (("V", em.now("V")),), inc=True)
        E("V", lambda e: e.tensor_copy(PD.ap()[0:1, 8 * 1028: 8 * 1028 + 1025],
                                       SCRb[0:1, 0:1025]),
          (("V", em.now("V")),), inc=True)
        for kc in range(9):
            E("V", lambda e, kc=kc: e.memset(PD.ap()[:, kc * 1028: kc * 1028 + 1], 0.0),
              (("V", em.now("V")),) if kc == 0 else (), inc=(kc == 8))
        E("V", lambda e: e.memset(PD.ap()[0:1, 0:1025], 0.0),
          (("V", em.now("V")),), inc=True)
        return {"bias_ready": [("V", em.now("V"))],
                "ps_free": [("V", em.now("V")), ("S", em.now("S"))],
                "po_free": [("V", em.now("V")), ("S", em.now("S")),
                            ("T", em.now("T"))]}

    def ph_tbuild(ev, pre=()):
        relb = SM2.ap().bitcast(BF16)
        y31 = E("Y", lambda e: e.dma_start(POf[0:1, 3072:3088],
                                           SM2.ap()[31:32, 0:16]),
                tuple(pre) + (("yinit", ev["init_y"]), ("V", ev["init_v"])),
                inc=True, tag="y31")
        t31 = E("T", lambda e: e.matmul(ps1[0:32, 0:16], ONERF[0:1, 0:32],
                                        POf[0:1, 3072:3088], start=True,
                                        stop=True), (("y31", y31),), inc=True)
        E("V", lambda e: e.tensor_tensor(SM2.ap()[0:32, 16:32], SM2.ap()[0:32, 0:16],
                                         ps1[0:32, 0:16], ALU.subtract), (("T", t31),),
          inc=True)
        E("V", lambda e: e.memset(SMB.ap()[32:64, 680:696], 0.0))
        E("V", lambda e: e.tensor_scalar(SMB.ap()[0:32, 680:696],
                                         SM2.ap()[0:32, 16:32],
                                         8.0, None, ALU.mult),
          (("V", em.now("V")),), inc=True)
        E("V", lambda e: e.memset(SMB.ap()[32:33, 680:696], NEGBIG),
          (("V", em.now("V")),), inc=True)
        vrel = em.now("V")
        hhsrc = [dr["hh0"], dr["hh1"]]
        E("V", lambda e: e.memset(PO.ap()[32:64, 0:4096], 0.0), tuple(pre), inc=True)
        vz = em.now("V")
        prev_t = [0, 0]
        sprev = 0
        yfprev = 0
        for t in range(2):
            for g in range(8):
                hb = PO.ap()[0:33, (g % 2) * 2048:(g % 2) * 2048 + 2048]
                hb64 = PO.ap()[0:64, (g % 2) * 2048:(g % 2) * 2048 + 2048]
                ghh = "ghh%d" % (g % 2)
                gw = E("G", lambda e, t=t, g=g, hb=hb: e.dma_start(
                    hb, hhsrc[t][:, g * 2048:(g + 1) * 2048]),
                    tuple(pre) + (("T", prev_t[g % 2]), ("V", vz)), inc=True,
                    tag=ghh)
                for c in range(4):
                    E("T", lambda e, hb=hb, hb64=hb64, c=c: e.matmul(
                        ps0[0:16, c * 512:(c + 1) * 512], SMB.ap()[0:64, 680:696],
                        hb64[:, c * 512:(c + 1) * 512], start=True, stop=True),
                      ((ghh, gw), ("V", vrel), ("S", sprev)) if c == 0 else (),
                      inc=(c == 3))
                tm = em.now("T")
                prev_t[g % 2] = tm
                sf = E("S", lambda e: e.copy(FLT, ps0[0:16, 0:2048]),
                       (("T", tm), ("yflt", yfprev)), inc=True)
                sprev = sf
                yf = E("Y", lambda e, t=t, g=g: e.dma_start(tflat_d[t, g], FLT),
                       (("S", sf),), inc=True, tag="yflt")
                yfprev = yf
                E("Y", lambda e, t=t, g=g: e.dma_start(
                    PD.ap()[g * 16:(g + 1) * 16, t * 4096: t * 4096 + 2048]
                    .rearrange("kp (h j) -> kp h j", j=128),
                    tflat_d[t, g].rearrange("h (kp j) -> kp h j", j=128)),
                  (("yflt", yf),), inc=True, tag="ypd")
        return {"bias_ready": [("ypd", em.now("ypd"))],
                "ps_free": [("V", em.now("V")), ("S", em.now("S"))],
                "po_free": [("T", em.now("T")), ("S", em.now("S"))]}

    def ph_prep(L, xsrc, ev, pa_free=(), ps_free=()):
        for c in range(4):
            E("G", lambda e, c=c: e.dma_start(
                PA.ap()[:, c * 4096:(c + 1) * 4096].rearrange("p (kc t) -> p kc t", t=2048),
                xsrc.rearrange("(kc p) t -> p kc t", p=128)[:, 2 * c: 2 * c + 2, :]),
              tuple(pa_free), inc=True, tag="gxb")
        gxb = em.now("gxb")
        out = {"xb_g": gxb}
        if L == 0:
            for tb in range(4):
                for kc in range(8):
                    E("T", lambda e, tb=tb, kc=kc: e.matmul(
                        ps1[0:1, tb * 512:(tb + 1) * 512], ONESB,
                        PA.ap()[:, kc * 2048 + tb * 512: kc * 2048 + (tb + 1) * 512],
                        start=(kc == 0), stop=(kc == 7)),
                      (("gxb", gxb), ("V", ev["init_v"])) + tuple(ps_free)
                      if (kc == 0 and tb == 0) else (),
                      inc=(kc == 7))
            tsum = em.now("T")
            E("V", lambda e: e.tensor_scalar(R0, ps1[0:1, 0:2048], 1.0 / 1024, None,
                                             ALU.mult), (("T", tsum),), inc=True)
            vmu = em.now("V")
            sq = SCRb[:, 0:2048]
            for kc in range(8):
                E("V", lambda e, kc=kc: e.tensor_tensor(
                    sq, PA.ap()[:, kc * 2048:(kc + 1) * 2048],
                    PA.ap()[:, kc * 2048:(kc + 1) * 2048], ALU.mult),
                  (("T", em.now("T")),), inc=True)
                vsq = em.now("V")
                for tb in range(4):
                    E("T", lambda e, tb=tb, kc=kc: e.matmul(
                        ps1[0:1, tb * 512:(tb + 1) * 512], ONESB,
                        sq[:, tb * 512:(tb + 1) * 512],
                        start=(kc == 0), stop=(kc == 7)),
                      (("V", vsq), ("V", vmu)) if tb == 0 else (),
                      inc=(tb == 3))
            tsq = em.now("T")
            T0 = PX.ap()[0:1, 0:2048]
            T1 = PX.ap()[0:1, 2048:4096]
            T2 = PX.ap()[0:1, 4096:6144]
            vt0 = E("V", lambda e: e.tensor_tensor(T0, R0, R0, ALU.mult),
                    (("T", tsq),), inc=True)
            E("V", lambda e: e.scalar_tensor_tensor(T1, ps1[0:1, 0:2048], 1.0 / 1024,
                                                    T0, ALU.mult, ALU.subtract),
              (("V", vt0),), inc=True)
            vvar = em.now("V")
            load_actset("natural_log_exp_and_others")
            s1 = E("S", lambda e: e.activation(T2, T1, AF.Ln, bias=EPSR),
                   (("V", vvar),), inc=True)
            s2 = E("S", lambda e: e.activation(T1, T2, AF.Exp, scale=-0.5),
                   (("S", s1),), inc=True)
            E("V", lambda e: e.tensor_copy(R_RS0, T1), (("S", s2),))
            vt0 = E("V", lambda e: e.tensor_tensor(T0, R0, T1, ALU.mult), inc=True)
            E("V", lambda e: e.tensor_scalar(R_M0R0, T0, -1.0, None, ALU.mult),
              (("V", vt0),), inc=True)
            out["rows_v"] = em.now("V")
        return out

    def ph_qkv(L, ev, pxb_free=(), po_free=(), ps_free=()):
        wq = dr["wqkvT0"] if L == 0 else dr["wqkvT1"]
        gxb = ev["xb_g"]
        t_after_fmt = {}
        last_evict_v = em.now("V")
        for fmt in range(16):
            buf = WSb[:, (fmt % 2) * 1024:(fmt % 2) * 1024 + 1024]
            gwq = "gwq%d" % (fmt % 2)
            w = [("T", t_after_fmt[fmt - 2])] if fmt >= 2 else []
            gw = E("G", lambda e, fmt=fmt, buf=buf: e.dma_start(
                buf.rearrange("p (kc f) -> p kc f", f=128),
                wq[:, fmt * 128:(fmt + 1) * 128]
                .rearrange("(kc p) f -> p kc f", p=128)), tuple(w), inc=True,
                tag=gwq)
            for kc in range(8):
                for qb in range(4):
                    w = ()
                    if kc == 0 and qb == 0:
                        w = ((gwq, gw), ("gxb", gxb), ("V", last_evict_v)) + \
                            tuple((tuple(pxb_free) + tuple(ps_free)) if fmt == 0
                                  else ())
                    E("T", lambda e, kc=kc, qb=qb, buf=buf: e.matmul(
                        ps0[:, qb * 512:(qb + 1) * 512], buf[:, kc * 128:(kc + 1) * 128],
                        PA.ap()[:, kc * 2048 + qb * 512: kc * 2048 + (qb + 1) * 512],
                        start=(kc == 0), stop=(kc == 7)), w,
                      inc=(kc == 7 and qb == 3))
            tmm = em.now("T")
            t_after_fmt[fmt] = tmm
            base = (fmt % 8) * 2048 + (0 if fmt < 8 else 16384)
            for qb in range(4):
                dst = PXb[:, base + qb * 512: base + (qb + 1) * 512]
                if L == 1:
                    E("V", lambda e, dst=dst, qb=qb, fmt=fmt: e.tensor_scalar(
                        dst, ps0[:, qb * 512:(qb + 1) * 512], BT1[:, fmt:fmt + 1],
                        None, ALU.add), (("T", tmm),) if qb == 0 else (), inc=(qb == 3))
                else:
                    E("V", lambda e, dst=dst, qb=qb: e.tensor_copy(
                        dst, ps0[:, qb * 512:(qb + 1) * 512]),
                      (("T", tmm),) if qb == 0 else (), inc=(qb == 3))
            last_evict_v = em.now("V")
        qk_v = last_evict_v
        gv = E("G", lambda e: e.dma_start(
            PO.ap()[:, 0:8192].rearrange("p (kc f) -> p kc f", f=1024),
            wq[:, 2048:3072].rearrange("(kc p) f -> p kc f", p=128)),
            tuple(po_free), inc=True, tag="gwv")
        vbias_v = 0
        if L == 1:
            ybv = E("Y", lambda e: e.dma_start(SCR.ap()[0:1, 1024:2048], dr["bv1"]),
                    inc=True, tag="ybv")
            E("T", lambda e: e.matmul(ps1[:, 0:512], ONERF, SCR.ap()[0:1, 1024:1536],
                                      start=True, stop=True), (("ybv", ybv),))
            tb2 = E("T", lambda e: e.matmul(ps1[:, 512:1024], ONERF,
                                            SCR.ap()[0:1, 1536:2048],
                                            start=True, stop=True), inc=True)
            vbias_v = E("V", lambda e: e.tensor_copy(PDf[:, 3072:4096],
                                                     ps1[:, 0:1024]),
                        (("T", tb2),), inc=True)
        ev_t = {}
        for tt in range(16):
            for fc in range(2):
                bank = (tt % 2) * 2 + fc
                w0 = [("gwv", gv)]
                if tt >= 2:
                    w0.append(("V", ev_t[(tt - 2, fc)]))
                if L == 1:
                    w0.append(("V", vbias_v))
                for kc in range(8):
                    E("T", lambda e, tt=tt, fc=fc, kc=kc, bank=bank: e.matmul(
                        ps1[:, bank * 512:(bank + 1) * 512],
                        PA.ap()[:, kc * 2048 + tt * 128: kc * 2048 + (tt + 1) * 128],
                        PO.ap()[:, kc * 1024 + fc * 512: kc * 1024 + (fc + 1) * 512],
                        start=(kc == 0), stop=(kc == 7)),
                      tuple(w0) if kc == 0 else (), inc=(kc == 7))
                tvm = em.now("T")
                base = tt * 1040 + fc * 8 * 65
                dst = PB.ap()[:, base: base + 520].rearrange("p (h e) -> p h e", e=65)[:, :, 0:64]
                src = ps1[:, bank * 512:(bank + 1) * 512].rearrange(
                    "p (h e) -> p h e", e=64)
                if L == 1:
                    bsl = PDf[:, 3072 + fc * 512: 3072 + (fc + 1) * 512] \
                        .rearrange("p (h e) -> p h e", e=64)
                    E("V", lambda e, dst=dst, src=src, bsl=bsl: e.tensor_tensor(
                        dst, src, bsl, ALU.add), (("T", tvm),), inc=True)
                else:
                    E("V", lambda e, dst=dst, src=src: e.tensor_copy(dst, src),
                      (("T", tvm),), inc=True)
                ev_t[(tt, fc)] = em.now("V")
            E("V", lambda e, tt=tt: e.memset(
                PB.ap()[:, tt * 1040: tt * 1040 + 1040].rearrange(
                    "p (h e) -> p h e", e=65)[:, :, 64:65], 1.0), inc=True)
        return {"qk_v": qk_v, "v_v": em.now("V"), "qkv_t": em.now("T")}

    ROWSB = ROWS.ap().bitcast(BF16)

    def ph_bounds(L, ev):
        qkv = ev["qk_v"]
        SA = PAf[0:1, 0:2048]
        SB = PAf[0:1, 2048:4096]
        SC = PAf[0:1, 4096:6144]
        KP = PAf[0:1, 6144:6152]
        ydrain = em.now("yms2")
        yms = []
        for hp in range(8):
            vsq = E("V", lambda e, hp=hp: e.tensor_tensor(
                SCRb[:, 0:2048], PXb[:, hp * 2048:(hp + 1) * 2048],
                PXb[:, hp * 2048:(hp + 1) * 2048], ALU.mult),
                ((("V", qkv),) + tuple(ev.get("bias_ready", ())) if hp == 0
                 else (("T", em.now("T")),)))
            E("V", lambda e, hp=hp: e.tensor_tensor(
                SCRb[:, 2048:4096], PXb[:, 16384 + hp * 2048: 16384 + (hp + 1) * 2048],
                PXb[:, 16384 + hp * 2048: 16384 + (hp + 1) * 2048], ALU.mult),
              inc=True)
            vsq = em.now("V")
            for half in range(2):
                h = hp * 2 + half
                p0 = half * 64
                for tb in range(4):
                    E("T", lambda e, p0=p0, tb=tb: e.matmul(
                        ps1[0:1, tb * 512:(tb + 1) * 512], ONESB[p0:p0 + 64, :],
                        SCRb[p0:p0 + 64, tb * 512:(tb + 1) * 512], start=True,
                        stop=True, tile_position=(p0, 0)),
                      (("V", vsq), ("V", em.now("V"))) if tb == 0 else ())
                    E("T", lambda e, p0=p0, tb=tb: e.matmul(
                        ps0[0:1, tb * 512:(tb + 1) * 512], ONESB[p0:p0 + 64, :],
                        SCRb[p0:p0 + 64, 2048 + tb * 512: 2048 + (tb + 1) * 512],
                        start=True, stop=True, tile_position=(p0, 0)),
                      inc=(tb == 3))
                tq = em.now("T")
                E("V", lambda e: e.tensor_copy(SA, ps1[0:1, 0:2048]),
                  (("T", tq), ("yms", yms[-1] if yms else 0)))
                vcp = E("V", lambda e: e.tensor_copy(SB, ps0[0:1, 0:2048]), inc=True)
                for qb in range(4):
                    E("V", lambda e, qb=qb: e.tensor_reduce(
                        KP[:, qb:qb + 1], SB[:, qb * 512:(qb + 1) * 512], axis=AX.X,
                        op=ALU.max), (("V", vcp),) if qb == 0 else (),
                      inc=(qb == 3))
                vkp = em.now("V")
                if L == 1:
                    for qb in range(1, 4):
                        vkp = E("V", lambda e, qb=qb: e.tensor_tensor(
                            KP[:, qb:qb + 1], KP[:, qb:qb + 1], KP[:, qb - 1:qb],
                            ALU.max), (("V", vkp),), inc=True)
                else:
                    for qb in range(3):
                        vkp = E("V", lambda e, qb=qb: e.tensor_tensor(
                            KP[:, 3:4], KP[:, 3:4], KP[:, qb:qb + 1], ALU.max),
                            (("V", vkp),), inc=True)
                    for qb in range(3):
                        E("V", lambda e, qb=qb: e.tensor_scalar(
                            KP[:, qb:qb + 1], KP[:, 3:4], 1.0, None, ALU.mult),
                          (("V", vkp),) if qb == 0 else (), inc=(qb == 2))
                    vkp = em.now("V")
                for qb in range(4):
                    E("V", lambda e, qb=qb: e.tensor_scalar(
                        SC[:, qb * 512:(qb + 1) * 512], SA[:, qb * 512:(qb + 1) * 512],
                        KP[:, qb:qb + 1], None, ALU.mult),
                      (("V", vkp),) if qb == 0 else (), inc=(qb == 3))
                vpr = em.now("V")
                sl = E("S", lambda e: e.activation(SC, SC, AF.Ln, bias=C30R),
                       (("V", vpr),), inc=True)
                sl = E("S", lambda e: e.activation(SC, SC, AF.Exp, scale=0.5),
                       (("S", sl),), inc=True)
                vm2 = E("V", lambda e: e.tensor_scalar(
                    PA.ap()[0:1, 12352:14400], SC, -1.0, -4.0, ALU.mult, ALU.add),
                    (("S", sl),), inc=True)
                yms.append(E("Y", lambda e, h=h: e.dma_start(
                    ROWSB[8 + h: 9 + h, 0:2048], PA.ap()[0:1, 12352:14400]),
                    (("V", vm2), ("yms2", ydrain)), inc=True, tag="yms"))
        return {"mrows_y": em.now("yms"), "mrows_v": em.now("V")}

    def ph_attncore(L, ev, bg=None):
        load_actset("natural_log_exp_and_others")
        prev_exp = em.now("S")
        prev_ev_v = em.now("V")
        prev_mT = 0
        first_all = True
        MSA = SCRb[0:1, 2048:2560]
        MSB = SCRb[0:1, 3072:3584]
        for hp in range(8):
            for qb in range(4):
                kc_hi = 16 if L == 0 else 4 * qb + 4
                yms = E("Y", lambda e, hp=hp, qb=qb: e.dma_start(
                    MSA, ROWSB[8 + 2 * hp: 9 + 2 * hp, qb * 512:(qb + 1) * 512]),
                    (("yms", ev["mrows_y"]), ("T", prev_mT)), inc=True, tag="yms2")
                yms = E("Y", lambda e, hp=hp, qb=qb: e.dma_start(
                    MSB, ROWSB[9 + 2 * hp: 10 + 2 * hp, qb * 512:(qb + 1) * 512]),
                    inc=True, tag="yms2")
                exps = {}

                def mk_steps(kc):
                    steps = []
                    for half in range(2):
                        p0 = half * 64
                        qoff = max(0, kc * 128 - qb * 512) if L == 1 else 0
                        col = (kc % 2) * 1024 + half * 512
                        lhs = PXb[p0:p0 + 64, 16384 + hp * 2048 + kc * 128:
                                  16384 + hp * 2048 + (kc + 1) * 128]
                        rhs = PXb[p0:p0 + 64, hp * 2048 + qb * 512 + qoff:
                                  hp * 2048 + (qb + 1) * 512]
                        badds = []
                        if L == 0 and qb <= 1 and kc <= 8:
                            badds.append((col, 512,
                                          PD.ap()[:, kc * 1028 + qb * 512:
                                                  kc * 1028 + (qb + 1) * 512]))
                        elif L == 0 and qb == 2 and kc <= 8:
                            badds.append((col, 1,
                                          PD.ap()[:, kc * 1028 + 1024:
                                                  kc * 1028 + 1025]))
                        elif L == 1:
                            h = hp * 2 + half
                            j0 = kc - 4 * qb
                            if 0 <= j0 <= 3:
                                badds.append((col + j0 * 128, 128,
                                              PD.ap()[:, h * 128:(h + 1) * 128]))
                            if 0 <= j0 + 1 <= 3:
                                badds.append((col + (j0 + 1) * 128, 128,
                                              PD.ap()[:, 4096 + h * 128:
                                                      4096 + (h + 1) * 128]))
                            if j0 == -1:
                                badds.append((col, 128,
                                              PD.ap()[:, 4096 + h * 128:
                                                      4096 + (h + 1) * 128]))
                        steps.append((lambda e, col=col, lhs=lhs, rhs=rhs,
                                      qoff=qoff, p0=p0: e.matmul(
                            ps0[:, col + qoff: col + 512], lhs, rhs,
                            start=True, stop=False, tile_position=(p0, 0)),
                            False))
                        for (c_, w_, tsl) in badds:
                            steps.append((lambda e, c_=c_, w_=w_, tsl=tsl: e.matmul(
                                ps0[:, c_: c_ + w_], IDENT, tsl,
                                start=False, stop=False), False))
                        ms = MSA if half == 0 else MSB
                        steps.append((lambda e, col=col, ms=ms, qoff=qoff:
                                      e.matmul(
                            ps0[:, col + qoff: col + 512], ONER,
                            ms[:, qoff:512], start=False, stop=True), True))
                    return steps

                def emit_scores(kc):
                    steps = mk_steps(kc)
                    for i, (fn, is_m) in enumerate(steps):
                        w = []
                        if i == 0:
                            w = [("S", exps[kc - 2] if kc >= 2 else prev_exp)]
                            if first_all:
                                w += [("V", ev["mrows_v"]), ("V", ev["v_v"])] + \
                                    list(ev.get("bias_ready", []))
                        if is_m:
                            w.append(("yms2", yms))
                        E("T", fn, tuple(w), inc=(i == len(steps) - 1))
                    return em.now("T")

                def emit_exp(kc):
                    tsc = em.now("T")
                    for half in range(2):
                        dst = PA.ap()[:, half * 8192 + kc * 512:
                                      half * 8192 + (kc + 1) * 512]
                        srcc = (kc % 2) * 1024 + half * 512
                        E("S", lambda e, dst=dst, srcc=srcc: e.activation(
                            dst, ps0[:, srcc: srcc + 512], AF.Exp, scale=0.125),
                          (("T", tsc),) if half == 0 else (), inc=(half == 1))
                    exps[kc] = em.now("S")

                def emit_av(kc):
                    for half in range(2):
                        h = hp * 2 + half
                        qoff = max(0, kc * 128 - qb * 512) if L == 1 else 0
                        vsl = PB.ap()[:, kc * 1040 + h * 65: kc * 1040 + h * 65 + 65]
                        psl = PA.ap()[:, half * 8192 + kc * 512 + qoff:
                                      half * 8192 + (kc + 1) * 512]
                        w = ()
                        if half == 0:
                            w = (("S", exps[kc]),)
                            if kc == 0:
                                w += (("V", prev_ev_v),)
                        E("T", lambda e, vsl=vsl, psl=psl, half=half, qoff=qoff,
                          st=(kc == 0), sp=(kc == kc_hi - 1): e.matmul(
                            ps1[0:65, half * 512 + qoff:(half + 1) * 512],
                            vsl, psl, start=st, stop=sp), w, inc=(half == 1))

                emit_scores(0)
                first_all = False
                emit_exp(0)
                for kc in range(1, kc_hi):
                    emit_scores(kc)
                    emit_av(kc - 1)
                    emit_exp(kc)
                emit_av(kc_hi - 1)
                prev_exp = exps[kc_hi - 1]
                tav = em.now("T")
                for half in range(2):
                    h = hp * 2 + half
                    vrec = E("V", lambda e, half=half: e.reciprocal(
                        SCR.ap()[0:1, 512:1024],
                        ps1[64:65, half * 512:(half + 1) * 512]), (("T", tav),),
                        inc=True)
                    trb = E("T", lambda e: e.matmul(
                        ps1[0:64, 1024 + 0:1024 + 512], ONERF[0:1, 0:64],
                        SCR.ap()[0:1, 512:1024], start=True, stop=True), (("V", vrec),),
                        inc=True)
                    vcb = E("V", lambda e: e.tensor_copy(SCR.ap()[0:64, 0:512],
                                                         ps1[0:64, 1024:1536]),
                            (("T", trb),), inc=True)
                    dst = PO.ap()[(h % 2) * 64:(h % 2) * 64 + 64,
                                  (h // 2) * 2048 + qb * 512:
                                  (h // 2) * 2048 + (qb + 1) * 512]
                    E("V", lambda e, dst=dst, half=half: e.tensor_tensor(
                        dst, ps1[0:64, half * 512:(half + 1) * 512],
                        SCR.ap()[0:64, 0:512], ALU.mult), (("V", vcb),), inc=True)
                prev_ev_v = em.now("V")
                prev_mT = em.now("T")
                if bg is not None:
                    next(bg)
        return {"attn_t": em.now("T"), "attn_v": em.now("V"), "attn_s": em.now("S")}

    def ph_proj(L, ev, xsrc, xdst):
        wp = dr["wprojT0"] if L == 0 else dr["wprojT1"]
        bcol = BPB0 if L == 0 else BP1
        gw = E("G", lambda e: e.dma_start(
            PB.ap()[:, 0:8192].rearrange("p (kc f) -> p kc f", f=1024),
            wp.rearrange("(kc p) f -> p kc f", p=128)),
            (("T", ev["attn_t"]),), inc=True, tag="gwp")
        tmp = PBf[:, 4096:8192]
        CPB = PA.ap()[:, 0:512]
        SQB = PA.ap()[:, 512:1024]
        vprev_tb = em.now("V")
        yout = 0
        for tb in range(4):
            if L == 0:
                PR1 = PAf[0:1, 512:1024]
                PR2 = PAf[0:1, 1024:1536]
                E("V", lambda e, tb=tb: e.tensor_copy(
                    PR1, R_RS0[0:1, tb * 512:(tb + 1) * 512]),
                  (("V", vprev_tb), ("V", ev["attn_v"])))
                vcp = E("V", lambda e, tb=tb: e.tensor_copy(
                    PR2, R_M0R0[0:1, tb * 512:(tb + 1) * 512]), inc=True)
                E("T", lambda e: e.matmul(
                    ps1[:, 1024:1536], ONERF, PR1,
                    start=True, stop=True), (("V", vcp),))
                tbc = E("T", lambda e: e.matmul(
                    ps1[:, 1536:2048], ONERF, PR2,
                    start=True, stop=True), inc=True)
                E("V", lambda e: e.tensor_copy(SCR.ap()[:, 512:1024], ps1[:, 1024:1536]),
                  (("T", tbc),))
                E("V", lambda e: e.tensor_copy(SCR.ap()[:, 1024:1536],
                                               ps1[:, 1536:2048]), inc=True)
            vbc = em.now("V")
            v_evict = {}
            for Mt in range(8):
                bank = Mt % 4
                yxr = "yxr%d" % (Mt % 2)
                yx = E("Y", lambda e, Mt=Mt, tb=tb: e.dma_start(
                    WS.ap()[:, (Mt % 2) * 512:(Mt % 2) * 512 + 512],
                    xsrc[Mt * 128:(Mt + 1) * 128, tb * 512:(tb + 1) * 512]),
                    ((("V", v_evict[Mt - 2]),) if Mt >= 2 else (("V", vprev_tb),)) +
                    (tuple(ev.get("xland", ())) if tb == 0 and Mt == 0 else ()),
                    inc=True, tag=yxr)
                w0 = [("gwp", gw), ("V", ev["attn_v"])]
                if Mt >= 4:
                    w0.append(("V", v_evict[Mt - 4]))
                for kc in range(8):
                    E("T", lambda e, tb=tb, Mt=Mt, kc=kc, bank=bank: e.matmul(
                        ps0[:, bank * 512:(bank + 1) * 512],
                        PB.ap()[:, kc * 1024 + Mt * 128: kc * 1024 + (Mt + 1) * 128],
                        PO.ap()[:, kc * 2048 + tb * 512: kc * 2048 + (tb + 1) * 512],
                        start=(kc == 0), stop=(kc == 7)),
                      tuple(w0) if kc == 0 else (), inc=(kc == 7))
                tmm = em.now("T")
                xs = WS.ap()[:, (Mt % 2) * 512:(Mt % 2) * 512 + 512]
                tmps = tmp[:, Mt * 512:(Mt + 1) * 512]
                wprev = (("V", v_evict[Mt - 1]),) if Mt >= 1 else ()
                if L == 0:
                    E("V", lambda e, xs=xs, Mt=Mt: e.scalar_tensor_tensor(
                        SCR.ap()[:, 1536:2048], xs, GAM0[:, Mt:Mt + 1],
                        SCR.ap()[:, 512:1024], ALU.mult, ALU.mult),
                      (("T", tmm), (yxr, yx), ("V", vbc)) + wprev, inc=True)
                    E("V", lambda e, tmps=tmps, xs=xs, bank=bank, Mt=Mt:
                      e.scalar_tensor_tensor(
                        tmps, ps0[:, bank * 512:(bank + 1) * 512], bcol[:, Mt:Mt + 1],
                        xs, ALU.add, ALU.add), inc=True)
                    E("V", lambda e, tmps=tmps: e.tensor_tensor(
                        tmps, tmps, SCR.ap()[:, 1536:2048], ALU.add),
                      (("V", em.now("V")),), inc=True)
                    E("V", lambda e, tmps=tmps, Mt=Mt: e.scalar_tensor_tensor(
                        tmps, SCR.ap()[:, 1024:1536], GAM0[:, Mt:Mt + 1], tmps,
                        ALU.mult, ALU.add), (("V", em.now("V")),), inc=True)
                else:
                    E("V", lambda e, tmps=tmps, xs=xs, bank=bank, Mt=Mt:
                      e.scalar_tensor_tensor(
                        tmps, ps0[:, bank * 512:(bank + 1) * 512], bcol[:, Mt:Mt + 1],
                        xs, ALU.add, ALU.add),
                      (("T", tmm), (yxr, yx)) + wprev, inc=True)
                v_evict[Mt] = em.now("V")
            for Mt in range(8):
                tmps = tmp[:, Mt * 512:(Mt + 1) * 512]
                E("V", lambda e, tmps=tmps: e.tensor_copy(CPB, tmps),
                  (("V", v_evict[7]), ("T", em.now("T"))) if Mt == 0
                  else (("T", em.now("T")),))
                E("V", lambda e, tmps=tmps: e.tensor_tensor(SQB, tmps, tmps, ALU.mult),
                  inc=True)
                vst = em.now("V")
                E("T", lambda e, Mt=Mt: e.matmul(
                    ps1[0:1, 0:512], ONESB, CPB, start=(Mt == 0), stop=(Mt == 7)),
                  (("V", vst),))
                E("T", lambda e, Mt=Mt: e.matmul(
                    ps1[0:1, 512:1024], ONESB, SQB, start=(Mt == 0), stop=(Mt == 7)),
                  inc=True)
            tst = em.now("T")
            rmu = PAf[0:1, 1536:2048]
            rsd = PAf[0:1, 2048:2560]
            vmu1 = E("V", lambda e: e.tensor_scalar(rmu, ps1[0:1, 0:512],
                                                    1.0 / 1024, None,
                                                    ALU.mult), (("T", tst),),
                     inc=True)
            E("V", lambda e: e.tensor_tensor(rsd, rmu, rmu, ALU.mult),
              (("V", vmu1),), inc=True)
            E("V", lambda e: e.scalar_tensor_tensor(rsd, ps1[0:1, 512:1024],
                                                    1.0 / 1024, rsd, ALU.mult,
                                                    ALU.subtract),
              (("V", em.now("V")),), inc=True)
            vvar = em.now("V")
            sl = E("S", lambda e: e.activation(rsd, rsd, AF.Ln, bias=EPSR),
                   (("V", vvar),), inc=True)
            sl = E("S", lambda e: e.activation(rsd, rsd, AF.Exp, scale=-0.5),
                   (("S", sl),), inc=True)
            E("T", lambda e: e.matmul(ps1[:, 1024:1536], ONERF, rmu, start=True,
                                      stop=True), (("S", sl),))
            tbb = E("T", lambda e: e.matmul(ps1[:, 1536:2048], ONERF, rsd, start=True,
                                            stop=True), inc=True)
            E("V", lambda e: e.tensor_copy(SCR.ap()[:, 0:512], ps1[:, 1024:1536]),
              (("T", tbb),))
            E("V", lambda e: e.tensor_copy(SCR.ap()[:, 512:1024], ps1[:, 1536:2048]),
              inc=True)
            vbb = em.now("V")
            for Mt in range(8):
                tmps = tmp[:, Mt * 512:(Mt + 1) * 512]
                E("V", lambda e, tmps=tmps: e.tensor_tensor(
                    tmps, tmps, SCR.ap()[:, 0:512], ALU.subtract),
                  (("V", vbb),) if Mt == 0 else (), inc=True)
                dst = PX.ap()[:, Mt * 2048 + tb * 512: Mt * 2048 + (tb + 1) * 512]
                E("V", lambda e, tmps=tmps, dst=dst: e.tensor_tensor(
                    dst, tmps, SCR.ap()[:, 512:1024], ALU.mult),
                  (("V", em.now("V")),), inc=True)
            vln = em.now("V")
            yout = E("Y", lambda e, tb=tb: e.dma_start(
                xdst[:, tb * 512:(tb + 1) * 512]
                .rearrange("(mt p) t -> p mt t", p=128),
                PX.ap().rearrange("p (mt t) -> p mt t", t=2048)[:, :, tb * 512:(tb + 1) * 512]),
                (("V", vln),), inc=True, tag="yo0")
            vprev_tb = vln
        return {"proj_v": em.now("V"), "proj_y": yout, "proj_t": em.now("T"),
                "proj_s": em.now("S"), "xland": (("yo0", yout),)}

    def ph_wmean_gen(L, ev):
        srcs = [dr["w1T0" if L == 0 else "w1T1"], dr["w2T0" if L == 0 else "w2T1"]]
        ACC = SMF.ap()[:, 523:587]
        vch = {}
        n = 0
        for wi in range(2):
            w = srcs[wi]
            flat = w.rearrange("a b -> (a b)").rearrange("(c p n) -> c p n", p=128,
                                                         n=1024)
            for c in range(32):
                buf = WS.ap()[:, (c % 2) * 1024:(c % 2) * 1024 + 1024]
                gwm = "gwm%d" % (c % 2)
                gd = E("G", lambda e, c=c, buf=buf, flat=flat: e.dma_start(
                    buf, flat[c]),
                    (("V", vch[(wi, c - 2)]),) if c >= 2 else
                    ((("V", ev["qk_v"]),) if wi == 0 else
                     (("V", vch[(0, 30 + c)]),)), inc=True, tag=gwm)
                E("V", lambda e, wi=wi, c=c, buf=buf: e.tensor_reduce(
                    ACC[:, wi * 32 + c: wi * 32 + c + 1], buf, axis=AX.X, op=ALU.add,
                    apply_absolute_value=True), ((gwm, gd),), inc=True)
                vch[(wi, c)] = em.now("V")
                n += 1
                if n % 2 == 0:
                    yield
        while True:
            yield

    def ph_wmean_fin(L):
        ACC = SMF.ap()[:, 523:587]
        out = {}
        for wi in range(2):
            E("V", lambda e, wi=wi: e.tensor_reduce(
                SMF.ap()[:, 587 + wi: 588 + wi], ACC[:, wi * 32: wi * 32 + 32],
                axis=AX.X, op=ALU.add), (("V", em.now("V")),), inc=True)
            vsum = em.now("V")
            tcp = E("T", lambda e, wi=wi: e.matmul(
                ps1[0:1, 1792 + wi * 8: 1793 + wi * 8], ONESFC,
                SMF.ap()[:, 587 + wi: 588 + wi], start=True, stop=True),
                (("V", vsum),), inc=True)
            MCF = SMF.ap()[0:1, 592 + wi: 593 + wi]
            MCQ = SMF.ap()[0:1, 594 + wi: 595 + wi]
            WSS = SMF.ap()[0:1, 596 + wi: 597 + wi]
            vmc = E("V", lambda e, wi=wi, MCF=MCF: e.tensor_scalar(
                MCF, ps1[0:1, 1792 + wi * 8: 1793 + wi * 8], 1.0 / (4096 * 1024),
                1e-5, ALU.mult, ALU.max), (("T", tcp),), inc=True)
            E("V", lambda e, MCF=MCF, MCQ=MCQ: e.tensor_scalar(
                MCQ, MCF, 1.0 / 127, None, ALU.mult), (("V", vmc),))
            E("V", lambda e, MCF=MCF, WSS=WSS: e.reciprocal(WSS, MCF), inc=True)
            vws = em.now("V")
            trp = E("T", lambda e, WSS=WSS: e.matmul(
                ps1[:, 1536:1537], ONERF, WSS, start=True, stop=True),
                (("V", vws),), inc=True)
            E("V", lambda e, wi=wi: e.tensor_copy(WSR[wi], ps1[:, 1536:1537]),
              (("T", trp),), inc=True)
            out[f"mcq{wi}"] = MCQ
        out["wmean_v"] = em.now("V")
        return out

    def ph_quantx(L, ev):
        pv = ev["proj_v"]
        Mp = POf[:, 0:2048]
        Mn = POf[:, 2048:4096]
        E("V", lambda e: e.tensor_copy(Mp, PX.ap()[:, 0:2048]), (("V", pv),))
        E("V", lambda e: e.tensor_copy(Mn, PX.ap()[:, 0:2048]), inc=True)
        for kc in range(1, 8):
            E("V", lambda e, kc=kc: e.tensor_tensor(
                Mp, Mp, PX.ap()[:, kc * 2048:(kc + 1) * 2048], ALU.max),
              (("V", em.now("V")),))
            E("V", lambda e, kc=kc: e.tensor_tensor(
                Mn, Mn, PX.ap()[:, kc * 2048:(kc + 1) * 2048], ALU.min), inc=True)
        E("V", lambda e: e.tensor_scalar(Mn, Mn, -1.0, None, ALU.mult),
          (("V", em.now("V")),), inc=True)
        E("V", lambda e: e.tensor_tensor(Mp, Mp, Mn, ALU.max),
          (("V", em.now("V")),), inc=True)
        vam = em.now("V")
        vred = 0
        for tt in range(16):
            ttr = E("T", lambda e, tt=tt: e.transpose(
                ps1[:, (tt % 2) * 512:(tt % 2) * 512 + 128],
                Mp[:, tt * 128:(tt + 1) * 128], IDF),
                (("V", vam),) if tt < 2 else (("V", vred),), inc=True)
            E("V", lambda e, tt=tt: e.tensor_reduce(
                AMC[:, tt:tt + 1], ps1[:, (tt % 2) * 512:(tt % 2) * 512 + 128],
                axis=AX.X, op=ALU.max), (("T", ttr),), inc=True)
            vred = em.now("V")
        E("V", lambda e: e.tensor_scalar(AMC, AMC, 1e-5, None, ALU.max),
          (("V", em.now("V")),), inc=True)
        vclip = em.now("V")
        y1 = E("Y", lambda e: e.dma_start(
            row_d[3:4, 0:2048].rearrange("o (c p) -> (o p) c", p=128), AMC[:, 0:16],
            allow_slow_non_contiguous=True), (("V", vclip),), inc=True, tag="yrow")
        AROW = POf[0:1, 4096:6144]
        y2 = E("Y", lambda e: e.dma_start(AROW, row_d[3:4, 0:2048]),
               (("yrow", y1),), inc=True, tag="yrow")
        SXR = POf[0:1, 6144:8192]
        vrc = E("V", lambda e: e.reciprocal(SXR, AROW), (("yrow", y2),), inc=True)
        E("V", lambda e: e.tensor_scalar(SXR, SXR, 127.0, None, ALU.mult),
          (("V", vrc),), inc=True)
        E("V", lambda e: e.tensor_scalar(R0[:, 0:2048], AROW, ev["mcq0"], None,
                                         ALU.mult), inc=True)
        vrows = em.now("V")
        for tb in range(4):
            tb1 = E("T", lambda e, tb=tb: e.matmul(
                ps1[:, 1024:1536], ONERF, R0[0:1, tb * 512:(tb + 1) * 512],
                start=True, stop=True), (("V", vrows), ("V", em.now("V"))),
                inc=True)
            E("V", lambda e, tb=tb: e.tensor_copy(
                PDf[:, tb * 512:(tb + 1) * 512], ps1[:, 1024:1536]),
              (("T", tb1),), inc=True)
        vxq = 0
        for tb in range(4):
            tsx = E("T", lambda e, tb=tb: e.matmul(
                ps1[:, 1536:2048], ONERF, SXR[:, tb * 512:(tb + 1) * 512],
                start=True, stop=True), (("V", em.now("V")),), inc=True)
            for kc in range(8):
                E("V", lambda e, tb=tb, kc=kc: e.tensor_tensor(
                    SCR.ap()[:, 0:512], PX.ap()[:, kc * 2048 + tb * 512:
                                                kc * 2048 + (tb + 1) * 512],
                    ps1[:, 1536:2048], ALU.mult),
                  (("T", tsx), ("V", em.now("V"))), inc=True)
                E("V", lambda e, tb=tb, kc=kc: e.tensor_scalar(
                    PA.ap()[:, kc * 2048 + tb * 512: kc * 2048 + (tb + 1) * 512],
                    SCR.ap()[:, 0:512], MAGIC, MAGIC, ALU.add, ALU.subtract),
                  (("V", em.now("V")),), inc=True)
        return {"xq_v": em.now("V"), "r1b_v": em.now("V")}

    def ph_ffnA(L, ev):
        cwL = CW[L]
        cbL = CB[L]
        w1 = dr["w1T0" if L == 0 else "w1T1"]
        Mp = PB.ap()[:, 0:2048]
        Mn = PB.ap()[:, 2048:4096]
        load_actset("silu_and_others")
        E("V", lambda e: e.memset(Mp, -3.0e38), (("T", ev.get("attn_t", 0)),))
        E("V", lambda e: e.memset(Mn, 3.0e38))
        E("V", lambda e: e.memset(PO.ap()[:, 0:1], 0.0))
        E("V", lambda e: e.memset(PO.ap()[:, 2049:2050], 0.0), inc=True)
        vinit = em.now("V")
        vq = {}
        vus = {}
        vamax = {}
        ysp = {}
        sse = {}
        tm1s = {}
        for ct in range(32):
            buf = WS.ap()[:, (ct % 2) * 1024:(ct % 2) * 1024 + 1024]
            qb_ = W1Q[:, (ct % 2) * 1024:(ct % 2) * 1024 + 1024]
            yw1 = "yw1%d" % (ct % 2)
            gw = E("Y", lambda e, ct=ct, buf=buf: e.dma_start(
                buf.rearrange("p (kc j) -> p kc j", j=128),
                w1[:, ct * 128:(ct + 1) * 128].rearrange("(kc p) j -> p kc j", p=128)),
                (("V", vq[ct - 2]),) if ct >= 2 else (("V", ev["wmean_v"]),),
                inc=True, tag=yw1)
            E("V", lambda e, buf=buf: e.tensor_scalar(
                SCR.ap()[:, 1024:2048], buf, WSR[0], MAGIC, ALU.mult, ALU.add),
              ((yw1, gw), ("T", tm1s.get(ct - 2, 0)), ("V", em.now("V"))), inc=True)
            E("V", lambda e: e.tensor_scalar(
                SCR.ap()[:, 1024:2048], SCR.ap()[:, 1024:2048], MAGIC, -1.0,
                ALU.subtract, ALU.max), (("V", em.now("V")),), inc=True)
            E("V", lambda e, qb_=qb_: e.tensor_scalar(
                qb_, SCR.ap()[:, 1024:2048], 1.0, None, ALU.min),
              (("V", em.now("V")),), inc=True)
            vq[ct] = em.now("V")
            for kc in range(8):
                for tb in range(4):
                    w = []
                    if kc == 0 and tb == 0:
                        w = [("V", vq[ct]), ("V", ev["xq_v"])]
                        if ct >= 1:
                            w.append(("V", vus[ct - 1]))
                    E("T", lambda e, kc=kc, tb=tb, qb_=qb_: e.matmul(
                        ps0[:, tb * 512:(tb + 1) * 512], qb_[:, kc * 128:(kc + 1) * 128],
                        PA.ap()[:, kc * 2048 + tb * 512: kc * 2048 + (tb + 1) * 512],
                        start=(kc == 0), stop=(kc == 7)), tuple(w),
                      inc=(kc == 7 and tb == 3))
            tm1 = em.now("T")
            tm1s[ct] = tm1
            E("V", lambda e: e.tensor_tensor(
                PO.ap()[:, 1:2049], ps0[:, 0:2048], PDf[:, 0:2048], ALU.mult),
              (("T", tm1), ("V", ev["r1b_v"])), inc=True)
            vus[ct] = em.now("V")
            for t in range(3):
                E("V", lambda e, t=t, ct=ct: e.tensor_scalar(
                    DIAG[t], IDENT, cwL[:, t * 32 + ct: t * 32 + ct + 1], None,
                    ALU.mult), (("T", em.now("T")),) if t == 0 else (),
                  inc=(t == 2))
            vdg = em.now("V")
            for tb in range(4):
                for t in range(3):
                    E("T", lambda e, tb=tb, t=t: e.matmul(
                        ps1[:, (tb % 2) * 512:(tb % 2) * 512 + 512], DIAG[t],
                        PO.ap()[:, tb * 512 + t: tb * 512 + t + 512],
                        start=(t == 0), stop=(t == 2)),
                      (("V", vdg), ("S", sse.get(ct - 1, 0))) if tb == 0 and t == 0
                      else ((("S", sse.get((ct, tb - 2), 0)),) if t == 0 and tb >= 2
                            else ()), inc=(t == 2))
                tcv = em.now("T")
                stg = PO.ap()[:, 4096 + (ct % 2) * 2048: 4096 + (ct % 2) * 2048 + 2048]
                E("S", lambda e, tb=tb, stg=stg, ct=ct: e.activation(
                    stg[:, tb * 512:(tb + 1) * 512],
                    ps1[:, (tb % 2) * 512:(tb % 2) * 512 + 512], AF.Silu,
                    bias=cbL[:, ct:ct + 1]),
                  (("T", tcv),) + ((("yhs%d" % (ct % 2), ysp[ct - 2]),)
                                   if tb == 0 and ct >= 2 else ()), inc=True)
                sse[(ct, tb)] = em.now("S")
            sse[ct] = em.now("S")
            stg = PO.ap()[:, 4096 + (ct % 2) * 2048: 4096 + (ct % 2) * 2048 + 2048]
            E("V", lambda e, stg=stg: e.tensor_tensor(Mp, Mp, stg, ALU.max),
              (("S", sse[ct]), ("V", vinit)) +
              ((("V", vamax[ct - 1]),) if ct >= 1 else ()))
            E("V", lambda e, stg=stg: e.tensor_tensor(Mn, Mn, stg, ALU.min), inc=True)
            vamax[ct] = em.now("V")
            ysp[ct] = E("Y", lambda e, ct=ct, stg=stg: e.dma_start(hsil_d[ct], stg),
                        (("S", sse[ct]),), inc=True, tag="yhs%d" % (ct % 2))
        return {"ffnA_v": em.now("V"), "ffnA_t": em.now("T"),
                "ffnA_y": (("yhs0", em.now("yhs0")), ("yhs1", em.now("yhs1"))),
                "ffnA_s": em.now("S"), "amax_v": vamax[31]}

    def ph_ffnmid(L, ev):
        Mp = PB.ap()[:, 0:2048]
        Mn = PB.ap()[:, 2048:4096]
        vneg = E("V", lambda e: e.tensor_scalar(Mn, Mn, -1.0, None, ALU.mult),
                 (("V", ev["ffnA_v"]),), inc=True)
        E("V", lambda e: e.tensor_tensor(Mp, Mp, Mn, ALU.max), (("V", vneg),),
          inc=True)
        vam = em.now("V")
        A2C = SMF.ap()[:, 603:619]
        vred = 0
        for tt in range(16):
            ttr = E("T", lambda e, tt=tt: e.transpose(
                ps1.ap().bitcast(BF16)[:, (tt % 2) * 1024:(tt % 2) * 1024 + 128]
                if False else
                ps1[:, (tt % 2) * 512:(tt % 2) * 512 + 64].bitcast(BF16),
                Mp[:, tt * 128:(tt + 1) * 128], IDENT),
                (("V", vam),) if tt < 2 else (("V", vred),), inc=True)
            E("V", lambda e, tt=tt: e.tensor_reduce(
                A2C[:, tt:tt + 1],
                ps1[:, (tt % 2) * 512:(tt % 2) * 512 + 64].bitcast(BF16),
                axis=AX.X, op=ALU.max), (("T", ttr),), inc=True)
            vred = em.now("V")
        E("V", lambda e: e.tensor_scalar(A2C, A2C, 1e-5, None, ALU.max),
          (("V", em.now("V")),), inc=True)
        vclip = em.now("V")
        y1 = E("Y", lambda e: e.dma_start(
            row_d[4:5, 0:2048].rearrange("o (c p) -> (o p) c", p=128), A2C,
            allow_slow_non_contiguous=True),
            (("V", vclip)) + tuple(ev["ffnA_y"])
            if False else (("V", vclip),) + tuple(ev["ffnA_y"]),
            inc=True, tag="yrow")
        AROW = POf[0:1, 4096:6144]
        y2 = E("Y", lambda e: e.dma_start(AROW, row_d[4:5, 0:2048]),
               (("yrow", y1),), inc=True, tag="yrow")
        S2R = POf[0:1, 6144:8192]
        vrc = E("V", lambda e: e.reciprocal(S2R, AROW),
                (("yrow", y2), ("V", em.now("V"))), inc=True)
        E("V", lambda e: e.tensor_scalar(S2R, S2R, 127.0, None, ALU.mult),
          (("V", vrc),))
        E("V", lambda e: e.tensor_scalar(R0[:, 0:2048], AROW, ev["mcq1"], None,
                                         ALU.mult), inc=True)
        vrows = em.now("V")
        for tb in range(4):
            t1 = E("T", lambda e, tb=tb: e.matmul(
                ps1[:, 1024:1536], ONERF, S2R[:, tb * 512:(tb + 1) * 512],
                start=True, stop=True),
                (("V", vrows), ("V", ev["ffnA_v"])), inc=True)
            E("V", lambda e, tb=tb: e.tensor_copy(
                PDf[:, 2048 + tb * 512: 2048 + (tb + 1) * 512], ps1[:, 1024:1536]),
              (("T", t1),), inc=True)
            t2 = E("T", lambda e, tb=tb: e.matmul(
                ps1[:, 1536:2048], ONERF, R0[0:1, tb * 512:(tb + 1) * 512],
                start=True, stop=True), (("V", em.now("V")),), inc=True)
            E("V", lambda e, tb=tb: e.tensor_copy(
                PDf[:, tb * 512:(tb + 1) * 512], ps1[:, 1536:2048]),
              (("T", t2),), inc=True)
        return {"mid_v": em.now("V"), "mid_y": em.now("yrow")}

    def ph_ffnB(L, ev, xsrc, xdst):
        load_actset("natural_log_exp_and_others")
        w2 = dr["w2T0" if L == 0 else "w2T1"]
        vqw = {}
        for cc in range(32):
            buf = WS.ap()[:, (cc % 2) * 1024:(cc % 2) * 1024 + 1024]
            yw2 = "yw2%d" % (cc % 2)
            gw = E("Y", lambda e, cc=cc, buf=buf: e.dma_start(
                buf, w2[cc * 128:(cc + 1) * 128, :]),
                (("V", vqw[cc - 2]),) if cc >= 2 else
                (("V", ev["ffnA_v"]), ("T", ev["ffnA_t"]), ("S", ev["ffnA_s"])),
                inc=True, tag=yw2)
            dst = (PO.ap()[:, cc * 1024:(cc + 1) * 1024] if cc < 16 else
                   PA.ap()[:, (cc - 16) * 1024:(cc - 15) * 1024])
            wA = [(yw2, gw), ("V", em.now("V"))]
            if cc == 0:
                wA.append(("V", ev["mid_v"]))
                wA.append(("yrow", ev["mid_y"]))
            if cc >= 16 and cc - 16 < 1:
                wA.append(("T", ev["ffnA_t"]))
            E("V", lambda e, buf=buf: e.tensor_scalar(
                SCR.ap()[:, 1024:2048], buf, WSR[1], MAGIC, ALU.mult, ALU.add),
              tuple(wA), inc=True)
            E("V", lambda e: e.tensor_scalar(
                SCR.ap()[:, 1024:2048], SCR.ap()[:, 1024:2048], MAGIC, -1.0,
                ALU.subtract, ALU.max), (("V", em.now("V")),), inc=True)
            E("V", lambda e, dst=dst: e.tensor_scalar(
                dst, SCR.ap()[:, 1024:2048], 1.0, None, ALU.min),
              (("V", em.now("V")),), inc=True)
            vqw[cc] = em.now("V")
        vw2 = em.now("V")
        tmp = PBf[:, 0:4096]
        outb = PBf[:, 4096:8192]
        CPB = SCRb[:, 3072:3584]
        SQB = SCRb[:, 3584:4096]
        yhs = {}
        vhq = {}
        tmm = {}
        youts = 0
        vev_prev = 0
        for tb in range(4):
            hqb = PXb[:, (tb % 2) * 16384:(tb % 2) * 16384 + 16384]
            for cc in range(32):
                q_ = ("yhq%d" % (tb % 2), "ghq%d" % (tb % 2))[cc % 2]
                yhs[(tb, cc)] = E(("Y", "G")[cc % 2],
                                  lambda e, tb=tb, cc=cc, hqb=hqb: e.dma_start(
                    hqb[:, cc * 512:(cc + 1) * 512],
                    hsil_d[cc][:, tb * 512:(tb + 1) * 512]),
                    ((tuple(ev["ffnA_y"]) + tuple(ev["xland"])) if tb == 0
                     else ()) +
                    ((("T", tmm.get(tb - 2, 0)),) if tb >= 2 and cc <= 1 else ()),
                    inc=True, tag=q_)
            hq_full = (("yhq%d" % (tb % 2), em.now("yhq%d" % (tb % 2))),
                       ("ghq%d" % (tb % 2), em.now("ghq%d" % (tb % 2))))
            for cc in range(32):
                wq_ = hq_full + (("V", ev["mid_v"]), ("V", em.now("V")))
                E("V", lambda e, cc=cc, hqb=hqb, tb=tb: e.tensor_tensor(
                    SCR.ap()[:, 0:512], hqb[:, cc * 512:(cc + 1) * 512],
                    PDf[:, 2048 + tb * 512: 2048 + (tb + 1) * 512], ALU.mult),
                  wq_, inc=True)
                E("V", lambda e, cc=cc, hqb=hqb: e.tensor_scalar(
                    hqb[:, cc * 512:(cc + 1) * 512], SCR.ap()[:, 0:512], MAGIC,
                    MAGIC, ALU.add, ALU.subtract), (("V", em.now("V")),), inc=True)
                vhq[(tb, cc)] = em.now("V")
            v_evict = {}
            for Mt in range(8):
                yxr = "yxr%d" % (Mt % 2)
                yx = E("Y", lambda e, Mt=Mt, tb=tb: e.dma_start(
                    WS.ap()[:, (Mt % 2) * 512:(Mt % 2) * 512 + 512],
                    xsrc[Mt * 128:(Mt + 1) * 128, tb * 512:(tb + 1) * 512]),
                    (("V", v_evict[Mt - 2]),) if Mt >= 2 else
                    ((("V", vw2),) + tuple(ev["xland"]) if tb == 0 and Mt == 0 else
                     ((("V", vev_prev),) if Mt == 0 else ())), inc=True, tag=yxr)
                for cc in range(32):
                    lhs = (PO.ap()[:, cc * 1024 + Mt * 128: cc * 1024 + (Mt + 1) * 128]
                           if cc < 16 else
                           PA.ap()[:, (cc - 16) * 1024 + Mt * 128:
                                   (cc - 16) * 1024 + (Mt + 1) * 128])
                    w = []
                    if Mt == 0:
                        w = [("V", vhq[(tb, cc)]), ("V", vqw[cc])]
                    elif cc == 0:
                        w = [("V", vhq[(tb, 31)]), ("V", vw2)]
                    if cc == 0 and Mt >= 4:
                        w.append(("V", v_evict[Mt - 4]))
                    E("T", lambda e, cc=cc, Mt=Mt, lhs=lhs, hqb=hqb: e.matmul(
                        ps0[:, (Mt % 4) * 512:(Mt % 4) * 512 + 512], lhs,
                        hqb[:, cc * 512:(cc + 1) * 512],
                        start=(cc == 0), stop=(cc == 31)), tuple(w), inc=(cc == 31))
                ttm = em.now("T")
                E("V", lambda e, Mt=Mt, tb=tb: e.scalar_tensor_tensor(
                    SCR.ap()[:, 512:1024], ps0[:, (Mt % 4) * 512:(Mt % 4) * 512 + 512],
                    1.0, PDf[:, tb * 512:(tb + 1) * 512], ALU.mult, ALU.mult),
                  (("T", ttm), ("V", em.now("V"))), inc=True)
                E("V", lambda e, Mt=Mt: e.tensor_tensor(
                    tmp[:, Mt * 512:(Mt + 1) * 512], SCR.ap()[:, 512:1024],
                    WS.ap()[:, (Mt % 2) * 512:(Mt % 2) * 512 + 512], ALU.add),
                  ((yxr, yx), ("V", em.now("V"))), inc=True)
                v_evict[Mt] = em.now("V")
            tmm[tb] = em.now("T")
            for Mt in range(8):
                tmps = tmp[:, Mt * 512:(Mt + 1) * 512]
                E("V", lambda e, tmps=tmps: e.tensor_copy(CPB, tmps),
                  (("T", em.now("T")),))
                E("V", lambda e, tmps=tmps: e.tensor_tensor(SQB, tmps, tmps,
                                                            ALU.mult), inc=True)
                vst = em.now("V")
                E("T", lambda e, Mt=Mt: e.matmul(
                    ps1[0:1, 0:512], ONESB, CPB, start=(Mt == 0), stop=(Mt == 7)),
                  (("V", vst), ("S", ev["ffnA_s"])))
                E("T", lambda e, Mt=Mt: e.matmul(
                    ps1[0:1, 512:1024], ONESB, SQB, start=(Mt == 0), stop=(Mt == 7)),
                  inc=True)
            tst = em.now("T")
            rmu = ROWS.ap()[0:1, 0:512]
            rsd = ROWS.ap()[0:1, 512:1024]
            vmu1 = E("V", lambda e, rmu=rmu: e.tensor_scalar(
                rmu, ps1[0:1, 0:512], 1.0 / 1024, None, ALU.mult), (("T", tst),),
                inc=True)
            E("V", lambda e, rmu=rmu, rsd=rsd: e.tensor_tensor(rsd, rmu, rmu,
                                                               ALU.mult),
              (("V", vmu1),), inc=True)
            E("V", lambda e, rsd=rsd: e.scalar_tensor_tensor(
                rsd, ps1[0:1, 512:1024], 1.0 / 1024, rsd, ALU.mult, ALU.subtract),
              (("V", em.now("V")),), inc=True)
            vvar = em.now("V")
            sl = E("S", lambda e, rsd=rsd: e.activation(rsd, rsd, AF.Ln, bias=EPSR),
                   (("V", vvar),), inc=True)
            sl = E("S", lambda e, rsd=rsd: e.activation(rsd, rsd, AF.Exp, scale=-0.5),
                   (("S", sl),), inc=True)
            t1 = E("T", lambda e, rmu=rmu: e.matmul(ps1[:, 1024:1536], ONERF, rmu,
                                                    start=True, stop=True),
                   (("S", sl),))
            t2 = E("T", lambda e, rsd=rsd: e.matmul(ps1[:, 1536:2048], ONERF, rsd,
                                                    start=True, stop=True), inc=True)
            tbb = em.now("T")
            E("V", lambda e: e.tensor_copy(SCR.ap()[:, 1024:1536], ps1[:, 1024:1536]),
              (("T", tbb),))
            E("V", lambda e: e.tensor_copy(PDf[:, 4096:4608], ps1[:, 1536:2048]),
              inc=True)
            vbb = em.now("V")
            for Mt in range(8):
                tmps = tmp[:, Mt * 512:(Mt + 1) * 512]
                E("V", lambda e, tmps=tmps: e.tensor_tensor(
                    tmps, tmps, SCR.ap()[:, 1024:1536], ALU.subtract),
                  (("V", vbb), ("yo1", youts)) if Mt == 0 else (), inc=True)
                E("V", lambda e, tmps=tmps, Mt=Mt: e.tensor_tensor(
                    outb[:, Mt * 512:(Mt + 1) * 512], tmps, PDf[:, 4096:4608],
                    ALU.mult), (("V", em.now("V")),), inc=True)
            vln = em.now("V")
            vev_prev = vln
            youts = E("Y", lambda e, tb=tb: e.dma_start(
                xdst[:, tb * 512:(tb + 1) * 512].rearrange("(mt p) t -> p mt t",
                                                           p=128),
                outb.rearrange("p (mt t) -> p mt t", t=512)),
                (("V", vln),), inc=True, tag="yo1")
        return {"ffnB_y": youts, "ffnB_v": em.now("V"), "ffnB_t": em.now("T"),
                "ffnB_s": em.now("S"), "xland": (("yo1", youts),)}

    evs = {}
    evs.update(ph_init())
    for L in range(2):
        ev_pre = dict(evs)
        if L == 0:
            evs.update(ph_dbuild(evs))
            xsrc = dr["xT"]
        else:
            evs.update(ph_tbuild(evs, pre=(("T", evs["ffnB_t"]),
                                           ("V", evs["ffnB_v"]))))
            xsrc = xh[0]
        pa_free = () if L == 0 else (("T", evs["ffnB_t"]),) + tuple(evs["xland"])
        if L == 0:
            dump("init_smf", SMF.ap()[:, 0:768], None)
            dump("init_smb", SMB.ap()[:, 0:1024], None)
            dump("dbuild_pd", PD.ap()[:, 0:2056], PD.ap()[:, 0:1025])
            dump("dbuild_stk", SMF.ap()[:, 474:490], SMF.ap()[0:1, 474:490])
            dump("dbuild_norc", SMF.ap()[:, 490:499])
            dump("dbuild_frep", SMF.ap()[:, 506:507])
        evs.update(ph_prep(L, xsrc, evs, pa_free=pa_free,
                           ps_free=tuple(evs.get("ps_free", ()))))
        if L == 0:
            dump("prep_rows", ROWS.ap()[:, 0:2048], ROWS.ap()[0:1, 0:2048])
            dump("prep_pxs", PX.ap()[0:32, 0:6144], PX.ap()[0:1, 0:6144])
            dump("prep_pa", PA.ap()[:, 0:2048])
        pbf = () if L == 0 else (("yo1", evs["ffnB_y"]),)
        evs.update(ph_qkv(L, evs, po_free=tuple(evs.get("po_free", ())) + pbf,
                          pxb_free=pbf, ps_free=tuple(evs.get("ps_free", ()))))
        if L == 0:
            dump("qkv_px", PX.ap()[:, 0:1024])
            dump("qkv_pxk", PX.ap()[:, 8192:9216])
            dump("qkv_pb", PB.ap()[:, 0:260])
        evs.update(ph_bounds(L, evs))
        if L == 0:
            dump("bounds_rows", ROWS.ap()[:, 0:2048], ROWS.ap()[8:24, 0:1024])
        wgen = ph_wmean_gen(L, evs) if stage > 2 * L + 1 else None
        evs.update(ph_attncore(L, evs, bg=wgen))
        if L == 0:
            dump("attn_po", PO.ap()[:, 0:2048])
            dump("attn_pa", PAf[:, 0:2048])
            if DBGHW:
                E("V", lambda e: e.tensor_copy(WS.ap()[:, 0:2048],
                                               ps0.ap()[:, 0:2048]),
                  tuple((k, v) for k, v in em.cnt.items() if v > 0), inc=True)
                dump("attn_ps0", WS.ap()[:, 0:2048])
                ps1v = (ps1.ap()[0:65, 0:1024] if DBG_SIM
                        else ps1.ap()[:, 0:2048])
                wsv = (WS.ap()[0:65, 0:1024] if DBG_SIM
                       else WS.ap()[:, 0:2048])
                E("V", lambda e, ps1v=ps1v, wsv=wsv: e.tensor_copy(wsv, ps1v),
                  (("ydbg", em.now("ydbg")),), inc=True)
                dump("attn_ps1", WS.ap()[:, 0:2048], WS.ap()[0:65, 0:1024])
            dump("attn_scr", SCR.ap()[0:64, 0:1024], SCR.ap()[0:64, 0:512])
        last = (stage == 2 * L + 1)
        pr = ph_proj(L, evs, xsrc, yT if last else xh[1])
        evs.update(pr)
        if L == 0:
            dump("proj_pa", PAf[0:32, 0:2048])
        if last:
            break
        evs.update(ph_wmean_fin(L))
        evs.update(ph_quantx(L, evs))
        evs.update(ph_ffnA(L, evs))
        evs.update(ph_ffnmid(L, evs))
        last = (stage == 2 * L + 2)
        fb = ph_ffnB(L, evs, xh[1], yT if last else xh[0])
        evs.update(fb)
        if last:
            break

    dma_tags = sorted({t for (eng, _, _, _, t) in em.steps if eng in ("Y", "G")})
    em.emit("Y", lambda e: e.nop(),
            tuple((t, em.now(t)) for t in dma_tags), False, tag="ynop")

    DBG.clear()
    DBG['em'] = em
    DBG.update(dict(PX=PX.ap(), PXb=PXb, PA=PA.ap(), PB=PB.ap(), PO=PO.ap(),
                    PD=PD.ap(), SMF=SMF.ap(), SMB=SMB.ap(), SCR=SCR.ap(),
                    ROWS=ROWS.ap(), WS=WS.ap(), SM2=SM2.ap(), ps0=ps0.ap(), ps1=ps1.ap()))
    all_tags = sorted({t for (_, _, _, _, t) in em.steps} |
                      {we for (_, _, ws, _, _) in em.steps for (we, _) in ws})
    sems = {t: es.enter_context(nc.semaphore(f"sem{t}")) for t in all_tags}
    eng_map = {"T": "tensor", "V": "vector", "S": "scalar", "Y": "sync", "G": "gpsimd"}
    with nc.Block() as block:
        def mk(engkey):
            def body(e):
                hw = dict.fromkeys(all_tags, 0)
                for eng, fn, waits, n, t in em.steps:
                    if eng != engkey:
                        continue
                    for (we, wv) in waits:
                        if isinstance(wv, str):
                            raise RuntimeError("unresolved wait " + wv)
                        if wv > hw[we]:
                            e.wait_ge(sems[we], wv)
                            hw[we] = wv
                    inst = fn(e)
                    if n:
                        inst.then_inc(sems[t], n)
            return body
        for k, name in eng_map.items():
            getattr(block, name)(mk(k))

    es.close()
    return nc


def kernel(**inp):
    f32 = np.float32
    x = np.asarray(inp["L0_attn_x"], f32)
    hh0, hh1 = _make_hh()
    sc_ = np.asarray(inp["L0_attn_spatial_centers"], f32)
    csT = np.tile(sc_[0].reshape(2, 1), (1, 1152)).astype(f32)
    csT[:, 1:1025] = sc_.T
    shared = {
        "wqkvT0": np.ascontiguousarray(np.asarray(inp["L0_attn_W_qkv"], f32).T),
        "wprojT0": np.ascontiguousarray(np.asarray(inp["L0_attn_W_proj"], f32).T),
        "bproj0": _col(inp["L0_attn_b_proj"], 8),
        "gamma0": _col(inp["L0_attn_ln_gamma"], 8),
        "beta0": _col(inp["L0_attn_ln_beta"], 8),
        "csT": csT,
        "w1T0": np.ascontiguousarray(np.asarray(inp["L0_ffn_w1"], f32).T),
        "cw0": np.concatenate([_col(np.asarray(inp["L0_ffn_conv_w"], f32)[:, 0, t], 32)
                               for t in range(3)], axis=1),
        "cb0": _col(inp["L0_ffn_conv_b"], 32),
        "w2T0": np.ascontiguousarray(np.asarray(inp["L0_ffn_w2"], f32).T),
        "wqkvT1": np.ascontiguousarray(np.asarray(inp["L1_attn_W_attn"], f32).T),
        "battn1": _col(np.asarray(inp["L1_attn_b_attn"], f32)[0:2048], 16),
        "bv1": np.ascontiguousarray(np.asarray(inp["L1_attn_b_attn"], f32)[2048:3072]
                                    .reshape(1, 1024)),
        "wprojT1": np.ascontiguousarray(np.asarray(inp["L1_attn_W_proj"], f32).T),
        "bproj1": _col(inp["L1_attn_b_proj"], 8),
        "relemb": np.asarray(inp["L1_attn_rel_emb"], f32),
        "w1T1": np.ascontiguousarray(np.asarray(inp["L1_ffn_w1"], f32).T),
        "cw1": np.concatenate([_col(np.asarray(inp["L1_ffn_conv_w"], f32)[:, 0, t], 32)
                               for t in range(3)], axis=1),
        "cb1": _col(inp["L1_ffn_conv_b"], 32),
        "w2T1": np.ascontiguousarray(np.asarray(inp["L1_ffn_w2"], f32).T),
        "ident": np.eye(128, dtype=ml_dtypes.bfloat16),
        "identf": np.eye(128, dtype=np.float32),
        "hh0": hh0, "hh1": hh1,
    }
    nc = build(bass.Bass("TRN2", target_bir_lowering=False,
                         detect_race_conditions=RACECHECK), stage=STAGE)
    in_maps = []
    for b in range(8):
        m = dict(shared)
        m["xT"] = np.ascontiguousarray(x[b].T)
        in_maps.append(m)
    res = run_bass_kernel_spmd(nc, in_maps, core_ids=list(range(8)))
    return np.stack([np.asarray(r["yT"], f32).T for r in res.results], axis=0)


# version 07
# speedup vs baseline: 1.1046x; 1.0119x over previous
import math
import numpy as np
import ml_dtypes
import concourse.bass as bass
import concourse.mybir as mybir
from concourse.bass_utils import run_bass_kernel_spmd
from contextlib import ExitStack

F32 = mybir.dt.float32
BF16 = mybir.dt.bfloat16
AF = mybir.ActivationFunctionType
ALU = mybir.AluOpType
AX = mybir.AxisListType

MAGIC = float(2 ** 23 + 2 ** 22)
NEGBIG = -8.0e30
LN_EPS = 1e-5

STAGE = 4
DBG = {}
DBGHW = False
DBG_SIM = False
DBG_NO_M = False
DBG_NO_D = False
RACECHECK = True


def _bucket(d):
    if d < 16:
        return d
    v = 16 + int(np.float32(np.log(np.float32(d) / 16 + 1e-10) / math.log(8.0) * 16))
    return min(v, 31)


def _make_hh():
    hh0 = np.zeros((33, 128 * 128), dtype=np.float32)
    hh1 = np.zeros((33, 128 * 128), dtype=np.float32)
    for k in range(128):
        for q in range(128):
            i = k * 128 + q
            d0 = q - k
            if d0 >= 0:
                hh0[_bucket(d0), i] = 1.0
            else:
                hh0[32, i] = 1.0
            hh1[_bucket(d0 + 128), i] = 1.0
    return hh0.astype(ml_dtypes.bfloat16), hh1.astype(ml_dtypes.bfloat16)


def _col(v, n):
    return np.ascontiguousarray(np.asarray(v, np.float32).reshape(n, 128).T)


ENGS = ("T", "V", "S", "Y", "G")


class EM:

    def __init__(self):
        self.steps = []
        self.cnt = {}

    def emit(self, eng, fn, waits=(), inc=False, tag=None):
        t = tag if tag is not None else eng
        if eng in ("Y", "G"):
            assert t not in ("Y", "G"), "DMA steps need an explicit tag"
        n = (16 if eng in ("Y", "G") else 1) if inc else 0
        self.cnt[t] = self.cnt.get(t, 0) + n
        self.steps.append((eng, fn, tuple(waits), n, t))
        return self.cnt[t]

    def now(self, tag):
        return self.cnt.get(tag, 0)


def build(nc, stage=4):
    D, T = 1024, 2048

    def din(name, shape, dt=F32):
        return nc.dram_tensor(name, list(shape), dt, kind="ExternalInput").ap()

    dr = {k: din(k, s, d) for k, s, d in [
        ("xT", (D, T), F32),
        ("wqkvT0", (D, 3 * D), F32), ("wprojT0", (D, D), F32),
        ("bproj0", (128, 8), F32), ("gamma0", (128, 8), F32), ("beta0", (128, 8), F32),
        ("csT", (2, 1152), F32),
        ("w1T0", (D, 4096), F32), ("cw0", (128, 96), F32), ("cb0", (128, 32), F32),
        ("w2T0", (4096, D), F32),
        ("wqkvT1", (D, 3 * D), F32), ("battn1", (128, 16), F32), ("bv1", (1, D), F32),
        ("wprojT1", (D, D), F32), ("bproj1", (128, 8), F32), ("relemb", (32, 16), F32),
        ("w1T1", (D, 4096), F32), ("cw1", (128, 96), F32), ("cb1", (128, 32), F32),
        ("w2T1", (4096, D), F32),
        ("ident", (128, 128), BF16), ("identf", (128, 128), F32),
        ("hh0", (33, 16384), BF16), ("hh1", (33, 16384), BF16),
    ]}

    yT = nc.dram_tensor("yT", [D, T], F32, kind="ExternalOutput").ap()
    xh = [nc.dram_tensor(f"xh{i}", [D, T], F32, kind="Internal").ap() for i in range(2)]
    hsil_d = nc.dram_tensor("hsil_d", [32, 128, T], BF16, kind="Internal").ap()
    w2q_d = nc.dram_tensor("w2q_d", [32, 128, D], BF16, kind="Internal").ap()
    tflat_d = nc.dram_tensor("tflat_d", [2, 8, 16, 2048], BF16, kind="Internal").ap()
    row_d = nc.dram_tensor("row_d", [8, T], F32, kind="Internal").ap()

    es = ExitStack()

    def sb(name, shape, dt):
        return es.enter_context(nc.sbuf_tensor(name, list(shape), dt))

    PX = sb("PX", (128, 16384), F32)
    PA = sb("PA", (128, 16384), BF16)
    PB = sb("PB", (128, 16640), BF16)
    PO = sb("PO", (128, 16384), BF16)
    PD = sb("PD", (128, 9252), BF16)
    WS = sb("WS", (128, 2048), F32)
    SCR = sb("SCR", (128, 2048), F32)
    SMB = sb("SMB", (128, 1024), BF16)
    SMF = sb("SMF", (128, 768), F32)
    SM2 = sb("SM2", (33, 64), F32)
    ROWS = sb("ROWS", (128, 2048), F32)

    ps0 = es.enter_context(nc.psum_tensor("ps0", [128, 2048], F32))
    ps1 = es.enter_context(nc.psum_tensor("ps1", [128, 2048], F32))

    PXb = PX.ap().bitcast(BF16)
    PBf = PB.ap().bitcast(F32)
    POf = PO.ap().bitcast(F32)
    PAf = PA.ap().bitcast(F32)
    PDf = PD.ap().bitcast(F32)
    WSb = WS.ap().bitcast(BF16)
    SCRb = SCR.ap().bitcast(BF16)
    FLT = SCRb[0:16, 0:2048]
    W1Q = PB.ap()[:, 4096:6144]

    IDENT = SMB.ap()[:, 0:128]
    ONESB = SMB.ap()[:, 128:129]
    ONER = SMB.ap()[0:1, 130:258]
    DIAG = [SMB.ap()[:, 258 + i * 128: 386 + i * 128] for i in range(3)]
    GAMB = SMB.ap()[:, 672:680]
    IDF = SMF.ap()[:, 640:768]

    ONERF = SMF.ap()[0:1, 0:128]
    BPB0 = SMF.ap()[:, 128:136]
    GAM0 = SMF.ap()[:, 136:144]
    NGAM0 = SMF.ap()[:, 144:152]
    BT1 = SMF.ap()[:, 152:168]
    BP1 = SMF.ap()[:, 176:184]
    CB = [SMF.ap()[:, 184 + i * 32: 216 + i * 32] for i in range(2)]
    CW = [SMF.ap()[:, 248 + i * 96: 344 + i * 96] for i in range(2)]
    NEGM = SMF.ap()[:, 440:456]
    AMC = SMF.ap()[:, 456:472]
    WSR = [SMF.ap()[:, 472 + i: 473 + i] for i in range(2)]
    STK = SMF.ap()[:, 474:490]
    NORC = SMF.ap()[:, 490:506]
    FREP = SMF.ap()[:, 506:507]
    BETA0 = SMF.ap()[:, 507:515]
    EPSR = SMF.ap()[0:1, 515:516]
    C30R = SMF.ap()[0:1, 516:517]
    ONESFC = SMF.ap()[:, 522:523]
    C1 = SMF.ap()[:, 520:521]
    C2 = SMF.ap()[:, 521:522]

    CST = POf[0:2, 4864:6016]
    CSN2 = POf[0:2, 1280:2432]
    CS2 = POf[0:2, 2560:3712]
    NORR = POf[0:1, 3712:4864]
    R0 = ROWS.ap()[0:1, :]
    R_RS0 = ROWS.ap()[32:33, :]
    R_M0R0 = ROWS.ap()[64:65, :]
    R96 = ROWS.ap()[96:97, :]

    em = EM()
    E = em.emit

    from concourse.hw_specs import get_activation_tables
    _set_ids = {name: i for i, name in enumerate(get_activation_tables(nc.m.arch))}

    def load_actset(name, waits=()):
        sid = _set_ids[name]
        return E("S", lambda e, sid=sid: e.add_instruction(
            mybir.InstLoadActFuncSet(
                name=nc.get_next_instruction_name(), act_func_set_id=sid,
                ins=[], outs=[])), waits, inc=False)

    def dump(name, ap, sim_ap="same"):
        if not DBGHW:
            return
        if DBG_SIM:
            if sim_ap is None:
                return
            if sim_ap != "same":
                ap = sim_ap
        d = nc.dram_tensor("dbg_" + name, [ap.shape[0], ap.shape[1]], ap.dtype,
                           kind="ExternalOutput").ap()
        w = tuple((k, v) for k, v in em.cnt.items() if v > 0)
        E("Y", lambda e, d=d, ap=ap: e.dma_start(d, ap), w, inc=True, tag="ydbg")

    def ph_init():
        for dst, src in [(IDENT, dr["ident"]), (IDF, dr["identf"]),
                         (BPB0, dr["bproj0"]), (GAM0, dr["gamma0"]),
                         (BETA0, dr["beta0"]), (BT1, dr["battn1"]),
                         (BP1, dr["bproj1"]), (CB[0], dr["cb0"]), (CB[1], dr["cb1"]),
                         (CW[0], dr["cw0"]), (CW[1], dr["cw1"]), (CST, dr["csT"]),
                         (SM2.ap()[0:32, 0:16], dr["relemb"])]:
            E("Y", lambda e, dst=dst, src=src: e.dma_start(dst, src), inc=True,
              tag="yinit")
        yw = em.now("yinit")
        E("V", lambda e: e.memset(ONESB, 1.0))
        E("V", lambda e: e.memset(ONER, 1.0))
        E("V", lambda e: e.memset(ONERF, 1.0))
        E("V", lambda e: e.memset(EPSR, LN_EPS))
        E("V", lambda e: e.memset(C30R, 1e-30))
        E("V", lambda e: e.memset(ONESFC, 1.0))
        E("V", lambda e: e.tensor_tensor(BPB0, BPB0, BETA0, ALU.add),
          (("yinit", yw),))
        E("V", lambda e: e.tensor_copy(GAMB, GAM0))
        E("V", lambda e: e.tensor_scalar(NGAM0, GAM0, -1.0, None, ALU.mult), inc=True)
        return {"init_v": em.now("V"), "init_y": yw}

    def ph_dbuild(ev):
        E("V", lambda e: e.tensor_scalar(CSN2, CST, -2.0, None, ALU.mult),
          (("yinit", ev["init_y"]),))
        vcs = E("V", lambda e: e.tensor_tensor(CS2, CST, CST, ALU.mult), inc=True)
        for j, w_ in ((0, 512), (512, 512), (1024, 128)):
            E("T", lambda e, j=j, w_=w_: e.matmul(
                ps1[0:1, j: j + w_], ONESFC[0:2, :], CS2[:, j:j + w_],
                start=True, stop=True), (("V", vcs),) if j == 0 else (),
              inc=(j == 1024))
        tno = em.now("T")
        E("V", lambda e: e.tensor_copy(NORR, ps1[0:1, 0:1152]), (("T", tno),),
          inc=True)
        vnorm = em.now("V")
        ynorr = E("Y", lambda e: e.dma_start(row_d[0:1, 0:1152], NORR),
                  (("V", vnorm),), inc=True, tag="yrow")
        ynorc = E("Y", lambda e: e.dma_start(
            NORC[:, 0:9],
            row_d[0:1, 0:1152].rearrange("o (c p) -> (o p) c", p=128),
            allow_slow_non_contiguous=True), (("yrow", ynorr),), inc=True,
            tag="yrow")
        E("V", lambda e: e.memset(STK[:, 0:9], -3.0e38), inc=True)
        vstk = em.now("V")
        E("S", lambda e: e.preload_act_table(AF.Sqrt))
        vprev = 0
        for kc in range(9):
            qn = [(0, 512), (512, 512), (1024, 1)]
            for qi, (q0, qw) in enumerate(qn):
                w = []
                if qi == 0:
                    w = [("V", vnorm)] if kc == 0 else [("V", vprev)]
                E("T", lambda e, kc=kc, q0=q0, qw=qw, qi=qi: e.matmul(
                    ps0[:, q0:q0 + qw], CST[:, kc * 128:(kc + 1) * 128],
                    CSN2[:, q0:q0 + qw], start=(True if qi == 0 else True), stop=False),
                  tuple(w))
                E("T", lambda e, q0=q0, qw=qw, qi=qi: e.matmul(
                    ps0[:, q0:q0 + qw], ONERF, NORR[:, q0:q0 + qw],
                    start=False, stop=True), inc=(qi == len(qn) - 1))
            tmm = em.now("T")
            E("V", lambda e, kc=kc: e.tensor_scalar(
                POf[:, 0:1025], ps0[:, 0:1025], NORC[:, kc:kc + 1], 1e-30,
                ALU.add, ALU.max),
              (("T", tmm), ("yrow", ynorc), ("S", em.now("S"))), inc=True)
            vd2 = em.now("V")
            E("V", lambda e, kc=kc: e.tensor_reduce(
                STK[:, kc:kc + 1], POf[:, 0:1025], axis=AX.X, op=ALU.max),
              (("V", vstk), ("V", vd2)), inc=True)
            vprev = em.now("V")
            E("S", lambda e, kc=kc: e.activation(
                PD.ap()[:, kc * 1028: kc * 1028 + 1025], POf[:, 0:1025], AF.Sqrt),
              (("V", vd2),), inc=True)
        sqd = em.now("S")
        vred = E("V", lambda e: e.tensor_reduce(
            STK[:, 15:16], STK[:, 0:9], axis=AX.X, op=ALU.max),
            (("V", em.now("V")),), inc=True)
        tmx = E("T", lambda e: e.transpose(ps1[0:1, 0:128], STK[:, 15:16], IDF),
                (("V", vred),), inc=True)
        E("V", lambda e: e.tensor_reduce(STK[0:1, 14:15], ps1[0:1, 0:128], axis=AX.X,
                                         op=ALU.max), (("T", tmx),), inc=True)
        smx = E("S", lambda e: e.activation(STK[0:1, 13:14], STK[0:1, 14:15], AF.Sqrt),
                (("V", em.now("V")),), inc=True)
        E("V", lambda e: e.tensor_scalar(STK[0:1, 12:13], STK[0:1, 13:14], 1e-6, None,
                                         ALU.max), (("S", smx),), inc=True)
        E("V", lambda e: e.reciprocal(STK[0:1, 11:12], STK[0:1, 12:13]),
          (("V", em.now("V")),), inc=True)
        E("V", lambda e: e.tensor_scalar(STK[0:1, 10:11], STK[0:1, 13:14], 1e-6, None,
                                         ALU.is_gt), inc=True)
        E("V", lambda e: e.tensor_tensor(STK[0:1, 9:10], STK[0:1, 11:12],
                                         STK[0:1, 10:11], ALU.mult),
          (("V", em.now("V")),), inc=True)
        vf = E("V", lambda e: e.tensor_scalar(STK[0:1, 9:10], STK[0:1, 9:10], -0.8,
                                              None, ALU.mult),
               (("V", em.now("V")),), inc=True)
        tfr = E("T", lambda e: e.matmul(ps1[:, 128:129], ONERF, STK[0:1, 9:10],
                                        start=True, stop=True), (("V", vf),), inc=True)
        vfr = E("V", lambda e: e.tensor_copy(FREP, ps1[:, 128:129]), (("T", tfr),),
                inc=True)
        for kc in range(9):
            E("V", lambda e, kc=kc: e.tensor_scalar(
                PD.ap()[:, kc * 1028: kc * 1028 + 1025],
                PD.ap()[:, kc * 1028: kc * 1028 + 1025], FREP, None, ALU.mult),
              (("S", sqd), ("V", vfr)), inc=(kc == 8))
        E("V", lambda e: e.tensor_copy(SCRb[0:1, 0:1025],
                                       PD.ap()[0:1, 8 * 1028: 8 * 1028 + 1025]),
          (("V", em.now("V")),), inc=True)
        E("V", lambda e: e.memset(PD.ap()[:, 8 * 1028: 8 * 1028 + 1025], 0.0),
          (("V", em.now("V")),), inc=True)
        E("V", lambda e: e.tensor_copy(PD.ap()[0:1, 8 * 1028: 8 * 1028 + 1025],
                                       SCRb[0:1, 0:1025]),
          (("V", em.now("V")),), inc=True)
        for kc in range(9):
            E("V", lambda e, kc=kc: e.memset(PD.ap()[:, kc * 1028: kc * 1028 + 1], 0.0),
              (("V", em.now("V")),) if kc == 0 else (), inc=(kc == 8))
        E("V", lambda e: e.memset(PD.ap()[0:1, 0:1025], 0.0),
          (("V", em.now("V")),), inc=True)
        return {"bias_ready": [("V", em.now("V"))],
                "ps_free": [("V", em.now("V")), ("S", em.now("S"))],
                "po_free": [("V", em.now("V")), ("S", em.now("S")),
                            ("T", em.now("T"))]}

    def ph_tbuild(ev, pre=()):
        relb = SM2.ap().bitcast(BF16)
        y31 = E("Y", lambda e: e.dma_start(POf[0:1, 3072:3088],
                                           SM2.ap()[31:32, 0:16]),
                tuple(pre) + (("yinit", ev["init_y"]), ("V", ev["init_v"])),
                inc=True, tag="y31")
        t31 = E("T", lambda e: e.matmul(ps1[0:32, 0:16], ONERF[0:1, 0:32],
                                        POf[0:1, 3072:3088], start=True,
                                        stop=True), (("y31", y31),), inc=True)
        E("V", lambda e: e.tensor_tensor(SM2.ap()[0:32, 16:32], SM2.ap()[0:32, 0:16],
                                         ps1[0:32, 0:16], ALU.subtract), (("T", t31),),
          inc=True)
        E("V", lambda e: e.memset(SMB.ap()[32:64, 680:696], 0.0))
        E("V", lambda e: e.tensor_scalar(SMB.ap()[0:32, 680:696],
                                         SM2.ap()[0:32, 16:32],
                                         8.0, None, ALU.mult),
          (("V", em.now("V")),), inc=True)
        E("V", lambda e: e.memset(SMB.ap()[32:33, 680:696], NEGBIG),
          (("V", em.now("V")),), inc=True)
        vrel = em.now("V")
        hhsrc = [dr["hh0"], dr["hh1"]]
        E("V", lambda e: e.memset(PO.ap()[32:64, 0:4096], 0.0), tuple(pre), inc=True)
        vz = em.now("V")
        prev_t = [0, 0]
        sprev = 0
        yfprev = 0
        for t in range(2):
            for g in range(8):
                hb = PO.ap()[0:33, (g % 2) * 2048:(g % 2) * 2048 + 2048]
                hb64 = PO.ap()[0:64, (g % 2) * 2048:(g % 2) * 2048 + 2048]
                ghh = "ghh%d" % (g % 2)
                gw = E("G", lambda e, t=t, g=g, hb=hb: e.dma_start(
                    hb, hhsrc[t][:, g * 2048:(g + 1) * 2048]),
                    tuple(pre) + (("T", prev_t[g % 2]), ("V", vz)), inc=True,
                    tag=ghh)
                for c in range(4):
                    E("T", lambda e, hb=hb, hb64=hb64, c=c: e.matmul(
                        ps0[0:16, c * 512:(c + 1) * 512], SMB.ap()[0:64, 680:696],
                        hb64[:, c * 512:(c + 1) * 512], start=True, stop=True),
                      ((ghh, gw), ("V", vrel), ("S", sprev)) if c == 0 else (),
                      inc=(c == 3))
                tm = em.now("T")
                prev_t[g % 2] = tm
                sf = E("S", lambda e: e.copy(FLT, ps0[0:16, 0:2048]),
                       (("T", tm), ("yflt", yfprev)), inc=True)
                sprev = sf
                yf = E("Y", lambda e, t=t, g=g: e.dma_start(tflat_d[t, g], FLT),
                       (("S", sf),), inc=True, tag="yflt")
                yfprev = yf
                E("Y", lambda e, t=t, g=g: e.dma_start(
                    PD.ap()[g * 16:(g + 1) * 16, t * 4096: t * 4096 + 2048]
                    .rearrange("kp (h j) -> kp h j", j=128),
                    tflat_d[t, g].rearrange("h (kp j) -> kp h j", j=128)),
                  (("yflt", yf),), inc=True, tag="ypd")
        return {"bias_ready": [("ypd", em.now("ypd"))],
                "ps_free": [("V", em.now("V")), ("S", em.now("S"))],
                "po_free": [("T", em.now("T")), ("S", em.now("S"))]}

    def ph_prep(L, xsrc, ev, pa_free=(), ps_free=()):
        for c in range(4):
            E("G", lambda e, c=c: e.dma_start(
                PA.ap()[:, c * 4096:(c + 1) * 4096].rearrange("p (kc t) -> p kc t", t=2048),
                xsrc.rearrange("(kc p) t -> p kc t", p=128)[:, 2 * c: 2 * c + 2, :]),
              tuple(pa_free), inc=True, tag="gxb")
        gxb = em.now("gxb")
        out = {"xb_g": gxb}
        if L == 0:
            for tb in range(4):
                for kc in range(8):
                    E("T", lambda e, tb=tb, kc=kc: e.matmul(
                        ps1[0:1, tb * 512:(tb + 1) * 512], ONESB,
                        PA.ap()[:, kc * 2048 + tb * 512: kc * 2048 + (tb + 1) * 512],
                        start=(kc == 0), stop=(kc == 7)),
                      (("gxb", gxb), ("V", ev["init_v"])) + tuple(ps_free)
                      if (kc == 0 and tb == 0) else (),
                      inc=(kc == 7))
            tsum = em.now("T")
            E("V", lambda e: e.tensor_scalar(R0, ps1[0:1, 0:2048], 1.0 / 1024, None,
                                             ALU.mult), (("T", tsum),), inc=True)
            vmu = em.now("V")
            sq = SCRb[:, 0:2048]
            for kc in range(8):
                E("V", lambda e, kc=kc: e.tensor_tensor(
                    sq, PA.ap()[:, kc * 2048:(kc + 1) * 2048],
                    PA.ap()[:, kc * 2048:(kc + 1) * 2048], ALU.mult),
                  (("T", em.now("T")),), inc=True)
                vsq = em.now("V")
                for tb in range(4):
                    E("T", lambda e, tb=tb, kc=kc: e.matmul(
                        ps1[0:1, tb * 512:(tb + 1) * 512], ONESB,
                        sq[:, tb * 512:(tb + 1) * 512],
                        start=(kc == 0), stop=(kc == 7)),
                      (("V", vsq), ("V", vmu)) if tb == 0 else (),
                      inc=(tb == 3))
            tsq = em.now("T")
            T0 = PX.ap()[0:1, 0:2048]
            T1 = PX.ap()[0:1, 2048:4096]
            T2 = PX.ap()[0:1, 4096:6144]
            vt0 = E("V", lambda e: e.tensor_tensor(T0, R0, R0, ALU.mult),
                    (("T", tsq),), inc=True)
            E("V", lambda e: e.scalar_tensor_tensor(T1, ps1[0:1, 0:2048], 1.0 / 1024,
                                                    T0, ALU.mult, ALU.subtract),
              (("V", vt0),), inc=True)
            vvar = em.now("V")
            load_actset("natural_log_exp_and_others")
            s1 = E("S", lambda e: e.activation(T2, T1, AF.Ln, bias=EPSR),
                   (("V", vvar),), inc=True)
            s2 = E("S", lambda e: e.activation(T1, T2, AF.Exp, scale=-0.5),
                   (("S", s1),), inc=True)
            E("V", lambda e: e.tensor_copy(R_RS0, T1), (("S", s2),))
            vt0 = E("V", lambda e: e.tensor_tensor(T0, R0, T1, ALU.mult), inc=True)
            E("V", lambda e: e.tensor_scalar(R_M0R0, T0, -1.0, None, ALU.mult),
              (("V", vt0),), inc=True)
            out["rows_v"] = em.now("V")
        return out

    def ph_qkv(L, ev, pxb_free=(), po_free=(), ps_free=()):
        wq = dr["wqkvT0"] if L == 0 else dr["wqkvT1"]
        gxb = ev["xb_g"]
        t_after_fmt = {}
        last_evict_v = em.now("V")
        for fmt in range(16):
            buf = WSb[:, (fmt % 2) * 1024:(fmt % 2) * 1024 + 1024]
            gwq = "gwq%d" % (fmt % 2)
            w = [("T", t_after_fmt[fmt - 2])] if fmt >= 2 else []
            gw = E("G", lambda e, fmt=fmt, buf=buf: e.dma_start(
                buf.rearrange("p (kc f) -> p kc f", f=128),
                wq[:, fmt * 128:(fmt + 1) * 128]
                .rearrange("(kc p) f -> p kc f", p=128)), tuple(w), inc=True,
                tag=gwq)
            for kc in range(8):
                for qb in range(4):
                    w = ()
                    if kc == 0 and qb == 0:
                        w = ((gwq, gw), ("gxb", gxb), ("V", last_evict_v)) + \
                            tuple((tuple(pxb_free) + tuple(ps_free)) if fmt == 0
                                  else ())
                    E("T", lambda e, kc=kc, qb=qb, buf=buf: e.matmul(
                        ps0[:, qb * 512:(qb + 1) * 512], buf[:, kc * 128:(kc + 1) * 128],
                        PA.ap()[:, kc * 2048 + qb * 512: kc * 2048 + (qb + 1) * 512],
                        start=(kc == 0), stop=(kc == 7)), w,
                      inc=(kc == 7 and qb == 3))
            tmm = em.now("T")
            t_after_fmt[fmt] = tmm
            base = (fmt % 8) * 2048 + (0 if fmt < 8 else 16384)
            for qb in range(4):
                dst = PXb[:, base + qb * 512: base + (qb + 1) * 512]
                if L == 1:
                    E("V", lambda e, dst=dst, qb=qb, fmt=fmt: e.tensor_scalar(
                        dst, ps0[:, qb * 512:(qb + 1) * 512], BT1[:, fmt:fmt + 1],
                        None, ALU.add), (("T", tmm),) if qb == 0 else (), inc=(qb == 3))
                else:
                    E("V", lambda e, dst=dst, qb=qb: e.tensor_copy(
                        dst, ps0[:, qb * 512:(qb + 1) * 512]),
                      (("T", tmm),) if qb == 0 else (), inc=(qb == 3))
            last_evict_v = em.now("V")
        qk_v = last_evict_v
        gv = E("G", lambda e: e.dma_start(
            PO.ap()[:, 0:8192].rearrange("p (kc f) -> p kc f", f=1024),
            wq[:, 2048:3072].rearrange("(kc p) f -> p kc f", p=128)),
            tuple(po_free), inc=True, tag="gwv")
        vbias_v = 0
        if L == 1:
            ybv = E("Y", lambda e: e.dma_start(SCR.ap()[0:1, 1024:2048], dr["bv1"]),
                    inc=True, tag="ybv")
            E("T", lambda e: e.matmul(ps1[:, 0:512], ONERF, SCR.ap()[0:1, 1024:1536],
                                      start=True, stop=True), (("ybv", ybv),))
            tb2 = E("T", lambda e: e.matmul(ps1[:, 512:1024], ONERF,
                                            SCR.ap()[0:1, 1536:2048],
                                            start=True, stop=True), inc=True)
            vbias_v = E("V", lambda e: e.tensor_copy(PDf[:, 3072:4096],
                                                     ps1[:, 0:1024]),
                        (("T", tb2),), inc=True)
        ev_t = {}
        for tt in range(16):
            for fc in range(2):
                bank = (tt % 2) * 2 + fc
                w0 = [("gwv", gv)]
                if tt >= 2:
                    w0.append(("V", ev_t[(tt - 2, fc)]))
                if L == 1:
                    w0.append(("V", vbias_v))
                for kc in range(8):
                    E("T", lambda e, tt=tt, fc=fc, kc=kc, bank=bank: e.matmul(
                        ps1[:, bank * 512:(bank + 1) * 512],
                        PA.ap()[:, kc * 2048 + tt * 128: kc * 2048 + (tt + 1) * 128],
                        PO.ap()[:, kc * 1024 + fc * 512: kc * 1024 + (fc + 1) * 512],
                        start=(kc == 0), stop=(kc == 7)),
                      tuple(w0) if kc == 0 else (), inc=(kc == 7))
                tvm = em.now("T")
                base = tt * 1040 + fc * 8 * 65
                dst = PB.ap()[:, base: base + 520].rearrange("p (h e) -> p h e", e=65)[:, :, 0:64]
                src = ps1[:, bank * 512:(bank + 1) * 512].rearrange(
                    "p (h e) -> p h e", e=64)
                if L == 1:
                    bsl = PDf[:, 3072 + fc * 512: 3072 + (fc + 1) * 512] \
                        .rearrange("p (h e) -> p h e", e=64)
                    E("V", lambda e, dst=dst, src=src, bsl=bsl: e.tensor_tensor(
                        dst, src, bsl, ALU.add), (("T", tvm),), inc=True)
                else:
                    E("V", lambda e, dst=dst, src=src: e.tensor_copy(dst, src),
                      (("T", tvm),), inc=True)
                ev_t[(tt, fc)] = em.now("V")
            E("V", lambda e, tt=tt: e.memset(
                PB.ap()[:, tt * 1040: tt * 1040 + 1040].rearrange(
                    "p (h e) -> p h e", e=65)[:, :, 64:65], 1.0), inc=True)
        return {"qk_v": qk_v, "v_v": em.now("V"), "qkv_t": em.now("T")}

    ROWSB = ROWS.ap().bitcast(BF16)

    def ph_bounds(L, ev):
        qkv = ev["qk_v"]
        SA = PAf[0:1, 0:2048]
        SB = PAf[0:1, 2048:4096]
        SC = PAf[0:1, 4096:6144]
        KP = PAf[0:1, 6144:6152]
        ydrain = em.now("yms2")
        yms = []
        for hp in range(8):
            vsq = E("V", lambda e, hp=hp: e.tensor_tensor(
                SCRb[:, 0:2048], PXb[:, hp * 2048:(hp + 1) * 2048],
                PXb[:, hp * 2048:(hp + 1) * 2048], ALU.mult),
                ((("V", qkv),) + tuple(ev.get("bias_ready", ())) if hp == 0
                 else (("T", em.now("T")),)))
            E("V", lambda e, hp=hp: e.tensor_tensor(
                SCRb[:, 2048:4096], PXb[:, 16384 + hp * 2048: 16384 + (hp + 1) * 2048],
                PXb[:, 16384 + hp * 2048: 16384 + (hp + 1) * 2048], ALU.mult),
              inc=True)
            vsq = em.now("V")
            for half in range(2):
                h = hp * 2 + half
                p0 = half * 64
                for tb in range(4):
                    E("T", lambda e, p0=p0, tb=tb: e.matmul(
                        ps1[0:1, tb * 512:(tb + 1) * 512], ONESB[p0:p0 + 64, :],
                        SCRb[p0:p0 + 64, tb * 512:(tb + 1) * 512], start=True,
                        stop=True, tile_position=(p0, 0)),
                      (("V", vsq), ("V", em.now("V"))) if tb == 0 else ())
                    E("T", lambda e, p0=p0, tb=tb: e.matmul(
                        ps0[0:1, tb * 512:(tb + 1) * 512], ONESB[p0:p0 + 64, :],
                        SCRb[p0:p0 + 64, 2048 + tb * 512: 2048 + (tb + 1) * 512],
                        start=True, stop=True, tile_position=(p0, 0)),
                      inc=(tb == 3))
                tq = em.now("T")
                E("V", lambda e: e.tensor_copy(SA, ps1[0:1, 0:2048]),
                  (("T", tq), ("yms", yms[-1] if yms else 0)))
                vcp = E("V", lambda e: e.tensor_copy(SB, ps0[0:1, 0:2048]), inc=True)
                for qb in range(4):
                    E("V", lambda e, qb=qb: e.tensor_reduce(
                        KP[:, qb:qb + 1], SB[:, qb * 512:(qb + 1) * 512], axis=AX.X,
                        op=ALU.max), (("V", vcp),) if qb == 0 else (),
                      inc=(qb == 3))
                vkp = em.now("V")
                if L == 1:
                    for qb in range(1, 4):
                        vkp = E("V", lambda e, qb=qb: e.tensor_tensor(
                            KP[:, qb:qb + 1], KP[:, qb:qb + 1], KP[:, qb - 1:qb],
                            ALU.max), (("V", vkp),), inc=True)
                else:
                    for qb in range(3):
                        vkp = E("V", lambda e, qb=qb: e.tensor_tensor(
                            KP[:, 3:4], KP[:, 3:4], KP[:, qb:qb + 1], ALU.max),
                            (("V", vkp),), inc=True)
                    for qb in range(3):
                        E("V", lambda e, qb=qb: e.tensor_scalar(
                            KP[:, qb:qb + 1], KP[:, 3:4], 1.0, None, ALU.mult),
                          (("V", vkp),) if qb == 0 else (), inc=(qb == 2))
                    vkp = em.now("V")
                for qb in range(4):
                    E("V", lambda e, qb=qb: e.tensor_scalar(
                        SC[:, qb * 512:(qb + 1) * 512], SA[:, qb * 512:(qb + 1) * 512],
                        KP[:, qb:qb + 1], None, ALU.mult),
                      (("V", vkp),) if qb == 0 else (), inc=(qb == 3))
                vpr = em.now("V")
                sl = E("S", lambda e: e.activation(SC, SC, AF.Ln, bias=C30R),
                       (("V", vpr),), inc=True)
                sl = E("S", lambda e: e.activation(SC, SC, AF.Exp, scale=0.5),
                       (("S", sl),), inc=True)
                vm2 = E("V", lambda e: e.tensor_scalar(
                    PA.ap()[0:1, 12352:14400], SC, -1.0, -4.0, ALU.mult, ALU.add),
                    (("S", sl),), inc=True)
                yms.append(E("Y", lambda e, h=h: e.dma_start(
                    ROWSB[8 + h: 9 + h, 0:2048], PA.ap()[0:1, 12352:14400]),
                    (("V", vm2), ("yms2", ydrain)), inc=True, tag="yms"))
        return {"mrows_y": em.now("yms"), "mrows_v": em.now("V")}

    def ph_attncore(L, ev, bg=None):
        load_actset("natural_log_exp_and_others")
        prev_exp = em.now("S")
        prev_ev_v = em.now("V")
        prev_mT = 0
        first_all = True
        MSA = SCRb[0:1, 2048:2560]
        MSB = SCRb[0:1, 3072:3584]
        for hp in range(8):
            for qb in range(4):
                kc_hi = 16 if L == 0 else 4 * qb + 4
                yms = E("Y", lambda e, hp=hp, qb=qb: e.dma_start(
                    MSA, ROWSB[8 + 2 * hp: 9 + 2 * hp, qb * 512:(qb + 1) * 512]),
                    (("yms", ev["mrows_y"]), ("T", prev_mT)), inc=True, tag="yms2")
                yms = E("Y", lambda e, hp=hp, qb=qb: e.dma_start(
                    MSB, ROWSB[9 + 2 * hp: 10 + 2 * hp, qb * 512:(qb + 1) * 512]),
                    inc=True, tag="yms2")
                exps = {}

                def mk_steps(kc):
                    steps = []
                    for half in range(2):
                        p0 = half * 64
                        qoff = max(0, kc * 128 - qb * 512) if L == 1 else 0
                        col = (kc % 2) * 1024 + half * 512
                        lhs = PXb[p0:p0 + 64, 16384 + hp * 2048 + kc * 128:
                                  16384 + hp * 2048 + (kc + 1) * 128]
                        rhs = PXb[p0:p0 + 64, hp * 2048 + qb * 512 + qoff:
                                  hp * 2048 + (qb + 1) * 512]
                        badds = []
                        if L == 0 and qb <= 1 and kc <= 8:
                            badds.append((col, 512,
                                          PD.ap()[:, kc * 1028 + qb * 512:
                                                  kc * 1028 + (qb + 1) * 512]))
                        elif L == 0 and qb == 2 and kc <= 8:
                            badds.append((col, 1,
                                          PD.ap()[:, kc * 1028 + 1024:
                                                  kc * 1028 + 1025]))
                        elif L == 1:
                            h = hp * 2 + half
                            j0 = kc - 4 * qb
                            if 0 <= j0 <= 3:
                                badds.append((col + j0 * 128, 128,
                                              PD.ap()[:, h * 128:(h + 1) * 128]))
                            if 0 <= j0 + 1 <= 3:
                                badds.append((col + (j0 + 1) * 128, 128,
                                              PD.ap()[:, 4096 + h * 128:
                                                      4096 + (h + 1) * 128]))
                            if j0 == -1:
                                badds.append((col, 128,
                                              PD.ap()[:, 4096 + h * 128:
                                                      4096 + (h + 1) * 128]))
                        steps.append((lambda e, col=col, lhs=lhs, rhs=rhs,
                                      qoff=qoff, p0=p0: e.matmul(
                            ps0[:, col + qoff: col + 512], lhs, rhs,
                            start=True, stop=False, tile_position=(p0, 0)),
                            False))
                        for (c_, w_, tsl) in badds:
                            steps.append((lambda e, c_=c_, w_=w_, tsl=tsl: e.matmul(
                                ps0[:, c_: c_ + w_], IDENT, tsl,
                                start=False, stop=False), False))
                        ms = MSA if half == 0 else MSB
                        steps.append((lambda e, col=col, ms=ms, qoff=qoff:
                                      e.matmul(
                            ps0[:, col + qoff: col + 512], ONER,
                            ms[:, qoff:512], start=False, stop=True), True))
                    return steps

                def emit_scores(kc):
                    steps = mk_steps(kc)
                    for i, (fn, is_m) in enumerate(steps):
                        w = []
                        if i == 0:
                            w = [("S", exps[kc - 2] if kc >= 2 else prev_exp)]
                            if first_all:
                                w += [("V", ev["mrows_v"]), ("V", ev["v_v"])] + \
                                    list(ev.get("bias_ready", []))
                        if is_m:
                            w.append(("yms2", yms))
                        E("T", fn, tuple(w), inc=(i == len(steps) - 1))
                    return em.now("T")

                def emit_exp(kc):
                    tsc = em.now("T")
                    for half in range(2):
                        dst = PA.ap()[:, half * 8192 + kc * 512:
                                      half * 8192 + (kc + 1) * 512]
                        srcc = (kc % 2) * 1024 + half * 512
                        E("S", lambda e, dst=dst, srcc=srcc: e.activation(
                            dst, ps0[:, srcc: srcc + 512], AF.Exp, scale=0.125),
                          (("T", tsc),) if half == 0 else (), inc=(half == 1))
                    exps[kc] = em.now("S")

                def emit_av(kc):
                    for half in range(2):
                        h = hp * 2 + half
                        qoff = max(0, kc * 128 - qb * 512) if L == 1 else 0
                        vsl = PB.ap()[:, kc * 1040 + h * 65: kc * 1040 + h * 65 + 65]
                        psl = PA.ap()[:, half * 8192 + kc * 512 + qoff:
                                      half * 8192 + (kc + 1) * 512]
                        w = ()
                        if half == 0:
                            w = (("S", exps[kc]),)
                            if kc == 0:
                                w += (("V", prev_ev_v),)
                        E("T", lambda e, vsl=vsl, psl=psl, half=half, qoff=qoff,
                          st=(kc == 0), sp=(kc == kc_hi - 1): e.matmul(
                            ps1[0:65, half * 512 + qoff:(half + 1) * 512],
                            vsl, psl, start=st, stop=sp), w, inc=(half == 1))

                emit_scores(0)
                first_all = False
                emit_exp(0)
                for kc in range(1, kc_hi):
                    emit_scores(kc)
                    emit_av(kc - 1)
                    emit_exp(kc)
                emit_av(kc_hi - 1)
                prev_exp = exps[kc_hi - 1]
                tav = em.now("T")
                for half in range(2):
                    h = hp * 2 + half
                    vrec = E("V", lambda e, half=half: e.reciprocal(
                        SCR.ap()[0:1, 512:1024],
                        ps1[64:65, half * 512:(half + 1) * 512]), (("T", tav),),
                        inc=True)
                    trb = E("T", lambda e: e.matmul(
                        ps1[0:64, 1024 + 0:1024 + 512], ONERF[0:1, 0:64],
                        SCR.ap()[0:1, 512:1024], start=True, stop=True), (("V", vrec),),
                        inc=True)
                    vcb = E("V", lambda e: e.tensor_copy(SCR.ap()[0:64, 0:512],
                                                         ps1[0:64, 1024:1536]),
                            (("T", trb),), inc=True)
                    dst = PO.ap()[(h % 2) * 64:(h % 2) * 64 + 64,
                                  (h // 2) * 2048 + qb * 512:
                                  (h // 2) * 2048 + (qb + 1) * 512]
                    E("V", lambda e, dst=dst, half=half: e.tensor_tensor(
                        dst, ps1[0:64, half * 512:(half + 1) * 512],
                        SCR.ap()[0:64, 0:512], ALU.mult), (("V", vcb),), inc=True)
                prev_ev_v = em.now("V")
                prev_mT = em.now("T")
                if bg is not None:
                    next(bg)
        return {"attn_t": em.now("T"), "attn_v": em.now("V"), "attn_s": em.now("S")}

    def ph_proj(L, ev, xsrc, xdst):
        wp = dr["wprojT0"] if L == 0 else dr["wprojT1"]
        bcol = BPB0 if L == 0 else BP1
        gw = E("G", lambda e: e.dma_start(
            PB.ap()[:, 0:8192].rearrange("p (kc f) -> p kc f", f=1024),
            wp.rearrange("(kc p) f -> p kc f", p=128)),
            (("T", ev["attn_t"]),), inc=True, tag="gwp")
        tmp = PBf[:, 4096:8192]
        CPB = PA.ap()[:, 0:512]
        SQB = PA.ap()[:, 512:1024]
        vprev_tb = em.now("V")
        yout = 0
        for tb in range(4):
            if L == 0:
                PR1 = PAf[0:1, 512:1024]
                PR2 = PAf[0:1, 1024:1536]
                E("V", lambda e, tb=tb: e.tensor_copy(
                    PR1, R_RS0[0:1, tb * 512:(tb + 1) * 512]),
                  (("V", vprev_tb), ("V", ev["attn_v"])))
                vcp = E("V", lambda e, tb=tb: e.tensor_copy(
                    PR2, R_M0R0[0:1, tb * 512:(tb + 1) * 512]), inc=True)
                E("T", lambda e: e.matmul(
                    ps1[:, 1024:1536], ONERF, PR1,
                    start=True, stop=True), (("V", vcp),))
                tbc = E("T", lambda e: e.matmul(
                    ps1[:, 1536:2048], ONERF, PR2,
                    start=True, stop=True), inc=True)
                E("V", lambda e: e.tensor_copy(SCR.ap()[:, 512:1024], ps1[:, 1024:1536]),
                  (("T", tbc),))
                E("V", lambda e: e.tensor_copy(SCR.ap()[:, 1024:1536],
                                               ps1[:, 1536:2048]), inc=True)
            vbc = em.now("V")
            v_evict = {}
            for Mt in range(8):
                bank = Mt % 4
                yxr = "yxr%d" % (Mt % 2)
                yx = E("Y", lambda e, Mt=Mt, tb=tb: e.dma_start(
                    WS.ap()[:, (Mt % 2) * 512:(Mt % 2) * 512 + 512],
                    xsrc[Mt * 128:(Mt + 1) * 128, tb * 512:(tb + 1) * 512]),
                    ((("V", v_evict[Mt - 2]),) if Mt >= 2 else (("V", vprev_tb),)) +
                    (tuple(ev.get("xland", ())) if tb == 0 and Mt == 0 else ()),
                    inc=True, tag=yxr)
                w0 = [("gwp", gw), ("V", ev["attn_v"])]
                if Mt >= 4:
                    w0.append(("V", v_evict[Mt - 4]))
                for kc in range(8):
                    E("T", lambda e, tb=tb, Mt=Mt, kc=kc, bank=bank: e.matmul(
                        ps0[:, bank * 512:(bank + 1) * 512],
                        PB.ap()[:, kc * 1024 + Mt * 128: kc * 1024 + (Mt + 1) * 128],
                        PO.ap()[:, kc * 2048 + tb * 512: kc * 2048 + (tb + 1) * 512],
                        start=(kc == 0), stop=(kc == 7)),
                      tuple(w0) if kc == 0 else (), inc=(kc == 7))
                tmm = em.now("T")
                xs = WS.ap()[:, (Mt % 2) * 512:(Mt % 2) * 512 + 512]
                tmps = tmp[:, Mt * 512:(Mt + 1) * 512]
                wprev = (("V", v_evict[Mt - 1]),) if Mt >= 1 else ()
                if L == 0:
                    E("V", lambda e, xs=xs, Mt=Mt: e.scalar_tensor_tensor(
                        SCR.ap()[:, 1536:2048], xs, GAM0[:, Mt:Mt + 1],
                        SCR.ap()[:, 512:1024], ALU.mult, ALU.mult),
                      (("T", tmm), (yxr, yx), ("V", vbc)) + wprev, inc=True)
                    E("V", lambda e, tmps=tmps, xs=xs, bank=bank, Mt=Mt:
                      e.scalar_tensor_tensor(
                        tmps, ps0[:, bank * 512:(bank + 1) * 512], bcol[:, Mt:Mt + 1],
                        xs, ALU.add, ALU.add), inc=True)
                    E("V", lambda e, tmps=tmps: e.tensor_tensor(
                        tmps, tmps, SCR.ap()[:, 1536:2048], ALU.add),
                      (("V", em.now("V")),), inc=True)
                    E("V", lambda e, tmps=tmps, Mt=Mt: e.scalar_tensor_tensor(
                        tmps, SCR.ap()[:, 1024:1536], GAM0[:, Mt:Mt + 1], tmps,
                        ALU.mult, ALU.add), (("V", em.now("V")),), inc=True)
                else:
                    E("V", lambda e, tmps=tmps, xs=xs, bank=bank, Mt=Mt:
                      e.scalar_tensor_tensor(
                        tmps, ps0[:, bank * 512:(bank + 1) * 512], bcol[:, Mt:Mt + 1],
                        xs, ALU.add, ALU.add),
                      (("T", tmm), (yxr, yx)) + wprev, inc=True)
                v_evict[Mt] = em.now("V")
            for Mt in range(8):
                tmps = tmp[:, Mt * 512:(Mt + 1) * 512]
                E("V", lambda e, tmps=tmps: e.tensor_copy(CPB, tmps),
                  (("V", v_evict[7]), ("T", em.now("T"))) if Mt == 0
                  else (("T", em.now("T")),))
                E("V", lambda e, tmps=tmps: e.tensor_tensor(SQB, tmps, tmps, ALU.mult),
                  inc=True)
                vst = em.now("V")
                E("T", lambda e, Mt=Mt: e.matmul(
                    ps1[0:1, 0:512], ONESB, CPB, start=(Mt == 0), stop=(Mt == 7)),
                  (("V", vst),))
                E("T", lambda e, Mt=Mt: e.matmul(
                    ps1[0:1, 512:1024], ONESB, SQB, start=(Mt == 0), stop=(Mt == 7)),
                  inc=True)
            tst = em.now("T")
            rmu = PAf[0:1, 1536:2048]
            rsd = PAf[0:1, 2048:2560]
            vmu1 = E("V", lambda e: e.tensor_scalar(rmu, ps1[0:1, 0:512],
                                                    1.0 / 1024, None,
                                                    ALU.mult), (("T", tst),),
                     inc=True)
            E("V", lambda e: e.tensor_tensor(rsd, rmu, rmu, ALU.mult),
              (("V", vmu1),), inc=True)
            E("V", lambda e: e.scalar_tensor_tensor(rsd, ps1[0:1, 512:1024],
                                                    1.0 / 1024, rsd, ALU.mult,
                                                    ALU.subtract),
              (("V", em.now("V")),), inc=True)
            vvar = em.now("V")
            sl = E("S", lambda e: e.activation(rsd, rsd, AF.Ln, bias=EPSR),
                   (("V", vvar),), inc=True)
            sl = E("S", lambda e: e.activation(rsd, rsd, AF.Exp, scale=-0.5),
                   (("S", sl),), inc=True)
            E("T", lambda e: e.matmul(ps1[:, 1024:1536], ONERF, rmu, start=True,
                                      stop=True), (("S", sl),))
            tbb = E("T", lambda e: e.matmul(ps1[:, 1536:2048], ONERF, rsd, start=True,
                                            stop=True), inc=True)
            E("V", lambda e: e.tensor_copy(SCR.ap()[:, 0:512], ps1[:, 1024:1536]),
              (("T", tbb),))
            E("V", lambda e: e.tensor_copy(SCR.ap()[:, 512:1024], ps1[:, 1536:2048]),
              inc=True)
            vbb = em.now("V")
            for Mt in range(8):
                tmps = tmp[:, Mt * 512:(Mt + 1) * 512]
                E("V", lambda e, tmps=tmps: e.tensor_tensor(
                    tmps, tmps, SCR.ap()[:, 0:512], ALU.subtract),
                  (("V", vbb),) if Mt == 0 else (), inc=True)
                dst = PX.ap()[:, Mt * 2048 + tb * 512: Mt * 2048 + (tb + 1) * 512]
                E("V", lambda e, tmps=tmps, dst=dst: e.tensor_tensor(
                    dst, tmps, SCR.ap()[:, 512:1024], ALU.mult),
                  (("V", em.now("V")),), inc=True)
            vln = em.now("V")
            yout = E("Y", lambda e, tb=tb: e.dma_start(
                xdst[:, tb * 512:(tb + 1) * 512]
                .rearrange("(mt p) t -> p mt t", p=128),
                PX.ap().rearrange("p (mt t) -> p mt t", t=2048)[:, :, tb * 512:(tb + 1) * 512]),
                (("V", vln),), inc=True, tag="yo0")
            vprev_tb = vln
        return {"proj_v": em.now("V"), "proj_y": yout, "proj_t": em.now("T"),
                "proj_s": em.now("S"), "xland": (("yo0", yout),)}

    def ph_wmean_gen(L, ev):
        srcs = [dr["w1T0" if L == 0 else "w1T1"], dr["w2T0" if L == 0 else "w2T1"]]
        ACC = SMF.ap()[:, 523:587]
        vch = {}
        n = 0
        for wi in range(2):
            w = srcs[wi]
            flat = w.rearrange("a b -> (a b)").rearrange("(c p n) -> c p n", p=128,
                                                         n=1024)
            for c in range(32):
                buf = WS.ap()[:, (c % 2) * 1024:(c % 2) * 1024 + 1024]
                gwm = "gwm%d" % (c % 2)
                gd = E("G", lambda e, c=c, buf=buf, flat=flat: e.dma_start(
                    buf, flat[c]),
                    (("V", vch[(wi, c - 2)]),) if c >= 2 else
                    ((("V", ev["qk_v"]),) if wi == 0 else
                     (("V", vch[(0, 30 + c)]),)), inc=True, tag=gwm)
                E("V", lambda e, wi=wi, c=c, buf=buf: e.tensor_reduce(
                    ACC[:, wi * 32 + c: wi * 32 + c + 1], buf, axis=AX.X, op=ALU.add,
                    apply_absolute_value=True), ((gwm, gd),), inc=True)
                vch[(wi, c)] = em.now("V")
                n += 1
                if n % 2 == 0:
                    yield
        while True:
            yield

    def ph_wmean_fin(L):
        ACC = SMF.ap()[:, 523:587]
        out = {}
        for wi in range(2):
            E("V", lambda e, wi=wi: e.tensor_reduce(
                SMF.ap()[:, 587 + wi: 588 + wi], ACC[:, wi * 32: wi * 32 + 32],
                axis=AX.X, op=ALU.add), (("V", em.now("V")),), inc=True)
            vsum = em.now("V")
            tcp = E("T", lambda e, wi=wi: e.matmul(
                ps1[0:1, 1792 + wi * 8: 1793 + wi * 8], ONESFC,
                SMF.ap()[:, 587 + wi: 588 + wi], start=True, stop=True),
                (("V", vsum),), inc=True)
            MCF = SMF.ap()[0:1, 592 + wi: 593 + wi]
            MCQ = SMF.ap()[0:1, 594 + wi: 595 + wi]
            WSS = SMF.ap()[0:1, 596 + wi: 597 + wi]
            vmc = E("V", lambda e, wi=wi, MCF=MCF: e.tensor_scalar(
                MCF, ps1[0:1, 1792 + wi * 8: 1793 + wi * 8], 1.0 / (4096 * 1024),
                1e-5, ALU.mult, ALU.max), (("T", tcp),), inc=True)
            E("V", lambda e, MCF=MCF, MCQ=MCQ: e.tensor_scalar(
                MCQ, MCF, 1.0 / 127, None, ALU.mult), (("V", vmc),))
            E("V", lambda e, MCF=MCF, WSS=WSS: e.reciprocal(WSS, MCF), inc=True)
            vws = em.now("V")
            trp = E("T", lambda e, WSS=WSS: e.matmul(
                ps1[:, 1536:1537], ONERF, WSS, start=True, stop=True),
                (("V", vws),), inc=True)
            E("V", lambda e, wi=wi: e.tensor_copy(WSR[wi], ps1[:, 1536:1537]),
              (("T", trp),), inc=True)
            out[f"mcq{wi}"] = MCQ
        out["wmean_v"] = em.now("V")
        return out

    def ph_quantx(L, ev):
        pv = ev["proj_v"]
        Mp = POf[:, 0:2048]
        Mn = POf[:, 2048:4096]
        E("V", lambda e: e.tensor_copy(Mp, PX.ap()[:, 0:2048]), (("V", pv),))
        E("V", lambda e: e.tensor_copy(Mn, PX.ap()[:, 0:2048]), inc=True)
        for kc in range(1, 8):
            E("V", lambda e, kc=kc: e.tensor_tensor(
                Mp, Mp, PX.ap()[:, kc * 2048:(kc + 1) * 2048], ALU.max),
              (("V", em.now("V")),))
            E("V", lambda e, kc=kc: e.tensor_tensor(
                Mn, Mn, PX.ap()[:, kc * 2048:(kc + 1) * 2048], ALU.min), inc=True)
        E("V", lambda e: e.tensor_scalar(Mn, Mn, -1.0, None, ALU.mult),
          (("V", em.now("V")),), inc=True)
        E("V", lambda e: e.tensor_tensor(Mp, Mp, Mn, ALU.max),
          (("V", em.now("V")),), inc=True)
        vam = em.now("V")
        vred = 0
        for tt in range(16):
            ttr = E("T", lambda e, tt=tt: e.transpose(
                ps1[:, (tt % 2) * 512:(tt % 2) * 512 + 128],
                Mp[:, tt * 128:(tt + 1) * 128], IDF),
                (("V", vam),) if tt < 2 else (("V", vred),), inc=True)
            E("V", lambda e, tt=tt: e.tensor_reduce(
                AMC[:, tt:tt + 1], ps1[:, (tt % 2) * 512:(tt % 2) * 512 + 128],
                axis=AX.X, op=ALU.max), (("T", ttr),), inc=True)
            vred = em.now("V")
        E("V", lambda e: e.tensor_scalar(AMC, AMC, 1e-5, None, ALU.max),
          (("V", em.now("V")),), inc=True)
        vclip = em.now("V")
        y1 = E("Y", lambda e: e.dma_start(
            row_d[3:4, 0:2048].rearrange("o (c p) -> (o p) c", p=128), AMC[:, 0:16],
            allow_slow_non_contiguous=True), (("V", vclip),), inc=True, tag="yrow")
        AROW = POf[0:1, 4096:6144]
        y2 = E("Y", lambda e: e.dma_start(AROW, row_d[3:4, 0:2048]),
               (("yrow", y1),), inc=True, tag="yrow")
        SXR = POf[0:1, 6144:8192]
        vrc = E("V", lambda e: e.reciprocal(SXR, AROW), (("yrow", y2),), inc=True)
        E("V", lambda e: e.tensor_scalar(SXR, SXR, 127.0, None, ALU.mult),
          (("V", vrc),), inc=True)
        E("V", lambda e: e.tensor_scalar(R0[:, 0:2048], AROW, ev["mcq0"], None,
                                         ALU.mult), inc=True)
        vrows = em.now("V")
        for tb in range(4):
            tb1 = E("T", lambda e, tb=tb: e.matmul(
                ps1[:, 1024:1536], ONERF, R0[0:1, tb * 512:(tb + 1) * 512],
                start=True, stop=True), (("V", vrows), ("V", em.now("V"))),
                inc=True)
            E("V", lambda e, tb=tb: e.tensor_copy(
                PDf[:, tb * 512:(tb + 1) * 512], ps1[:, 1024:1536]),
              (("T", tb1),), inc=True)
        slots = [SCR.ap()[:, i * 512:(i + 1) * 512] for i in range(4)]
        s1c = {}
        s2c = {}
        pend = []

        def xq_s2():
            p, sl = pend.pop(0)
            tb_, kc_ = divmod(p, 8)
            E("V", lambda e, sl=sl, tb_=tb_, kc_=kc_: e.tensor_scalar(
                PA.ap()[:, kc_ * 2048 + tb_ * 512: kc_ * 2048 + (tb_ + 1) * 512],
                sl, MAGIC, MAGIC, ALU.add, ALU.subtract),
              (("V", s1c[p]),), inc=True)
            s2c[p] = em.now("V")
        for tb in range(4):
            tsx = E("T", lambda e, tb=tb: e.matmul(
                ps1[:, 1536:2048], ONERF, SXR[:, tb * 512:(tb + 1) * 512],
                start=True, stop=True), (("V", em.now("V")),), inc=True)
            for kc in range(8):
                p = tb * 8 + kc
                sl = slots[p % 4]
                E("V", lambda e, sl=sl, tb=tb, kc=kc: e.tensor_tensor(
                    sl, PX.ap()[:, kc * 2048 + tb * 512:
                                kc * 2048 + (tb + 1) * 512],
                    ps1[:, 1536:2048], ALU.mult),
                  (("T", tsx), ("V", s2c.get(p - 4, 0))), inc=True)
                s1c[p] = em.now("V")
                pend.append((p, sl))
                if len(pend) >= 4:
                    xq_s2()
        while pend:
            xq_s2()
        return {"xq_v": em.now("V"), "r1b_v": em.now("V")}

    def ph_ffnA(L, ev):
        cwL = CW[L]
        cbL = CB[L]
        w1 = dr["w1T0" if L == 0 else "w1T1"]
        Mp = PB.ap()[:, 0:2048]
        Mn = PB.ap()[:, 2048:4096]
        load_actset("silu_and_others")
        E("V", lambda e: e.memset(Mp, -3.0e38), (("T", ev.get("attn_t", 0)),))
        E("V", lambda e: e.memset(Mn, 3.0e38))
        E("V", lambda e: e.memset(PO.ap()[:, 0:1], 0.0))
        E("V", lambda e: e.memset(PO.ap()[:, 2049:2050], 0.0), inc=True)
        vinit = em.now("V")
        vq = {}
        vus = {}
        vamax = {}
        ysp = {}
        sse = {}
        tm1s = {}
        for ct in range(32):
            buf = WS.ap()[:, (ct % 2) * 1024:(ct % 2) * 1024 + 1024]
            qb_ = W1Q[:, (ct % 2) * 1024:(ct % 2) * 1024 + 1024]
            yw1 = "yw1%d" % (ct % 2)
            gw = E("Y", lambda e, ct=ct, buf=buf: e.dma_start(
                buf.rearrange("p (kc j) -> p kc j", j=128),
                w1[:, ct * 128:(ct + 1) * 128].rearrange("(kc p) j -> p kc j", p=128)),
                (("V", vq[ct - 2]),) if ct >= 2 else (("V", ev["wmean_v"]),),
                inc=True, tag=yw1)
            E("V", lambda e, buf=buf: e.tensor_scalar(
                SCR.ap()[:, 1024:2048], buf, WSR[0], MAGIC, ALU.mult, ALU.add),
              ((yw1, gw), ("T", tm1s.get(ct - 2, 0)), ("V", em.now("V"))), inc=True)
            E("V", lambda e: e.tensor_scalar(
                SCR.ap()[:, 1024:2048], SCR.ap()[:, 1024:2048], MAGIC, -1.0,
                ALU.subtract, ALU.max), (("V", em.now("V")),), inc=True)
            E("V", lambda e, qb_=qb_: e.tensor_scalar(
                qb_, SCR.ap()[:, 1024:2048], 1.0, None, ALU.min),
              (("V", em.now("V")),), inc=True)
            vq[ct] = em.now("V")
            for kc in range(8):
                for tb in range(4):
                    w = []
                    if kc == 0 and tb == 0:
                        w = [("V", vq[ct]), ("V", ev["xq_v"])]
                        if ct >= 1:
                            w.append(("V", vus[ct - 1]))
                    E("T", lambda e, kc=kc, tb=tb, qb_=qb_: e.matmul(
                        ps0[:, tb * 512:(tb + 1) * 512], qb_[:, kc * 128:(kc + 1) * 128],
                        PA.ap()[:, kc * 2048 + tb * 512: kc * 2048 + (tb + 1) * 512],
                        start=(kc == 0), stop=(kc == 7)), tuple(w),
                      inc=(kc == 7 and tb == 3))
            tm1 = em.now("T")
            tm1s[ct] = tm1
            E("V", lambda e: e.tensor_tensor(
                PO.ap()[:, 1:2049], ps0[:, 0:2048], PDf[:, 0:2048], ALU.mult),
              (("T", tm1), ("V", ev["r1b_v"])), inc=True)
            vus[ct] = em.now("V")
            for t in range(3):
                E("V", lambda e, t=t, ct=ct: e.tensor_scalar(
                    DIAG[t], IDENT, cwL[:, t * 32 + ct: t * 32 + ct + 1], None,
                    ALU.mult), (("T", em.now("T")),) if t == 0 else (),
                  inc=(t == 2))
            vdg = em.now("V")
            for tb in range(4):
                for t in range(3):
                    E("T", lambda e, tb=tb, t=t: e.matmul(
                        ps1[:, (tb % 2) * 512:(tb % 2) * 512 + 512], DIAG[t],
                        PO.ap()[:, tb * 512 + t: tb * 512 + t + 512],
                        start=(t == 0), stop=(t == 2)),
                      (("V", vdg), ("S", sse.get(ct - 1, 0))) if tb == 0 and t == 0
                      else ((("S", sse.get((ct, tb - 2), 0)),) if t == 0 and tb >= 2
                            else ()), inc=(t == 2))
                tcv = em.now("T")
                stg = PO.ap()[:, 4096 + (ct % 2) * 2048: 4096 + (ct % 2) * 2048 + 2048]
                E("S", lambda e, tb=tb, stg=stg, ct=ct: e.activation(
                    stg[:, tb * 512:(tb + 1) * 512],
                    ps1[:, (tb % 2) * 512:(tb % 2) * 512 + 512], AF.Silu,
                    bias=cbL[:, ct:ct + 1]),
                  (("T", tcv),) + ((("yhs%d" % (ct % 2), ysp[ct - 2]),)
                                   if tb == 0 and ct >= 2 else ()), inc=True)
                sse[(ct, tb)] = em.now("S")
            sse[ct] = em.now("S")
            stg = PO.ap()[:, 4096 + (ct % 2) * 2048: 4096 + (ct % 2) * 2048 + 2048]
            E("V", lambda e, stg=stg: e.tensor_tensor(Mp, Mp, stg, ALU.max),
              (("S", sse[ct]), ("V", vinit)) +
              ((("V", vamax[ct - 1]),) if ct >= 1 else ()))
            E("V", lambda e, stg=stg: e.tensor_tensor(Mn, Mn, stg, ALU.min), inc=True)
            vamax[ct] = em.now("V")
            ysp[ct] = E("Y", lambda e, ct=ct, stg=stg: e.dma_start(hsil_d[ct], stg),
                        (("S", sse[ct]),), inc=True, tag="yhs%d" % (ct % 2))
        return {"ffnA_v": em.now("V"), "ffnA_t": em.now("T"),
                "ffnA_y": (("yhs0", em.now("yhs0")), ("yhs1", em.now("yhs1"))),
                "ffnA_s": em.now("S"), "amax_v": vamax[31]}

    def ph_ffnmid(L, ev):
        Mp = PB.ap()[:, 0:2048]
        Mn = PB.ap()[:, 2048:4096]
        vneg = E("V", lambda e: e.tensor_scalar(Mn, Mn, -1.0, None, ALU.mult),
                 (("V", ev["ffnA_v"]),), inc=True)
        E("V", lambda e: e.tensor_tensor(Mp, Mp, Mn, ALU.max), (("V", vneg),),
          inc=True)
        vam = em.now("V")
        A2C = SMF.ap()[:, 603:619]
        vred = 0
        for tt in range(16):
            ttr = E("T", lambda e, tt=tt: e.transpose(
                ps1.ap().bitcast(BF16)[:, (tt % 2) * 1024:(tt % 2) * 1024 + 128]
                if False else
                ps1[:, (tt % 2) * 512:(tt % 2) * 512 + 64].bitcast(BF16),
                Mp[:, tt * 128:(tt + 1) * 128], IDENT),
                (("V", vam),) if tt < 2 else (("V", vred),), inc=True)
            E("V", lambda e, tt=tt: e.tensor_reduce(
                A2C[:, tt:tt + 1],
                ps1[:, (tt % 2) * 512:(tt % 2) * 512 + 64].bitcast(BF16),
                axis=AX.X, op=ALU.max), (("T", ttr),), inc=True)
            vred = em.now("V")
        E("V", lambda e: e.tensor_scalar(A2C, A2C, 1e-5, None, ALU.max),
          (("V", em.now("V")),), inc=True)
        vclip = em.now("V")
        y1 = E("Y", lambda e: e.dma_start(
            row_d[4:5, 0:2048].rearrange("o (c p) -> (o p) c", p=128), A2C,
            allow_slow_non_contiguous=True),
            (("V", vclip)) + tuple(ev["ffnA_y"])
            if False else (("V", vclip),) + tuple(ev["ffnA_y"]),
            inc=True, tag="yrow")
        AROW = POf[0:1, 4096:6144]
        y2 = E("Y", lambda e: e.dma_start(AROW, row_d[4:5, 0:2048]),
               (("yrow", y1),), inc=True, tag="yrow")
        S2R = POf[0:1, 6144:8192]
        vrc = E("V", lambda e: e.reciprocal(S2R, AROW),
                (("yrow", y2), ("V", em.now("V"))), inc=True)
        E("V", lambda e: e.tensor_scalar(S2R, S2R, 127.0, None, ALU.mult),
          (("V", vrc),))
        E("V", lambda e: e.tensor_scalar(R0[:, 0:2048], AROW, ev["mcq1"], None,
                                         ALU.mult), inc=True)
        vrows = em.now("V")
        for tb in range(4):
            t1 = E("T", lambda e, tb=tb: e.matmul(
                ps1[:, 1024:1536], ONERF, S2R[:, tb * 512:(tb + 1) * 512],
                start=True, stop=True),
                (("V", vrows), ("V", ev["ffnA_v"])), inc=True)
            E("V", lambda e, tb=tb: e.tensor_copy(
                PDf[:, 2048 + tb * 512: 2048 + (tb + 1) * 512], ps1[:, 1024:1536]),
              (("T", t1),), inc=True)
            t2 = E("T", lambda e, tb=tb: e.matmul(
                ps1[:, 1536:2048], ONERF, R0[0:1, tb * 512:(tb + 1) * 512],
                start=True, stop=True), (("V", em.now("V")),), inc=True)
            E("V", lambda e, tb=tb: e.tensor_copy(
                PDf[:, tb * 512:(tb + 1) * 512], ps1[:, 1536:2048]),
              (("T", t2),), inc=True)
        return {"mid_v": em.now("V"), "mid_y": em.now("yrow")}

    def ph_ffnB(L, ev, xsrc, xdst):
        load_actset("natural_log_exp_and_others")
        w2 = dr["w2T0" if L == 0 else "w2T1"]
        vqw = {}
        wbufs = [SCR.ap()[:, 1024:2048], ROWS.ap()[:, 1024:2048],
                 PBf[:, 4096:5120], PBf[:, 5120:6144]]
        q1c, q2c, q3c = {}, {}, {}

        def w2_q2(cc):
            b = wbufs[cc % 4]
            E("V", lambda e, b=b: e.tensor_scalar(
                b, b, MAGIC, -1.0, ALU.subtract, ALU.max),
              (("V", q1c[cc]),), inc=True)
            q2c[cc] = em.now("V")

        def w2_q3(cc):
            b = wbufs[cc % 4]
            dst = (PO.ap()[:, cc * 1024:(cc + 1) * 1024] if cc < 16 else
                   PA.ap()[:, (cc - 16) * 1024:(cc - 15) * 1024])
            w = [("V", q2c[cc])]
            if cc == 0:
                w += [("V", ev["mid_v"]), ("yrow", ev["mid_y"])]
            if cc == 16:
                w.append(("T", ev["ffnA_t"]))
            E("V", lambda e, dst=dst, b=b: e.tensor_scalar(
                dst, b, 1.0, None, ALU.min), tuple(w), inc=True)
            q3c[cc] = em.now("V")
            vqw[cc] = q3c[cc]
        for cc in range(32):
            buf = WS.ap()[:, (cc % 2) * 1024:(cc % 2) * 1024 + 1024]
            yw2 = "yw2%d" % (cc % 2)
            gw = E("Y", lambda e, cc=cc, buf=buf: e.dma_start(
                buf, w2[cc * 128:(cc + 1) * 128, :]),
                (("V", q1c[cc - 2]),) if cc >= 2 else
                (("V", ev["ffnA_v"]), ("T", ev["ffnA_t"]), ("S", ev["ffnA_s"])),
                inc=True, tag=yw2)
            wA = [(yw2, gw)]
            if cc >= 4:
                wA.append(("V", q3c[cc - 4]))
            E("V", lambda e, buf=buf, b=wbufs[cc % 4]: e.tensor_scalar(
                b, buf, WSR[1], MAGIC, ALU.mult, ALU.add), tuple(wA), inc=True)
            q1c[cc] = em.now("V")
            if cc >= 1:
                w2_q2(cc - 1)
            if cc >= 2:
                w2_q3(cc - 2)
        w2_q2(31)
        w2_q3(30)
        w2_q3(31)
        vw2 = em.now("V")
        tmp = PBf[:, 0:4096]
        outb = PBf[:, 4096:8192]
        CPB = SCRb[:, 3072:3584]
        SQB = SCRb[:, 3584:4096]
        yhs = {}
        vhq = {}
        tmm = {}
        youts = 0
        vev_prev = 0
        for tb in range(4):
            hqb = PXb[:, (tb % 2) * 16384:(tb % 2) * 16384 + 16384]
            for cc in range(32):
                q_ = ("yhq%d" % (tb % 2), "ghq%d" % (tb % 2))[cc % 2]
                yhs[(tb, cc)] = E(("Y", "G")[cc % 2],
                                  lambda e, tb=tb, cc=cc, hqb=hqb: e.dma_start(
                    hqb[:, cc * 512:(cc + 1) * 512],
                    hsil_d[cc][:, tb * 512:(tb + 1) * 512]),
                    ((tuple(ev["ffnA_y"]) + tuple(ev["xland"])) if tb == 0
                     else ()) +
                    ((("T", tmm.get(tb - 2, 0)),) if tb >= 2 and cc <= 1 else ()),
                    inc=True, tag=q_)
            hq_full = (("yhq%d" % (tb % 2), em.now("yhq%d" % (tb % 2))),
                       ("ghq%d" % (tb % 2), em.now("ghq%d" % (tb % 2))))
            qslots = [SCR.ap()[:, 0:512], WS.ap()[:, 1024:1536],
                      WS.ap()[:, 1536:2048], ROWS.ap()[:, 1024:1536]]
            qs1 = {}
            qpend = []

            def hq_s2():
                cc_, sl = qpend.pop(0)
                E("V", lambda e, cc_=cc_, sl=sl, hqb=hqb: e.tensor_scalar(
                    hqb[:, cc_ * 512:(cc_ + 1) * 512], sl, MAGIC,
                    MAGIC, ALU.add, ALU.subtract), (("V", qs1[cc_]),), inc=True)
                vhq[(tb, cc_)] = em.now("V")
            for cc in range(32):
                sl = qslots[cc % 4]
                wq_ = ()
                if cc == 0:
                    wq_ = hq_full + (("V", ev["mid_v"]), ("V", vw2))
                    if tb >= 1:
                        wq_ += (("V", vhq[(tb - 1, 31)]),)
                elif cc >= 4:
                    wq_ = (("V", vhq.get((tb, cc - 4), 0)),)
                E("V", lambda e, cc=cc, sl=sl, hqb=hqb, tb=tb: e.tensor_tensor(
                    sl, hqb[:, cc * 512:(cc + 1) * 512],
                    PDf[:, 2048 + tb * 512: 2048 + (tb + 1) * 512], ALU.mult),
                  wq_, inc=True)
                qs1[cc] = em.now("V")
                qpend.append((cc, sl))
                if len(qpend) >= 4:
                    hq_s2()
            while qpend:
                hq_s2()
            v_evict = {}
            for Mt in range(8):
                yxr = "yxr%d" % (Mt % 2)
                yx = E("Y", lambda e, Mt=Mt, tb=tb: e.dma_start(
                    WS.ap()[:, (Mt % 2) * 512:(Mt % 2) * 512 + 512],
                    xsrc[Mt * 128:(Mt + 1) * 128, tb * 512:(tb + 1) * 512]),
                    (("V", v_evict[Mt - 2]),) if Mt >= 2 else
                    ((("V", vw2),) + tuple(ev["xland"]) if tb == 0 and Mt == 0 else
                     ((("V", vev_prev),) if Mt == 0 else ())), inc=True, tag=yxr)
                for cc in range(32):
                    lhs = (PO.ap()[:, cc * 1024 + Mt * 128: cc * 1024 + (Mt + 1) * 128]
                           if cc < 16 else
                           PA.ap()[:, (cc - 16) * 1024 + Mt * 128:
                                   (cc - 16) * 1024 + (Mt + 1) * 128])
                    w = []
                    if Mt == 0:
                        w = [("V", vhq[(tb, cc)]), ("V", vqw[cc])]
                    elif cc == 0:
                        w = [("V", vhq[(tb, 31)]), ("V", vw2)]
                    if cc == 0 and Mt >= 4:
                        w.append(("V", v_evict[Mt - 4]))
                    E("T", lambda e, cc=cc, Mt=Mt, lhs=lhs, hqb=hqb: e.matmul(
                        ps0[:, (Mt % 4) * 512:(Mt % 4) * 512 + 512], lhs,
                        hqb[:, cc * 512:(cc + 1) * 512],
                        start=(cc == 0), stop=(cc == 31)), tuple(w), inc=(cc == 31))
                ttm = em.now("T")
                E("V", lambda e, Mt=Mt, tb=tb: e.scalar_tensor_tensor(
                    SCR.ap()[:, 512:1024], ps0[:, (Mt % 4) * 512:(Mt % 4) * 512 + 512],
                    1.0, PDf[:, tb * 512:(tb + 1) * 512], ALU.mult, ALU.mult),
                  (("T", ttm), ("V", em.now("V"))), inc=True)
                E("V", lambda e, Mt=Mt: e.tensor_tensor(
                    tmp[:, Mt * 512:(Mt + 1) * 512], SCR.ap()[:, 512:1024],
                    WS.ap()[:, (Mt % 2) * 512:(Mt % 2) * 512 + 512], ALU.add),
                  ((yxr, yx), ("V", em.now("V"))), inc=True)
                v_evict[Mt] = em.now("V")
            tmm[tb] = em.now("T")
            for Mt in range(8):
                tmps = tmp[:, Mt * 512:(Mt + 1) * 512]
                E("V", lambda e, tmps=tmps: e.tensor_copy(CPB, tmps),
                  (("T", em.now("T")),))
                E("V", lambda e, tmps=tmps: e.tensor_tensor(SQB, tmps, tmps,
                                                            ALU.mult), inc=True)
                vst = em.now("V")
                E("T", lambda e, Mt=Mt: e.matmul(
                    ps1[0:1, 0:512], ONESB, CPB, start=(Mt == 0), stop=(Mt == 7)),
                  (("V", vst), ("S", ev["ffnA_s"])))
                E("T", lambda e, Mt=Mt: e.matmul(
                    ps1[0:1, 512:1024], ONESB, SQB, start=(Mt == 0), stop=(Mt == 7)),
                  inc=True)
            tst = em.now("T")
            rmu = ROWS.ap()[0:1, 0:512]
            rsd = ROWS.ap()[0:1, 512:1024]
            vmu1 = E("V", lambda e, rmu=rmu: e.tensor_scalar(
                rmu, ps1[0:1, 0:512], 1.0 / 1024, None, ALU.mult), (("T", tst),),
                inc=True)
            E("V", lambda e, rmu=rmu, rsd=rsd: e.tensor_tensor(rsd, rmu, rmu,
                                                               ALU.mult),
              (("V", vmu1),), inc=True)
            E("V", lambda e, rsd=rsd: e.scalar_tensor_tensor(
                rsd, ps1[0:1, 512:1024], 1.0 / 1024, rsd, ALU.mult, ALU.subtract),
              (("V", em.now("V")),), inc=True)
            vvar = em.now("V")
            sl = E("S", lambda e, rsd=rsd: e.activation(rsd, rsd, AF.Ln, bias=EPSR),
                   (("V", vvar),), inc=True)
            sl = E("S", lambda e, rsd=rsd: e.activation(rsd, rsd, AF.Exp, scale=-0.5),
                   (("S", sl),), inc=True)
            t1 = E("T", lambda e, rmu=rmu: e.matmul(ps1[:, 1024:1536], ONERF, rmu,
                                                    start=True, stop=True),
                   (("S", sl),))
            t2 = E("T", lambda e, rsd=rsd: e.matmul(ps1[:, 1536:2048], ONERF, rsd,
                                                    start=True, stop=True), inc=True)
            tbb = em.now("T")
            E("V", lambda e: e.tensor_copy(SCR.ap()[:, 1024:1536], ps1[:, 1024:1536]),
              (("T", tbb),))
            E("V", lambda e: e.tensor_copy(PDf[:, 4096:4608], ps1[:, 1536:2048]),
              inc=True)
            vbb = em.now("V")
            for Mt in range(8):
                tmps = tmp[:, Mt * 512:(Mt + 1) * 512]
                E("V", lambda e, tmps=tmps: e.tensor_tensor(
                    tmps, tmps, SCR.ap()[:, 1024:1536], ALU.subtract),
                  (("V", vbb), ("yo1", youts)) if Mt == 0 else (), inc=True)
                E("V", lambda e, tmps=tmps, Mt=Mt: e.tensor_tensor(
                    outb[:, Mt * 512:(Mt + 1) * 512], tmps, PDf[:, 4096:4608],
                    ALU.mult), (("V", em.now("V")),), inc=True)
            vln = em.now("V")
            vev_prev = vln
            youts = E("Y", lambda e, tb=tb: e.dma_start(
                xdst[:, tb * 512:(tb + 1) * 512].rearrange("(mt p) t -> p mt t",
                                                           p=128),
                outb.rearrange("p (mt t) -> p mt t", t=512)),
                (("V", vln),), inc=True, tag="yo1")
        return {"ffnB_y": youts, "ffnB_v": em.now("V"), "ffnB_t": em.now("T"),
                "ffnB_s": em.now("S"), "xland": (("yo1", youts),)}

    evs = {}
    evs.update(ph_init())
    for L in range(2):
        ev_pre = dict(evs)
        if L == 0:
            evs.update(ph_dbuild(evs))
            xsrc = dr["xT"]
        else:
            evs.update(ph_tbuild(evs, pre=(("T", evs["ffnB_t"]),
                                           ("V", evs["ffnB_v"]))))
            xsrc = xh[0]
        pa_free = () if L == 0 else (("T", evs["ffnB_t"]),) + tuple(evs["xland"])
        if L == 0:
            dump("init_smf", SMF.ap()[:, 0:768], None)
            dump("init_smb", SMB.ap()[:, 0:1024], None)
            dump("dbuild_pd", PD.ap()[:, 0:2056], PD.ap()[:, 0:1025])
            dump("dbuild_stk", SMF.ap()[:, 474:490], SMF.ap()[0:1, 474:490])
            dump("dbuild_norc", SMF.ap()[:, 490:499])
            dump("dbuild_frep", SMF.ap()[:, 506:507])
        evs.update(ph_prep(L, xsrc, evs, pa_free=pa_free,
                           ps_free=tuple(evs.get("ps_free", ()))))
        if L == 0:
            dump("prep_rows", ROWS.ap()[:, 0:2048], ROWS.ap()[0:1, 0:2048])
            dump("prep_pxs", PX.ap()[0:32, 0:6144], PX.ap()[0:1, 0:6144])
            dump("prep_pa", PA.ap()[:, 0:2048])
        pbf = () if L == 0 else (("yo1", evs["ffnB_y"]),)
        evs.update(ph_qkv(L, evs, po_free=tuple(evs.get("po_free", ())) + pbf,
                          pxb_free=pbf, ps_free=tuple(evs.get("ps_free", ()))))
        if L == 0:
            dump("qkv_px", PX.ap()[:, 0:1024])
            dump("qkv_pxk", PX.ap()[:, 8192:9216])
            dump("qkv_pb", PB.ap()[:, 0:260])
        evs.update(ph_bounds(L, evs))
        if L == 0:
            dump("bounds_rows", ROWS.ap()[:, 0:2048], ROWS.ap()[8:24, 0:1024])
        wgen = ph_wmean_gen(L, evs) if stage > 2 * L + 1 else None
        evs.update(ph_attncore(L, evs, bg=wgen))
        if L == 0:
            dump("attn_po", PO.ap()[:, 0:2048])
            dump("attn_pa", PAf[:, 0:2048])
            if DBGHW:
                E("V", lambda e: e.tensor_copy(WS.ap()[:, 0:2048],
                                               ps0.ap()[:, 0:2048]),
                  tuple((k, v) for k, v in em.cnt.items() if v > 0), inc=True)
                dump("attn_ps0", WS.ap()[:, 0:2048])
                ps1v = (ps1.ap()[0:65, 0:1024] if DBG_SIM
                        else ps1.ap()[:, 0:2048])
                wsv = (WS.ap()[0:65, 0:1024] if DBG_SIM
                       else WS.ap()[:, 0:2048])
                E("V", lambda e, ps1v=ps1v, wsv=wsv: e.tensor_copy(wsv, ps1v),
                  (("ydbg", em.now("ydbg")),), inc=True)
                dump("attn_ps1", WS.ap()[:, 0:2048], WS.ap()[0:65, 0:1024])
            dump("attn_scr", SCR.ap()[0:64, 0:1024], SCR.ap()[0:64, 0:512])
        last = (stage == 2 * L + 1)
        pr = ph_proj(L, evs, xsrc, yT if last else xh[1])
        evs.update(pr)
        if L == 0:
            dump("proj_pa", PAf[0:32, 0:2048])
        if last:
            break
        evs.update(ph_wmean_fin(L))
        evs.update(ph_quantx(L, evs))
        evs.update(ph_ffnA(L, evs))
        evs.update(ph_ffnmid(L, evs))
        last = (stage == 2 * L + 2)
        fb = ph_ffnB(L, evs, xh[1], yT if last else xh[0])
        evs.update(fb)
        if last:
            break

    dma_tags = sorted({t for (eng, _, _, _, t) in em.steps if eng in ("Y", "G")})
    em.emit("Y", lambda e: e.nop(),
            tuple((t, em.now(t)) for t in dma_tags), False, tag="ynop")

    DBG.clear()
    DBG['em'] = em
    DBG.update(dict(PX=PX.ap(), PXb=PXb, PA=PA.ap(), PB=PB.ap(), PO=PO.ap(),
                    PD=PD.ap(), SMF=SMF.ap(), SMB=SMB.ap(), SCR=SCR.ap(),
                    ROWS=ROWS.ap(), WS=WS.ap(), SM2=SM2.ap(), ps0=ps0.ap(), ps1=ps1.ap()))
    all_tags = sorted({t for (_, _, _, _, t) in em.steps} |
                      {we for (_, _, ws, _, _) in em.steps for (we, _) in ws})
    sems = {t: es.enter_context(nc.semaphore(f"sem{t}")) for t in all_tags}
    eng_map = {"T": "tensor", "V": "vector", "S": "scalar", "Y": "sync", "G": "gpsimd"}
    with nc.Block() as block:
        def mk(engkey):
            def body(e):
                hw = dict.fromkeys(all_tags, 0)
                for eng, fn, waits, n, t in em.steps:
                    if eng != engkey:
                        continue
                    for (we, wv) in waits:
                        if isinstance(wv, str):
                            raise RuntimeError("unresolved wait " + wv)
                        if wv > hw[we]:
                            e.wait_ge(sems[we], wv)
                            hw[we] = wv
                    inst = fn(e)
                    if n:
                        inst.then_inc(sems[t], n)
            return body
        for k, name in eng_map.items():
            getattr(block, name)(mk(k))

    es.close()
    return nc


def kernel(**inp):
    f32 = np.float32
    x = np.asarray(inp["L0_attn_x"], f32)
    hh0, hh1 = _make_hh()
    sc_ = np.asarray(inp["L0_attn_spatial_centers"], f32)
    csT = np.tile(sc_[0].reshape(2, 1), (1, 1152)).astype(f32)
    csT[:, 1:1025] = sc_.T
    shared = {
        "wqkvT0": np.ascontiguousarray(np.asarray(inp["L0_attn_W_qkv"], f32).T),
        "wprojT0": np.ascontiguousarray(np.asarray(inp["L0_attn_W_proj"], f32).T),
        "bproj0": _col(inp["L0_attn_b_proj"], 8),
        "gamma0": _col(inp["L0_attn_ln_gamma"], 8),
        "beta0": _col(inp["L0_attn_ln_beta"], 8),
        "csT": csT,
        "w1T0": np.ascontiguousarray(np.asarray(inp["L0_ffn_w1"], f32).T),
        "cw0": np.concatenate([_col(np.asarray(inp["L0_ffn_conv_w"], f32)[:, 0, t], 32)
                               for t in range(3)], axis=1),
        "cb0": _col(inp["L0_ffn_conv_b"], 32),
        "w2T0": np.ascontiguousarray(np.asarray(inp["L0_ffn_w2"], f32).T),
        "wqkvT1": np.ascontiguousarray(np.asarray(inp["L1_attn_W_attn"], f32).T),
        "battn1": _col(np.asarray(inp["L1_attn_b_attn"], f32)[0:2048], 16),
        "bv1": np.ascontiguousarray(np.asarray(inp["L1_attn_b_attn"], f32)[2048:3072]
                                    .reshape(1, 1024)),
        "wprojT1": np.ascontiguousarray(np.asarray(inp["L1_attn_W_proj"], f32).T),
        "bproj1": _col(inp["L1_attn_b_proj"], 8),
        "relemb": np.asarray(inp["L1_attn_rel_emb"], f32),
        "w1T1": np.ascontiguousarray(np.asarray(inp["L1_ffn_w1"], f32).T),
        "cw1": np.concatenate([_col(np.asarray(inp["L1_ffn_conv_w"], f32)[:, 0, t], 32)
                               for t in range(3)], axis=1),
        "cb1": _col(inp["L1_ffn_conv_b"], 32),
        "w2T1": np.ascontiguousarray(np.asarray(inp["L1_ffn_w2"], f32).T),
        "ident": np.eye(128, dtype=ml_dtypes.bfloat16),
        "identf": np.eye(128, dtype=np.float32),
        "hh0": hh0, "hh1": hh1,
    }
    nc = build(bass.Bass("TRN2", target_bir_lowering=False,
                         detect_race_conditions=RACECHECK), stage=STAGE)
    in_maps = []
    for b in range(8):
        m = dict(shared)
        m["xT"] = np.ascontiguousarray(x[b].T)
        in_maps.append(m)
    res = run_bass_kernel_spmd(nc, in_maps, core_ids=list(range(8)))
    return np.stack([np.asarray(r["yT"], f32).T for r in res.results], axis=0)


# version 08 (final)
# speedup vs baseline: 1.1201x; 1.0141x over previous
import math
import numpy as np
import ml_dtypes
import concourse.bass as bass
import concourse.mybir as mybir
from concourse.bass_utils import run_bass_kernel_spmd
from contextlib import ExitStack

F32 = mybir.dt.float32
BF16 = mybir.dt.bfloat16
AF = mybir.ActivationFunctionType
ALU = mybir.AluOpType
AX = mybir.AxisListType

MAGIC = float(2 ** 23 + 2 ** 22)
NEGBIG = -8.0e30
LN_EPS = 1e-5

STAGE = 4
DBG = {}
DBGHW = False
DBG_SIM = False
DBG_NO_M = False
DBG_NO_D = False
RACECHECK = True


def _bucket(d):
    if d < 16:
        return d
    v = 16 + int(np.float32(np.log(np.float32(d) / 16 + 1e-10) / math.log(8.0) * 16))
    return min(v, 31)


def _make_hh():
    hh0 = np.zeros((33, 128 * 128), dtype=np.float32)
    hh1 = np.zeros((33, 128 * 128), dtype=np.float32)
    for k in range(128):
        for q in range(128):
            i = k * 128 + q
            d0 = q - k
            if d0 >= 0:
                hh0[_bucket(d0), i] = 1.0
            else:
                hh0[32, i] = 1.0
            hh1[_bucket(d0 + 128), i] = 1.0
    return hh0.astype(ml_dtypes.bfloat16), hh1.astype(ml_dtypes.bfloat16)


def _col(v, n):
    return np.ascontiguousarray(np.asarray(v, np.float32).reshape(n, 128).T)


ENGS = ("T", "V", "S", "Y", "G")


class EM:

    def __init__(self):
        self.steps = []
        self.cnt = {}

    def emit(self, eng, fn, waits=(), inc=False, tag=None, dma=False):
        t = tag if tag is not None else eng
        if eng in ("Y", "G") or dma:
            assert t not in ENGS, "DMA steps need an explicit tag"
        n = (16 if (eng in ("Y", "G") or dma) else 1) if inc else 0
        self.cnt[t] = self.cnt.get(t, 0) + n
        self.steps.append((eng, fn, tuple(waits), n, t))
        return self.cnt[t]

    def now(self, tag):
        return self.cnt.get(tag, 0)


def build(nc, stage=4):
    D, T = 1024, 2048

    def din(name, shape, dt=F32):
        return nc.dram_tensor(name, list(shape), dt, kind="ExternalInput").ap()

    dr = {k: din(k, s, d) for k, s, d in [
        ("xT", (D, T), F32),
        ("wqkvT0", (D, 3 * D), F32), ("wprojT0", (D, D), F32),
        ("bproj0", (128, 8), F32), ("gamma0", (128, 8), F32), ("beta0", (128, 8), F32),
        ("csT", (2, 1152), F32),
        ("w1T0", (D, 4096), F32), ("cw0", (128, 96), F32), ("cb0", (128, 32), F32),
        ("w2T0", (4096, D), F32),
        ("wqkvT1", (D, 3 * D), F32), ("battn1", (128, 16), F32), ("bv1", (1, D), F32),
        ("wprojT1", (D, D), F32), ("bproj1", (128, 8), F32), ("relemb", (32, 16), F32),
        ("w1T1", (D, 4096), F32), ("cw1", (128, 96), F32), ("cb1", (128, 32), F32),
        ("w2T1", (4096, D), F32),
        ("ident", (128, 128), BF16), ("identf", (128, 128), F32),
        ("hh0", (33, 16384), BF16), ("hh1", (33, 16384), BF16),
    ]}

    yT = nc.dram_tensor("yT", [D, T], F32, kind="ExternalOutput").ap()
    xh = [nc.dram_tensor(f"xh{i}", [D, T], F32, kind="Internal").ap() for i in range(2)]
    hsil_d = nc.dram_tensor("hsil_d", [32, 128, T], BF16, kind="Internal").ap()
    w2q_d = nc.dram_tensor("w2q_d", [32, 128, D], BF16, kind="Internal").ap()
    tflat_d = nc.dram_tensor("tflat_d", [2, 8, 16, 2048], BF16, kind="Internal").ap()
    row_d = nc.dram_tensor("row_d", [8, T], F32, kind="Internal").ap()

    es = ExitStack()

    def sb(name, shape, dt):
        return es.enter_context(nc.sbuf_tensor(name, list(shape), dt))

    PX = sb("PX", (128, 16384), F32)
    PA = sb("PA", (128, 16384), BF16)
    PB = sb("PB", (128, 16640), BF16)
    PO = sb("PO", (128, 16384), BF16)
    PD = sb("PD", (128, 9252), BF16)
    WS = sb("WS", (128, 2048), F32)
    SCR = sb("SCR", (128, 2048), F32)
    SMB = sb("SMB", (128, 1024), BF16)
    SMF = sb("SMF", (128, 768), F32)
    SM2 = sb("SM2", (33, 64), F32)
    ROWS = sb("ROWS", (128, 2048), F32)

    ps0 = es.enter_context(nc.psum_tensor("ps0", [128, 2048], F32))
    ps1 = es.enter_context(nc.psum_tensor("ps1", [128, 2048], F32))

    PXb = PX.ap().bitcast(BF16)
    PBf = PB.ap().bitcast(F32)
    POf = PO.ap().bitcast(F32)
    PAf = PA.ap().bitcast(F32)
    PDf = PD.ap().bitcast(F32)
    WSb = WS.ap().bitcast(BF16)
    SCRb = SCR.ap().bitcast(BF16)
    FLT = SCRb[0:16, 0:2048]
    W1Q = PB.ap()[:, 4096:6144]

    IDENT = SMB.ap()[:, 0:128]
    ONESB = SMB.ap()[:, 128:129]
    ONER = SMB.ap()[0:1, 130:258]
    DIAG = [SMB.ap()[:, 258 + i * 128: 386 + i * 128] for i in range(3)]
    GAMB = SMB.ap()[:, 672:680]
    IDF = SMF.ap()[:, 640:768]

    ONERF = SMF.ap()[0:1, 0:128]
    BPB0 = SMF.ap()[:, 128:136]
    GAM0 = SMF.ap()[:, 136:144]
    NGAM0 = SMF.ap()[:, 144:152]
    BT1 = SMF.ap()[:, 152:168]
    BP1 = SMF.ap()[:, 176:184]
    CB = [SMF.ap()[:, 184 + i * 32: 216 + i * 32] for i in range(2)]
    CW = [SMF.ap()[:, 248 + i * 96: 344 + i * 96] for i in range(2)]
    NEGM = SMF.ap()[:, 440:456]
    AMC = SMF.ap()[:, 456:472]
    WSR = [SMF.ap()[:, 472 + i: 473 + i] for i in range(2)]
    STK = SMF.ap()[:, 474:490]
    NORC = SMF.ap()[:, 490:506]
    FREP = SMF.ap()[:, 506:507]
    BETA0 = SMF.ap()[:, 507:515]
    EPSR = SMF.ap()[0:1, 515:516]
    C30R = SMF.ap()[0:1, 516:517]
    ONESFC = SMF.ap()[:, 522:523]
    C1 = SMF.ap()[:, 520:521]
    C2 = SMF.ap()[:, 521:522]

    CST = POf[0:2, 4864:6016]
    CSN2 = POf[0:2, 1280:2432]
    CS2 = POf[0:2, 2560:3712]
    NORR = POf[0:1, 3712:4864]
    R0 = ROWS.ap()[0:1, :]
    R_RS0 = ROWS.ap()[32:33, :]
    R_M0R0 = ROWS.ap()[64:65, :]
    R96 = ROWS.ap()[96:97, :]

    em = EM()
    E = em.emit

    from concourse.hw_specs import get_activation_tables
    _set_ids = {name: i for i, name in enumerate(get_activation_tables(nc.m.arch))}

    def load_actset(name, waits=()):
        sid = _set_ids[name]
        return E("S", lambda e, sid=sid: e.add_instruction(
            mybir.InstLoadActFuncSet(
                name=nc.get_next_instruction_name(), act_func_set_id=sid,
                ins=[], outs=[])), waits, inc=False)

    def dump(name, ap, sim_ap="same"):
        if not DBGHW:
            return
        if DBG_SIM:
            if sim_ap is None:
                return
            if sim_ap != "same":
                ap = sim_ap
        d = nc.dram_tensor("dbg_" + name, [ap.shape[0], ap.shape[1]], ap.dtype,
                           kind="ExternalOutput").ap()
        w = tuple((k, v) for k, v in em.cnt.items() if v > 0)
        E("Y", lambda e, d=d, ap=ap: e.dma_start(d, ap), w, inc=True, tag="ydbg")

    def ph_init():
        for dst, src in [(IDENT, dr["ident"]), (IDF, dr["identf"]),
                         (BPB0, dr["bproj0"]), (GAM0, dr["gamma0"]),
                         (BETA0, dr["beta0"]), (BT1, dr["battn1"]),
                         (BP1, dr["bproj1"]), (CB[0], dr["cb0"]), (CB[1], dr["cb1"]),
                         (CW[0], dr["cw0"]), (CW[1], dr["cw1"]), (CST, dr["csT"]),
                         (SM2.ap()[0:32, 0:16], dr["relemb"])]:
            E("Y", lambda e, dst=dst, src=src: e.dma_start(dst, src), inc=True,
              tag="yinit")
        yw = em.now("yinit")
        E("V", lambda e: e.memset(ONESB, 1.0))
        E("V", lambda e: e.memset(ONER, 1.0))
        E("V", lambda e: e.memset(ONERF, 1.0))
        E("V", lambda e: e.memset(EPSR, LN_EPS))
        E("V", lambda e: e.memset(C30R, 1e-30))
        E("V", lambda e: e.memset(ONESFC, 1.0))
        E("V", lambda e: e.tensor_tensor(BPB0, BPB0, BETA0, ALU.add),
          (("yinit", yw),))
        E("V", lambda e: e.tensor_copy(GAMB, GAM0))
        E("V", lambda e: e.tensor_scalar(NGAM0, GAM0, -1.0, None, ALU.mult), inc=True)
        return {"init_v": em.now("V"), "init_y": yw}

    def ph_dbuild(ev):
        E("V", lambda e: e.tensor_scalar(CSN2, CST, -2.0, None, ALU.mult),
          (("yinit", ev["init_y"]),))
        vcs = E("V", lambda e: e.tensor_tensor(CS2, CST, CST, ALU.mult), inc=True)
        for j, w_ in ((0, 512), (512, 512), (1024, 128)):
            E("T", lambda e, j=j, w_=w_: e.matmul(
                ps1[0:1, j: j + w_], ONESFC[0:2, :], CS2[:, j:j + w_],
                start=True, stop=True), (("V", vcs),) if j == 0 else (),
              inc=(j == 1024))
        tno = em.now("T")
        E("V", lambda e: e.tensor_copy(NORR, ps1[0:1, 0:1152]), (("T", tno),),
          inc=True)
        vnorm = em.now("V")
        ynorr = E("Y", lambda e: e.dma_start(row_d[0:1, 0:1152], NORR),
                  (("V", vnorm),), inc=True, tag="yrow")
        ynorc = E("Y", lambda e: e.dma_start(
            NORC[:, 0:9],
            row_d[0:1, 0:1152].rearrange("o (c p) -> (o p) c", p=128),
            allow_slow_non_contiguous=True), (("yrow", ynorr),), inc=True,
            tag="yrow")
        E("V", lambda e: e.memset(STK[:, 0:9], -3.0e38), inc=True)
        vstk = em.now("V")
        E("S", lambda e: e.preload_act_table(AF.Sqrt))
        vprev = 0
        for kc in range(9):
            qn = [(0, 512), (512, 512), (1024, 1)]
            for qi, (q0, qw) in enumerate(qn):
                w = []
                if qi == 0:
                    w = [("V", vnorm)] if kc == 0 else [("V", vprev)]
                E("T", lambda e, kc=kc, q0=q0, qw=qw, qi=qi: e.matmul(
                    ps0[:, q0:q0 + qw], CST[:, kc * 128:(kc + 1) * 128],
                    CSN2[:, q0:q0 + qw], start=(True if qi == 0 else True), stop=False),
                  tuple(w))
                E("T", lambda e, q0=q0, qw=qw, qi=qi: e.matmul(
                    ps0[:, q0:q0 + qw], ONERF, NORR[:, q0:q0 + qw],
                    start=False, stop=True), inc=(qi == len(qn) - 1))
            tmm = em.now("T")
            E("V", lambda e, kc=kc: e.tensor_scalar(
                POf[:, 0:1025], ps0[:, 0:1025], NORC[:, kc:kc + 1], 1e-30,
                ALU.add, ALU.max),
              (("T", tmm), ("yrow", ynorc), ("S", em.now("S"))), inc=True)
            vd2 = em.now("V")
            E("V", lambda e, kc=kc: e.tensor_reduce(
                STK[:, kc:kc + 1], POf[:, 0:1025], axis=AX.X, op=ALU.max),
              (("V", vstk), ("V", vd2)), inc=True)
            vprev = em.now("V")
            E("S", lambda e, kc=kc: e.activation(
                PD.ap()[:, kc * 1028: kc * 1028 + 1025], POf[:, 0:1025], AF.Sqrt),
              (("V", vd2),), inc=True)
        sqd = em.now("S")
        vred = E("V", lambda e: e.tensor_reduce(
            STK[:, 15:16], STK[:, 0:9], axis=AX.X, op=ALU.max),
            (("V", em.now("V")),), inc=True)
        tmx = E("T", lambda e: e.transpose(ps1[0:1, 0:128], STK[:, 15:16], IDF),
                (("V", vred),), inc=True)
        E("V", lambda e: e.tensor_reduce(STK[0:1, 14:15], ps1[0:1, 0:128], axis=AX.X,
                                         op=ALU.max), (("T", tmx),), inc=True)
        smx = E("S", lambda e: e.activation(STK[0:1, 13:14], STK[0:1, 14:15], AF.Sqrt),
                (("V", em.now("V")),), inc=True)
        E("V", lambda e: e.tensor_scalar(STK[0:1, 12:13], STK[0:1, 13:14], 1e-6, None,
                                         ALU.max), (("S", smx),), inc=True)
        E("V", lambda e: e.reciprocal(STK[0:1, 11:12], STK[0:1, 12:13]),
          (("V", em.now("V")),), inc=True)
        E("V", lambda e: e.tensor_scalar(STK[0:1, 10:11], STK[0:1, 13:14], 1e-6, None,
                                         ALU.is_gt), inc=True)
        E("V", lambda e: e.tensor_tensor(STK[0:1, 9:10], STK[0:1, 11:12],
                                         STK[0:1, 10:11], ALU.mult),
          (("V", em.now("V")),), inc=True)
        vf = E("V", lambda e: e.tensor_scalar(STK[0:1, 9:10], STK[0:1, 9:10], -0.8,
                                              None, ALU.mult),
               (("V", em.now("V")),), inc=True)
        tfr = E("T", lambda e: e.matmul(ps1[:, 128:129], ONERF, STK[0:1, 9:10],
                                        start=True, stop=True), (("V", vf),), inc=True)
        vfr = E("V", lambda e: e.tensor_copy(FREP, ps1[:, 128:129]), (("T", tfr),),
                inc=True)
        for kc in range(9):
            E("V", lambda e, kc=kc: e.tensor_scalar(
                PD.ap()[:, kc * 1028: kc * 1028 + 1025],
                PD.ap()[:, kc * 1028: kc * 1028 + 1025], FREP, None, ALU.mult),
              (("S", sqd), ("V", vfr)), inc=(kc == 8))
        E("V", lambda e: e.tensor_copy(SCRb[0:1, 0:1025],
                                       PD.ap()[0:1, 8 * 1028: 8 * 1028 + 1025]),
          (("V", em.now("V")),), inc=True)
        E("V", lambda e: e.memset(PD.ap()[:, 8 * 1028: 8 * 1028 + 1025], 0.0),
          (("V", em.now("V")),), inc=True)
        E("V", lambda e: e.tensor_copy(PD.ap()[0:1, 8 * 1028: 8 * 1028 + 1025],
                                       SCRb[0:1, 0:1025]),
          (("V", em.now("V")),), inc=True)
        for kc in range(9):
            E("V", lambda e, kc=kc: e.memset(PD.ap()[:, kc * 1028: kc * 1028 + 1], 0.0),
              (("V", em.now("V")),) if kc == 0 else (), inc=(kc == 8))
        E("V", lambda e: e.memset(PD.ap()[0:1, 0:1025], 0.0),
          (("V", em.now("V")),), inc=True)
        return {"bias_ready": [("V", em.now("V"))],
                "ps_free": [("V", em.now("V")), ("S", em.now("S"))],
                "po_free": [("V", em.now("V")), ("S", em.now("S")),
                            ("T", em.now("T"))]}

    def ph_tbuild(ev, pre=()):
        relb = SM2.ap().bitcast(BF16)
        y31 = E("Y", lambda e: e.dma_start(POf[0:1, 3072:3088],
                                           SM2.ap()[31:32, 0:16]),
                tuple(pre) + (("yinit", ev["init_y"]), ("V", ev["init_v"])),
                inc=True, tag="y31")
        t31 = E("T", lambda e: e.matmul(ps1[0:32, 0:16], ONERF[0:1, 0:32],
                                        POf[0:1, 3072:3088], start=True,
                                        stop=True), (("y31", y31),), inc=True)
        E("V", lambda e: e.tensor_tensor(SM2.ap()[0:32, 16:32], SM2.ap()[0:32, 0:16],
                                         ps1[0:32, 0:16], ALU.subtract), (("T", t31),),
          inc=True)
        E("V", lambda e: e.memset(SMB.ap()[32:64, 680:696], 0.0))
        E("V", lambda e: e.tensor_scalar(SMB.ap()[0:32, 680:696],
                                         SM2.ap()[0:32, 16:32],
                                         8.0, None, ALU.mult),
          (("V", em.now("V")),), inc=True)
        E("V", lambda e: e.memset(SMB.ap()[32:33, 680:696], NEGBIG),
          (("V", em.now("V")),), inc=True)
        vrel = em.now("V")
        hhsrc = [dr["hh0"], dr["hh1"]]
        E("V", lambda e: e.memset(PO.ap()[32:64, 0:4096], 0.0), tuple(pre), inc=True)
        vz = em.now("V")
        prev_t = [0, 0]
        sprev = 0
        yfprev = 0
        for t in range(2):
            for g in range(8):
                hb = PO.ap()[0:33, (g % 2) * 2048:(g % 2) * 2048 + 2048]
                hb64 = PO.ap()[0:64, (g % 2) * 2048:(g % 2) * 2048 + 2048]
                ghh = "ghh%d" % (g % 2)
                gw = E("G", lambda e, t=t, g=g, hb=hb: e.dma_start(
                    hb, hhsrc[t][:, g * 2048:(g + 1) * 2048]),
                    tuple(pre) + (("T", prev_t[g % 2]), ("V", vz)), inc=True,
                    tag=ghh)
                for c in range(4):
                    E("T", lambda e, hb=hb, hb64=hb64, c=c: e.matmul(
                        ps0[0:16, c * 512:(c + 1) * 512], SMB.ap()[0:64, 680:696],
                        hb64[:, c * 512:(c + 1) * 512], start=True, stop=True),
                      ((ghh, gw), ("V", vrel), ("S", sprev)) if c == 0 else (),
                      inc=(c == 3))
                tm = em.now("T")
                prev_t[g % 2] = tm
                sf = E("S", lambda e: e.copy(FLT, ps0[0:16, 0:2048]),
                       (("T", tm), ("yflt", yfprev)), inc=True)
                sprev = sf
                yf = E("Y", lambda e, t=t, g=g: e.dma_start(tflat_d[t, g], FLT),
                       (("S", sf),), inc=True, tag="yflt")
                yfprev = yf
                E("Y", lambda e, t=t, g=g: e.dma_start(
                    PD.ap()[g * 16:(g + 1) * 16, t * 4096: t * 4096 + 2048]
                    .rearrange("kp (h j) -> kp h j", j=128),
                    tflat_d[t, g].rearrange("h (kp j) -> kp h j", j=128)),
                  (("yflt", yf),), inc=True, tag="ypd")
        return {"bias_ready": [("ypd", em.now("ypd"))],
                "ps_free": [("V", em.now("V")), ("S", em.now("S"))],
                "po_free": [("T", em.now("T")), ("S", em.now("S"))]}

    def ph_prep(L, xsrc, ev, pa_free=(), ps_free=()):
        for c in range(4):
            E("G", lambda e, c=c: e.dma_start(
                PA.ap()[:, c * 4096:(c + 1) * 4096].rearrange("p (kc t) -> p kc t", t=2048),
                xsrc.rearrange("(kc p) t -> p kc t", p=128)[:, 2 * c: 2 * c + 2, :]),
              tuple(pa_free), inc=True, tag="gxb")
        gxb = em.now("gxb")
        out = {"xb_g": gxb}
        if L == 0:
            for tb in range(4):
                for kc in range(8):
                    E("T", lambda e, tb=tb, kc=kc: e.matmul(
                        ps1[0:1, tb * 512:(tb + 1) * 512], ONESB,
                        PA.ap()[:, kc * 2048 + tb * 512: kc * 2048 + (tb + 1) * 512],
                        start=(kc == 0), stop=(kc == 7)),
                      (("gxb", gxb), ("V", ev["init_v"])) + tuple(ps_free)
                      if (kc == 0 and tb == 0) else (),
                      inc=(kc == 7))
            tsum = em.now("T")
            E("V", lambda e: e.tensor_scalar(R0, ps1[0:1, 0:2048], 1.0 / 1024, None,
                                             ALU.mult), (("T", tsum),), inc=True)
            vmu = em.now("V")
            sq = SCRb[:, 0:2048]
            for kc in range(8):
                E("V", lambda e, kc=kc: e.tensor_tensor(
                    sq, PA.ap()[:, kc * 2048:(kc + 1) * 2048],
                    PA.ap()[:, kc * 2048:(kc + 1) * 2048], ALU.mult),
                  (("T", em.now("T")),), inc=True)
                vsq = em.now("V")
                for tb in range(4):
                    E("T", lambda e, tb=tb, kc=kc: e.matmul(
                        ps1[0:1, tb * 512:(tb + 1) * 512], ONESB,
                        sq[:, tb * 512:(tb + 1) * 512],
                        start=(kc == 0), stop=(kc == 7)),
                      (("V", vsq), ("V", vmu)) if tb == 0 else (),
                      inc=(tb == 3))
            tsq = em.now("T")
            T0 = PX.ap()[0:1, 0:2048]
            T1 = PX.ap()[0:1, 2048:4096]
            T2 = PX.ap()[0:1, 4096:6144]
            vt0 = E("V", lambda e: e.tensor_tensor(T0, R0, R0, ALU.mult),
                    (("T", tsq),), inc=True)
            E("V", lambda e: e.scalar_tensor_tensor(T1, ps1[0:1, 0:2048], 1.0 / 1024,
                                                    T0, ALU.mult, ALU.subtract),
              (("V", vt0),), inc=True)
            vvar = em.now("V")
            load_actset("natural_log_exp_and_others")
            s1 = E("S", lambda e: e.activation(T2, T1, AF.Ln, bias=EPSR),
                   (("V", vvar),), inc=True)
            s2 = E("S", lambda e: e.activation(T1, T2, AF.Exp, scale=-0.5),
                   (("S", s1),), inc=True)
            E("V", lambda e: e.tensor_copy(R_RS0, T1), (("S", s2),))
            vt0 = E("V", lambda e: e.tensor_tensor(T0, R0, T1, ALU.mult), inc=True)
            E("V", lambda e: e.tensor_scalar(R_M0R0, T0, -1.0, None, ALU.mult),
              (("V", vt0),), inc=True)
            out["rows_v"] = em.now("V")
        return out

    def ph_qkv(L, ev, pxb_free=(), po_free=(), ps_free=()):
        wq = dr["wqkvT0"] if L == 0 else dr["wqkvT1"]
        gxb = ev["xb_g"]
        t_after_fmt = {}
        last_evict_v = em.now("V")
        for fmt in range(16):
            buf = WSb[:, (fmt % 2) * 1024:(fmt % 2) * 1024 + 1024]
            gwq = "gwq%d" % (fmt % 2)
            w = [("T", t_after_fmt[fmt - 2])] if fmt >= 2 else []
            gw = E("G", lambda e, fmt=fmt, buf=buf: e.dma_start(
                buf.rearrange("p (kc f) -> p kc f", f=128),
                wq[:, fmt * 128:(fmt + 1) * 128]
                .rearrange("(kc p) f -> p kc f", p=128)), tuple(w), inc=True,
                tag=gwq)
            for kc in range(8):
                for qb in range(4):
                    w = ()
                    if kc == 0 and qb == 0:
                        w = ((gwq, gw), ("gxb", gxb), ("V", last_evict_v)) + \
                            tuple((tuple(pxb_free) + tuple(ps_free)) if fmt == 0
                                  else ())
                    E("T", lambda e, kc=kc, qb=qb, buf=buf: e.matmul(
                        ps0[:, qb * 512:(qb + 1) * 512], buf[:, kc * 128:(kc + 1) * 128],
                        PA.ap()[:, kc * 2048 + qb * 512: kc * 2048 + (qb + 1) * 512],
                        start=(kc == 0), stop=(kc == 7)), w,
                      inc=(kc == 7 and qb == 3))
            tmm = em.now("T")
            t_after_fmt[fmt] = tmm
            base = (fmt % 8) * 2048 + (0 if fmt < 8 else 16384)
            for qb in range(4):
                dst = PXb[:, base + qb * 512: base + (qb + 1) * 512]
                if L == 1:
                    E("V", lambda e, dst=dst, qb=qb, fmt=fmt: e.tensor_scalar(
                        dst, ps0[:, qb * 512:(qb + 1) * 512], BT1[:, fmt:fmt + 1],
                        None, ALU.add), (("T", tmm),) if qb == 0 else (), inc=(qb == 3))
                else:
                    E("V", lambda e, dst=dst, qb=qb: e.tensor_copy(
                        dst, ps0[:, qb * 512:(qb + 1) * 512]),
                      (("T", tmm),) if qb == 0 else (), inc=(qb == 3))
            last_evict_v = em.now("V")
        qk_v = last_evict_v
        gv = E("G", lambda e: e.dma_start(
            PO.ap()[:, 0:8192].rearrange("p (kc f) -> p kc f", f=1024),
            wq[:, 2048:3072].rearrange("(kc p) f -> p kc f", p=128)),
            tuple(po_free), inc=True, tag="gwv")
        vbias_v = 0
        if L == 1:
            ybv = E("Y", lambda e: e.dma_start(SCR.ap()[0:1, 1024:2048], dr["bv1"]),
                    inc=True, tag="ybv")
            E("T", lambda e: e.matmul(ps1[:, 0:512], ONERF, SCR.ap()[0:1, 1024:1536],
                                      start=True, stop=True), (("ybv", ybv),))
            tb2 = E("T", lambda e: e.matmul(ps1[:, 512:1024], ONERF,
                                            SCR.ap()[0:1, 1536:2048],
                                            start=True, stop=True), inc=True)
            vbias_v = E("V", lambda e: e.tensor_copy(PDf[:, 3072:4096],
                                                     ps1[:, 0:1024]),
                        (("T", tb2),), inc=True)
        ev_t = {}
        for tt in range(16):
            for fc in range(2):
                bank = (tt % 2) * 2 + fc
                w0 = [("gwv", gv)]
                if tt >= 2:
                    w0.append(("V", ev_t[(tt - 2, fc)]))
                if L == 1:
                    w0.append(("V", vbias_v))
                for kc in range(8):
                    E("T", lambda e, tt=tt, fc=fc, kc=kc, bank=bank: e.matmul(
                        ps1[:, bank * 512:(bank + 1) * 512],
                        PA.ap()[:, kc * 2048 + tt * 128: kc * 2048 + (tt + 1) * 128],
                        PO.ap()[:, kc * 1024 + fc * 512: kc * 1024 + (fc + 1) * 512],
                        start=(kc == 0), stop=(kc == 7)),
                      tuple(w0) if kc == 0 else (), inc=(kc == 7))
                tvm = em.now("T")
                base = tt * 1040 + fc * 8 * 65
                dst = PB.ap()[:, base: base + 520].rearrange("p (h e) -> p h e", e=65)[:, :, 0:64]
                src = ps1[:, bank * 512:(bank + 1) * 512].rearrange(
                    "p (h e) -> p h e", e=64)
                if L == 1:
                    bsl = PDf[:, 3072 + fc * 512: 3072 + (fc + 1) * 512] \
                        .rearrange("p (h e) -> p h e", e=64)
                    E("V", lambda e, dst=dst, src=src, bsl=bsl: e.tensor_tensor(
                        dst, src, bsl, ALU.add), (("T", tvm),), inc=True)
                else:
                    E("V", lambda e, dst=dst, src=src: e.tensor_copy(dst, src),
                      (("T", tvm),), inc=True)
                ev_t[(tt, fc)] = em.now("V")
            E("V", lambda e, tt=tt: e.memset(
                PB.ap()[:, tt * 1040: tt * 1040 + 1040].rearrange(
                    "p (h e) -> p h e", e=65)[:, :, 64:65], 1.0), inc=True)
        return {"qk_v": qk_v, "v_v": em.now("V"), "qkv_t": em.now("T")}

    ROWSB = ROWS.ap().bitcast(BF16)

    def ph_bounds(L, ev):
        qkv = ev["qk_v"]
        SA = PAf[0:1, 0:2048]
        SB = PAf[0:1, 2048:4096]
        SC = PAf[0:1, 4096:6144]
        KP = PAf[0:1, 6144:6152]
        ydrain = (("yms20", em.now("yms20")), ("yms21", em.now("yms21")))
        yms = []
        for hp in range(8):
            vsq = E("V", lambda e, hp=hp: e.tensor_tensor(
                SCRb[:, 0:2048], PXb[:, hp * 2048:(hp + 1) * 2048],
                PXb[:, hp * 2048:(hp + 1) * 2048], ALU.mult),
                ((("V", qkv),) + tuple(ev.get("bias_ready", ())) if hp == 0
                 else (("T", em.now("T")),)))
            E("V", lambda e, hp=hp: e.tensor_tensor(
                SCRb[:, 2048:4096], PXb[:, 16384 + hp * 2048: 16384 + (hp + 1) * 2048],
                PXb[:, 16384 + hp * 2048: 16384 + (hp + 1) * 2048], ALU.mult),
              inc=True)
            vsq = em.now("V")
            for half in range(2):
                h = hp * 2 + half
                p0 = half * 64
                for tb in range(4):
                    E("T", lambda e, p0=p0, tb=tb: e.matmul(
                        ps1[0:1, tb * 512:(tb + 1) * 512], ONESB[p0:p0 + 64, :],
                        SCRb[p0:p0 + 64, tb * 512:(tb + 1) * 512], start=True,
                        stop=True, tile_position=(p0, 0)),
                      (("V", vsq), ("V", em.now("V"))) if tb == 0 else ())
                    E("T", lambda e, p0=p0, tb=tb: e.matmul(
                        ps0[0:1, tb * 512:(tb + 1) * 512], ONESB[p0:p0 + 64, :],
                        SCRb[p0:p0 + 64, 2048 + tb * 512: 2048 + (tb + 1) * 512],
                        start=True, stop=True, tile_position=(p0, 0)),
                      inc=(tb == 3))
                tq = em.now("T")
                E("V", lambda e: e.tensor_copy(SA, ps1[0:1, 0:2048]),
                  (("T", tq), ("yms", yms[-1] if yms else 0)))
                vcp = E("V", lambda e: e.tensor_copy(SB, ps0[0:1, 0:2048]), inc=True)
                for qb in range(4):
                    E("V", lambda e, qb=qb: e.tensor_reduce(
                        KP[:, qb:qb + 1], SB[:, qb * 512:(qb + 1) * 512], axis=AX.X,
                        op=ALU.max), (("V", vcp),) if qb == 0 else (),
                      inc=(qb == 3))
                vkp = em.now("V")
                if L == 1:
                    for qb in range(1, 4):
                        vkp = E("V", lambda e, qb=qb: e.tensor_tensor(
                            KP[:, qb:qb + 1], KP[:, qb:qb + 1], KP[:, qb - 1:qb],
                            ALU.max), (("V", vkp),), inc=True)
                else:
                    for qb in range(3):
                        vkp = E("V", lambda e, qb=qb: e.tensor_tensor(
                            KP[:, 3:4], KP[:, 3:4], KP[:, qb:qb + 1], ALU.max),
                            (("V", vkp),), inc=True)
                    for qb in range(3):
                        E("V", lambda e, qb=qb: e.tensor_scalar(
                            KP[:, qb:qb + 1], KP[:, 3:4], 1.0, None, ALU.mult),
                          (("V", vkp),) if qb == 0 else (), inc=(qb == 2))
                    vkp = em.now("V")
                for qb in range(4):
                    E("V", lambda e, qb=qb: e.tensor_scalar(
                        SC[:, qb * 512:(qb + 1) * 512], SA[:, qb * 512:(qb + 1) * 512],
                        KP[:, qb:qb + 1], None, ALU.mult),
                      (("V", vkp),) if qb == 0 else (), inc=(qb == 3))
                vpr = em.now("V")
                sl = E("S", lambda e: e.activation(SC, SC, AF.Ln, bias=C30R),
                       (("V", vpr),), inc=True)
                sl = E("S", lambda e: e.activation(SC, SC, AF.Exp, scale=0.5),
                       (("S", sl),), inc=True)
                vm2 = E("V", lambda e: e.tensor_scalar(
                    PA.ap()[0:1, 12352:14400], SC, -1.0, -4.0, ALU.mult, ALU.add),
                    (("S", sl),), inc=True)
                yms.append(E("Y", lambda e, h=h: e.dma_start(
                    ROWSB[8 + h: 9 + h, 0:2048], PA.ap()[0:1, 12352:14400]),
                    (("V", vm2),) + ydrain, inc=True, tag="yms"))
        return {"mrows_y": em.now("yms"), "mrows_v": em.now("V")}

    def ph_attncore(L, ev, bg=None):
        load_actset("natural_log_exp_and_others")
        prev_exp = em.now("S")
        prev_ev_v = em.now("V")
        t_entry = em.now("T")
        mT_hist = [t_entry, t_entry]
        first_all = True
        MSAs = [SCRb[0:1, 2048:2560], SCRb[0:1, 2560:3072]]
        MSBs = [SCRb[0:1, 3072:3584], SCRb[0:1, 3584:4096]]
        for hp in range(8):
            for qb in range(4):
                it = hp * 4 + qb
                MSA = MSAs[it % 2]
                MSB = MSBs[it % 2]
                kc_hi = 16 if L == 0 else 4 * qb + 4
                ymt = "yms2%d" % (it % 2)
                yms = E("Y", lambda e, hp=hp, qb=qb, MSA=MSA: e.dma_start(
                    MSA, ROWSB[8 + 2 * hp: 9 + 2 * hp, qb * 512:(qb + 1) * 512]),
                    (("yms", ev["mrows_y"]), ("T", mT_hist[it % 2])), inc=True,
                    tag=ymt)
                yms = E("Y", lambda e, hp=hp, qb=qb, MSB=MSB: e.dma_start(
                    MSB, ROWSB[9 + 2 * hp: 10 + 2 * hp, qb * 512:(qb + 1) * 512]),
                    inc=True, tag=ymt)
                exps = {}

                def mk_steps(kc):
                    steps = []
                    for half in range(2):
                        p0 = half * 64
                        qoff = max(0, kc * 128 - qb * 512) if L == 1 else 0
                        col = (kc % 2) * 1024 + half * 512
                        lhs = PXb[p0:p0 + 64, 16384 + hp * 2048 + kc * 128:
                                  16384 + hp * 2048 + (kc + 1) * 128]
                        rhs = PXb[p0:p0 + 64, hp * 2048 + qb * 512 + qoff:
                                  hp * 2048 + (qb + 1) * 512]
                        badds = []
                        if L == 0 and qb <= 1 and kc <= 8:
                            badds.append((col, 512,
                                          PD.ap()[:, kc * 1028 + qb * 512:
                                                  kc * 1028 + (qb + 1) * 512]))
                        elif L == 0 and qb == 2 and kc <= 8:
                            badds.append((col, 1,
                                          PD.ap()[:, kc * 1028 + 1024:
                                                  kc * 1028 + 1025]))
                        elif L == 1:
                            h = hp * 2 + half
                            j0 = kc - 4 * qb
                            if 0 <= j0 <= 3:
                                badds.append((col + j0 * 128, 128,
                                              PD.ap()[:, h * 128:(h + 1) * 128]))
                            if 0 <= j0 + 1 <= 3:
                                badds.append((col + (j0 + 1) * 128, 128,
                                              PD.ap()[:, 4096 + h * 128:
                                                      4096 + (h + 1) * 128]))
                            if j0 == -1:
                                badds.append((col, 128,
                                              PD.ap()[:, 4096 + h * 128:
                                                      4096 + (h + 1) * 128]))
                        steps.append((lambda e, col=col, lhs=lhs, rhs=rhs,
                                      qoff=qoff, p0=p0: e.matmul(
                            ps0[:, col + qoff: col + 512], lhs, rhs,
                            start=True, stop=False, tile_position=(p0, 0)),
                            False))
                        for (c_, w_, tsl) in badds:
                            steps.append((lambda e, c_=c_, w_=w_, tsl=tsl: e.matmul(
                                ps0[:, c_: c_ + w_], IDENT, tsl,
                                start=False, stop=False), False))
                        ms = MSA if half == 0 else MSB
                        steps.append((lambda e, col=col, ms=ms, qoff=qoff:
                                      e.matmul(
                            ps0[:, col + qoff: col + 512], ONER,
                            ms[:, qoff:512], start=False, stop=True), True))
                    return steps

                def emit_scores(kc):
                    steps = mk_steps(kc)
                    for i, (fn, is_m) in enumerate(steps):
                        w = []
                        if i == 0:
                            w = [("S", exps[kc - 2] if kc >= 2 else prev_exp)]
                            if first_all:
                                w += [("V", ev["mrows_v"]), ("V", ev["v_v"])] + \
                                    list(ev.get("bias_ready", []))
                        if is_m:
                            w.append((ymt, yms))
                        E("T", fn, tuple(w), inc=(i == len(steps) - 1))
                    return em.now("T")

                def emit_exp(kc):
                    tsc = em.now("T")
                    for half in range(2):
                        dst = PA.ap()[:, half * 8192 + kc * 512:
                                      half * 8192 + (kc + 1) * 512]
                        srcc = (kc % 2) * 1024 + half * 512
                        E("S", lambda e, dst=dst, srcc=srcc: e.activation(
                            dst, ps0[:, srcc: srcc + 512], AF.Exp, scale=0.125),
                          (("T", tsc),) if half == 0 else (), inc=(half == 1))
                    exps[kc] = em.now("S")

                def emit_av(kc):
                    for half in range(2):
                        h = hp * 2 + half
                        qoff = max(0, kc * 128 - qb * 512) if L == 1 else 0
                        vsl = PB.ap()[:, kc * 1040 + h * 65: kc * 1040 + h * 65 + 65]
                        psl = PA.ap()[:, half * 8192 + kc * 512 + qoff:
                                      half * 8192 + (kc + 1) * 512]
                        w = ()
                        if half == 0:
                            w = (("S", exps[kc]),)
                            if kc == 0:
                                w += (("V", prev_ev_v),)
                        E("T", lambda e, vsl=vsl, psl=psl, half=half, qoff=qoff,
                          st=(kc == 0), sp=(kc == kc_hi - 1): e.matmul(
                            ps1[0:65, half * 512 + qoff:(half + 1) * 512],
                            vsl, psl, start=st, stop=sp), w, inc=(half == 1))

                emit_scores(0)
                first_all = False
                emit_exp(0)
                for kc in range(1, kc_hi):
                    emit_scores(kc)
                    emit_av(kc - 1)
                    emit_exp(kc)
                emit_av(kc_hi - 1)
                prev_exp = exps[kc_hi - 1]
                tav = em.now("T")
                for half in range(2):
                    h = hp * 2 + half
                    vrec = E("V", lambda e, half=half: e.reciprocal(
                        SCR.ap()[0:1, 512:1024],
                        ps1[64:65, half * 512:(half + 1) * 512]), (("T", tav),),
                        inc=True)
                    trb = E("T", lambda e: e.matmul(
                        ps1[0:64, 1024 + 0:1024 + 512], ONERF[0:1, 0:64],
                        SCR.ap()[0:1, 512:1024], start=True, stop=True), (("V", vrec),),
                        inc=True)
                    vcb = E("V", lambda e: e.tensor_copy(SCR.ap()[0:64, 0:512],
                                                         ps1[0:64, 1024:1536]),
                            (("T", trb),), inc=True)
                    dst = PO.ap()[(h % 2) * 64:(h % 2) * 64 + 64,
                                  (h // 2) * 2048 + qb * 512:
                                  (h // 2) * 2048 + (qb + 1) * 512]
                    E("V", lambda e, dst=dst, half=half: e.tensor_tensor(
                        dst, ps1[0:64, half * 512:(half + 1) * 512],
                        SCR.ap()[0:64, 0:512], ALU.mult), (("V", vcb),), inc=True)
                prev_ev_v = em.now("V")
                mT_hist[it % 2] = em.now("T")
                if bg is not None:
                    next(bg)
        return {"attn_t": em.now("T"), "attn_v": em.now("V"), "attn_s": em.now("S")}

    def ph_proj(L, ev, xsrc, xdst):
        wp = dr["wprojT0"] if L == 0 else dr["wprojT1"]
        bcol = BPB0 if L == 0 else BP1
        gw = E("G", lambda e: e.dma_start(
            PB.ap()[:, 0:8192].rearrange("p (kc f) -> p kc f", f=1024),
            wp.rearrange("(kc p) f -> p kc f", p=128)),
            (("T", ev["attn_t"]),), inc=True, tag="gwp")
        tmp = PBf[:, 4096:8192]
        CPB = PA.ap()[:, 0:512]
        SQB = PA.ap()[:, 512:1024]
        vprev_tb = em.now("V")
        yout = 0
        for tb in range(4):
            if L == 0:
                PR1 = PAf[0:1, 512:1024]
                PR2 = PAf[0:1, 1024:1536]
                E("V", lambda e, tb=tb: e.tensor_copy(
                    PR1, R_RS0[0:1, tb * 512:(tb + 1) * 512]),
                  (("V", vprev_tb), ("V", ev["attn_v"])))
                vcp = E("V", lambda e, tb=tb: e.tensor_copy(
                    PR2, R_M0R0[0:1, tb * 512:(tb + 1) * 512]), inc=True)
                E("T", lambda e: e.matmul(
                    ps1[:, 1024:1536], ONERF, PR1,
                    start=True, stop=True), (("V", vcp),))
                tbc = E("T", lambda e: e.matmul(
                    ps1[:, 1536:2048], ONERF, PR2,
                    start=True, stop=True), inc=True)
                E("V", lambda e: e.tensor_copy(SCR.ap()[:, 512:1024], ps1[:, 1024:1536]),
                  (("T", tbc),))
                E("V", lambda e: e.tensor_copy(SCR.ap()[:, 1024:1536],
                                               ps1[:, 1536:2048]), inc=True)
            vbc = em.now("V")
            v_evict = {}
            for Mt in range(8):
                bank = Mt % 4
                yxr = "yxr%d" % (Mt % 2)
                yx = E("S", lambda e, Mt=Mt, tb=tb: e.dma_start(
                    WS.ap()[:, (Mt % 2) * 512:(Mt % 2) * 512 + 512],
                    xsrc[Mt * 128:(Mt + 1) * 128, tb * 512:(tb + 1) * 512]),
                    ((("V", v_evict[Mt - 2]),) if Mt >= 2 else (("V", vprev_tb),)) +
                    (tuple(ev.get("xland", ())) if tb == 0 and Mt == 0 else ()),
                    inc=True, tag=yxr, dma=True)
                w0 = [("gwp", gw), ("V", ev["attn_v"])]
                if Mt >= 4:
                    w0.append(("V", v_evict[Mt - 4]))
                for kc in range(8):
                    E("T", lambda e, tb=tb, Mt=Mt, kc=kc, bank=bank: e.matmul(
                        ps0[:, bank * 512:(bank + 1) * 512],
                        PB.ap()[:, kc * 1024 + Mt * 128: kc * 1024 + (Mt + 1) * 128],
                        PO.ap()[:, kc * 2048 + tb * 512: kc * 2048 + (tb + 1) * 512],
                        start=(kc == 0), stop=(kc == 7)),
                      tuple(w0) if kc == 0 else (), inc=(kc == 7))
                tmm = em.now("T")
                xs = WS.ap()[:, (Mt % 2) * 512:(Mt % 2) * 512 + 512]
                tmps = tmp[:, Mt * 512:(Mt + 1) * 512]
                wprev = (("V", v_evict[Mt - 1]),) if Mt >= 1 else ()
                if L == 0:
                    E("V", lambda e, xs=xs, Mt=Mt: e.scalar_tensor_tensor(
                        SCR.ap()[:, 1536:2048], xs, GAM0[:, Mt:Mt + 1],
                        SCR.ap()[:, 512:1024], ALU.mult, ALU.mult),
                      (("T", tmm), (yxr, yx), ("V", vbc)) + wprev, inc=True)
                    E("V", lambda e, tmps=tmps, xs=xs, bank=bank, Mt=Mt:
                      e.scalar_tensor_tensor(
                        tmps, ps0[:, bank * 512:(bank + 1) * 512], bcol[:, Mt:Mt + 1],
                        xs, ALU.add, ALU.add), inc=True)
                    E("V", lambda e, tmps=tmps: e.tensor_tensor(
                        tmps, tmps, SCR.ap()[:, 1536:2048], ALU.add),
                      (("V", em.now("V")),), inc=True)
                    E("V", lambda e, tmps=tmps, Mt=Mt: e.scalar_tensor_tensor(
                        tmps, SCR.ap()[:, 1024:1536], GAM0[:, Mt:Mt + 1], tmps,
                        ALU.mult, ALU.add), (("V", em.now("V")),), inc=True)
                else:
                    E("V", lambda e, tmps=tmps, xs=xs, bank=bank, Mt=Mt:
                      e.scalar_tensor_tensor(
                        tmps, ps0[:, bank * 512:(bank + 1) * 512], bcol[:, Mt:Mt + 1],
                        xs, ALU.add, ALU.add),
                      (("T", tmm), (yxr, yx)) + wprev, inc=True)
                v_evict[Mt] = em.now("V")
            for Mt in range(8):
                tmps = tmp[:, Mt * 512:(Mt + 1) * 512]
                E("V", lambda e, tmps=tmps: e.tensor_copy(CPB, tmps),
                  (("V", v_evict[7]), ("T", em.now("T"))) if Mt == 0
                  else (("T", em.now("T")),))
                E("V", lambda e, tmps=tmps: e.tensor_tensor(SQB, tmps, tmps, ALU.mult),
                  inc=True)
                vst = em.now("V")
                E("T", lambda e, Mt=Mt: e.matmul(
                    ps1[0:1, 0:512], ONESB, CPB, start=(Mt == 0), stop=(Mt == 7)),
                  (("V", vst),))
                E("T", lambda e, Mt=Mt: e.matmul(
                    ps1[0:1, 512:1024], ONESB, SQB, start=(Mt == 0), stop=(Mt == 7)),
                  inc=True)
            tst = em.now("T")
            rmu = PAf[0:1, 1536:2048]
            rsd = PAf[0:1, 2048:2560]
            vmu1 = E("V", lambda e: e.tensor_scalar(rmu, ps1[0:1, 0:512],
                                                    1.0 / 1024, None,
                                                    ALU.mult), (("T", tst),),
                     inc=True)
            E("V", lambda e: e.tensor_tensor(rsd, rmu, rmu, ALU.mult),
              (("V", vmu1),), inc=True)
            E("V", lambda e: e.scalar_tensor_tensor(rsd, ps1[0:1, 512:1024],
                                                    1.0 / 1024, rsd, ALU.mult,
                                                    ALU.subtract),
              (("V", em.now("V")),), inc=True)
            vvar = em.now("V")
            sl = E("S", lambda e: e.activation(rsd, rsd, AF.Ln, bias=EPSR),
                   (("V", vvar),), inc=True)
            sl = E("S", lambda e: e.activation(rsd, rsd, AF.Exp, scale=-0.5),
                   (("S", sl),), inc=True)
            E("T", lambda e: e.matmul(ps1[:, 1024:1536], ONERF, rmu, start=True,
                                      stop=True), (("S", sl),))
            tbb = E("T", lambda e: e.matmul(ps1[:, 1536:2048], ONERF, rsd, start=True,
                                            stop=True), inc=True)
            E("V", lambda e: e.tensor_copy(SCR.ap()[:, 0:512], ps1[:, 1024:1536]),
              (("T", tbb),))
            E("V", lambda e: e.tensor_copy(SCR.ap()[:, 512:1024], ps1[:, 1536:2048]),
              inc=True)
            vbb = em.now("V")
            for Mt in range(8):
                tmps = tmp[:, Mt * 512:(Mt + 1) * 512]
                E("V", lambda e, tmps=tmps: e.tensor_tensor(
                    tmps, tmps, SCR.ap()[:, 0:512], ALU.subtract),
                  (("V", vbb),) if Mt == 0 else (), inc=True)
                dst = PX.ap()[:, Mt * 2048 + tb * 512: Mt * 2048 + (tb + 1) * 512]
                E("V", lambda e, tmps=tmps, dst=dst: e.tensor_tensor(
                    dst, tmps, SCR.ap()[:, 512:1024], ALU.mult),
                  (("V", em.now("V")),), inc=True)
            vln = em.now("V")
            yout = E("Y", lambda e, tb=tb: e.dma_start(
                xdst[:, tb * 512:(tb + 1) * 512]
                .rearrange("(mt p) t -> p mt t", p=128),
                PX.ap().rearrange("p (mt t) -> p mt t", t=2048)[:, :, tb * 512:(tb + 1) * 512]),
                (("V", vln),), inc=True, tag="yo0")
            vprev_tb = vln
        return {"proj_v": em.now("V"), "proj_y": yout, "proj_t": em.now("T"),
                "proj_s": em.now("S"), "xland": (("yo0", yout),)}

    def ph_wmean_gen(L, ev):
        srcs = [dr["w1T0" if L == 0 else "w1T1"], dr["w2T0" if L == 0 else "w2T1"]]
        ACC = SMF.ap()[:, 523:587]
        vch = {}
        n = 0
        for wi in range(2):
            w = srcs[wi]
            flat = w.rearrange("a b -> (a b)").rearrange("(c p n) -> c p n", p=128,
                                                         n=1024)
            for c in range(32):
                buf = WS.ap()[:, (c % 2) * 1024:(c % 2) * 1024 + 1024]
                gwm = "gwm%d" % (c % 2)
                gd = E("G", lambda e, c=c, buf=buf, flat=flat: e.dma_start(
                    buf, flat[c]),
                    (("V", vch[(wi, c - 2)]),) if c >= 2 else
                    ((("V", ev["qk_v"]),) if wi == 0 else
                     (("V", vch[(0, 30 + c)]),)), inc=True, tag=gwm)
                E("V", lambda e, wi=wi, c=c, buf=buf: e.tensor_reduce(
                    ACC[:, wi * 32 + c: wi * 32 + c + 1], buf, axis=AX.X, op=ALU.add,
                    apply_absolute_value=True), ((gwm, gd),), inc=True)
                vch[(wi, c)] = em.now("V")
                n += 1
                if n % 2 == 0:
                    yield
        while True:
            yield

    def ph_wmean_fin(L):
        ACC = SMF.ap()[:, 523:587]
        out = {}
        for wi in range(2):
            E("V", lambda e, wi=wi: e.tensor_reduce(
                SMF.ap()[:, 587 + wi: 588 + wi], ACC[:, wi * 32: wi * 32 + 32],
                axis=AX.X, op=ALU.add), (("V", em.now("V")),), inc=True)
            vsum = em.now("V")
            tcp = E("T", lambda e, wi=wi: e.matmul(
                ps1[0:1, 1792 + wi * 8: 1793 + wi * 8], ONESFC,
                SMF.ap()[:, 587 + wi: 588 + wi], start=True, stop=True),
                (("V", vsum),), inc=True)
            MCF = SMF.ap()[0:1, 592 + wi: 593 + wi]
            MCQ = SMF.ap()[0:1, 594 + wi: 595 + wi]
            WSS = SMF.ap()[0:1, 596 + wi: 597 + wi]
            vmc = E("V", lambda e, wi=wi, MCF=MCF: e.tensor_scalar(
                MCF, ps1[0:1, 1792 + wi * 8: 1793 + wi * 8], 1.0 / (4096 * 1024),
                1e-5, ALU.mult, ALU.max), (("T", tcp),), inc=True)
            E("V", lambda e, MCF=MCF, MCQ=MCQ: e.tensor_scalar(
                MCQ, MCF, 1.0 / 127, None, ALU.mult), (("V", vmc),))
            E("V", lambda e, MCF=MCF, WSS=WSS: e.reciprocal(WSS, MCF), inc=True)
            vws = em.now("V")
            trp = E("T", lambda e, WSS=WSS: e.matmul(
                ps1[:, 1536:1537], ONERF, WSS, start=True, stop=True),
                (("V", vws),), inc=True)
            E("V", lambda e, wi=wi: e.tensor_copy(WSR[wi], ps1[:, 1536:1537]),
              (("T", trp),), inc=True)
            out[f"mcq{wi}"] = MCQ
        out["wmean_v"] = em.now("V")
        return out

    def ph_quantx(L, ev):
        pv = ev["proj_v"]
        Mp = POf[:, 0:2048]
        Mn = POf[:, 2048:4096]
        E("V", lambda e: e.tensor_copy(Mp, PX.ap()[:, 0:2048]), (("V", pv),))
        E("V", lambda e: e.tensor_copy(Mn, PX.ap()[:, 0:2048]), inc=True)
        for kc in range(1, 8):
            E("V", lambda e, kc=kc: e.tensor_tensor(
                Mp, Mp, PX.ap()[:, kc * 2048:(kc + 1) * 2048], ALU.max),
              (("V", em.now("V")),))
            E("V", lambda e, kc=kc: e.tensor_tensor(
                Mn, Mn, PX.ap()[:, kc * 2048:(kc + 1) * 2048], ALU.min), inc=True)
        E("V", lambda e: e.tensor_scalar(Mn, Mn, -1.0, None, ALU.mult),
          (("V", em.now("V")),), inc=True)
        E("V", lambda e: e.tensor_tensor(Mp, Mp, Mn, ALU.max),
          (("V", em.now("V")),), inc=True)
        vam = em.now("V")
        vred = 0
        for tt in range(16):
            ttr = E("T", lambda e, tt=tt: e.transpose(
                ps1[:, (tt % 2) * 512:(tt % 2) * 512 + 128],
                Mp[:, tt * 128:(tt + 1) * 128], IDF),
                (("V", vam),) if tt < 2 else (("V", vred),), inc=True)
            E("V", lambda e, tt=tt: e.tensor_reduce(
                AMC[:, tt:tt + 1], ps1[:, (tt % 2) * 512:(tt % 2) * 512 + 128],
                axis=AX.X, op=ALU.max), (("T", ttr),), inc=True)
            vred = em.now("V")
        E("V", lambda e: e.tensor_scalar(AMC, AMC, 1e-5, None, ALU.max),
          (("V", em.now("V")),), inc=True)
        vclip = em.now("V")
        y1 = E("Y", lambda e: e.dma_start(
            row_d[3:4, 0:2048].rearrange("o (c p) -> (o p) c", p=128), AMC[:, 0:16],
            allow_slow_non_contiguous=True), (("V", vclip),), inc=True, tag="yrow")
        AROW = POf[0:1, 4096:6144]
        y2 = E("Y", lambda e: e.dma_start(AROW, row_d[3:4, 0:2048]),
               (("yrow", y1),), inc=True, tag="yrow")
        SXR = POf[0:1, 6144:8192]
        vrc = E("V", lambda e: e.reciprocal(SXR, AROW), (("yrow", y2),), inc=True)
        E("V", lambda e: e.tensor_scalar(SXR, SXR, 127.0, None, ALU.mult),
          (("V", vrc),), inc=True)
        E("V", lambda e: e.tensor_scalar(R0[:, 0:2048], AROW, ev["mcq0"], None,
                                         ALU.mult), inc=True)
        vrows = em.now("V")
        for tb in range(4):
            tb1 = E("T", lambda e, tb=tb: e.matmul(
                ps1[:, 1024:1536], ONERF, R0[0:1, tb * 512:(tb + 1) * 512],
                start=True, stop=True), (("V", vrows), ("V", em.now("V"))),
                inc=True)
            E("V", lambda e, tb=tb: e.tensor_copy(
                PDf[:, tb * 512:(tb + 1) * 512], ps1[:, 1024:1536]),
              (("T", tb1),), inc=True)
        slots = [SCR.ap()[:, i * 512:(i + 1) * 512] for i in range(4)]
        s1c = {}
        s2c = {}
        pend = []

        def xq_s2():
            p, sl = pend.pop(0)
            tb_, kc_ = divmod(p, 8)
            E("V", lambda e, sl=sl, tb_=tb_, kc_=kc_: e.tensor_scalar(
                PA.ap()[:, kc_ * 2048 + tb_ * 512: kc_ * 2048 + (tb_ + 1) * 512],
                sl, MAGIC, MAGIC, ALU.add, ALU.subtract),
              (("V", s1c[p]),), inc=True)
            s2c[p] = em.now("V")
        for tb in range(4):
            tsx = E("T", lambda e, tb=tb: e.matmul(
                ps1[:, 1536:2048], ONERF, SXR[:, tb * 512:(tb + 1) * 512],
                start=True, stop=True), (("V", em.now("V")),), inc=True)
            for kc in range(8):
                p = tb * 8 + kc
                sl = slots[p % 4]
                E("V", lambda e, sl=sl, tb=tb, kc=kc: e.tensor_tensor(
                    sl, PX.ap()[:, kc * 2048 + tb * 512:
                                kc * 2048 + (tb + 1) * 512],
                    ps1[:, 1536:2048], ALU.mult),
                  (("T", tsx), ("V", s2c.get(p - 4, 0))), inc=True)
                s1c[p] = em.now("V")
                pend.append((p, sl))
                if len(pend) >= 4:
                    xq_s2()
        while pend:
            xq_s2()
        return {"xq_v": em.now("V"), "r1b_v": em.now("V")}

    def ph_ffnA(L, ev):
        cwL = CW[L]
        cbL = CB[L]
        w1 = dr["w1T0" if L == 0 else "w1T1"]
        Mp = PB.ap()[:, 0:2048]
        Mn = PB.ap()[:, 2048:4096]
        load_actset("silu_and_others")
        E("V", lambda e: e.memset(Mp, -3.0e38), (("T", ev.get("attn_t", 0)),))
        E("V", lambda e: e.memset(Mn, 3.0e38))
        E("V", lambda e: e.memset(PO.ap()[:, 0:1], 0.0))
        E("V", lambda e: e.memset(PO.ap()[:, 2049:2050], 0.0), inc=True)
        vinit = em.now("V")
        vq = {}
        vus = {}
        vamax = {}
        ysp = {}
        sse = {}
        tm1s = {}
        for ct in range(32):
            buf = WS.ap()[:, (ct % 2) * 1024:(ct % 2) * 1024 + 1024]
            qb_ = W1Q[:, (ct % 2) * 1024:(ct % 2) * 1024 + 1024]
            yw1 = "yw1%d" % (ct % 2)
            gw = E("Y", lambda e, ct=ct, buf=buf: e.dma_start(
                buf.rearrange("p (kc j) -> p kc j", j=128),
                w1[:, ct * 128:(ct + 1) * 128].rearrange("(kc p) j -> p kc j", p=128)),
                (("V", vq[ct - 2]),) if ct >= 2 else (("V", ev["wmean_v"]),),
                inc=True, tag=yw1)
            E("V", lambda e, buf=buf: e.tensor_scalar(
                SCR.ap()[:, 1024:2048], buf, WSR[0], MAGIC, ALU.mult, ALU.add),
              ((yw1, gw), ("T", tm1s.get(ct - 2, 0)), ("V", em.now("V"))), inc=True)
            E("V", lambda e: e.tensor_scalar(
                SCR.ap()[:, 1024:2048], SCR.ap()[:, 1024:2048], MAGIC, -1.0,
                ALU.subtract, ALU.max), (("V", em.now("V")),), inc=True)
            E("V", lambda e, qb_=qb_: e.tensor_scalar(
                qb_, SCR.ap()[:, 1024:2048], 1.0, None, ALU.min),
              (("V", em.now("V")),), inc=True)
            vq[ct] = em.now("V")
            for kc in range(8):
                for tb in range(4):
                    w = []
                    if kc == 0 and tb == 0:
                        w = [("V", vq[ct]), ("V", ev["xq_v"])]
                        if ct >= 1:
                            w.append(("V", vus[ct - 1]))
                    E("T", lambda e, kc=kc, tb=tb, qb_=qb_: e.matmul(
                        ps0[:, tb * 512:(tb + 1) * 512], qb_[:, kc * 128:(kc + 1) * 128],
                        PA.ap()[:, kc * 2048 + tb * 512: kc * 2048 + (tb + 1) * 512],
                        start=(kc == 0), stop=(kc == 7)), tuple(w),
                      inc=(kc == 7 and tb == 3))
            tm1 = em.now("T")
            tm1s[ct] = tm1
            E("V", lambda e: e.tensor_tensor(
                PO.ap()[:, 1:2049], ps0[:, 0:2048], PDf[:, 0:2048], ALU.mult),
              (("T", tm1), ("V", ev["r1b_v"])), inc=True)
            vus[ct] = em.now("V")
            for t in range(3):
                E("V", lambda e, t=t, ct=ct: e.tensor_scalar(
                    DIAG[t], IDENT, cwL[:, t * 32 + ct: t * 32 + ct + 1], None,
                    ALU.mult), (("T", em.now("T")),) if t == 0 else (),
                  inc=(t == 2))
            vdg = em.now("V")
            for tb in range(4):
                for t in range(3):
                    E("T", lambda e, tb=tb, t=t: e.matmul(
                        ps1[:, (tb % 2) * 512:(tb % 2) * 512 + 512], DIAG[t],
                        PO.ap()[:, tb * 512 + t: tb * 512 + t + 512],
                        start=(t == 0), stop=(t == 2)),
                      (("V", vdg), ("S", sse.get(ct - 1, 0))) if tb == 0 and t == 0
                      else ((("S", sse.get((ct, tb - 2), 0)),) if t == 0 and tb >= 2
                            else ()), inc=(t == 2))
                tcv = em.now("T")
                stg = PO.ap()[:, 4096 + (ct % 2) * 2048: 4096 + (ct % 2) * 2048 + 2048]
                E("S", lambda e, tb=tb, stg=stg, ct=ct: e.activation(
                    stg[:, tb * 512:(tb + 1) * 512],
                    ps1[:, (tb % 2) * 512:(tb % 2) * 512 + 512], AF.Silu,
                    bias=cbL[:, ct:ct + 1]),
                  (("T", tcv),) + ((("yhs%d" % (ct % 2), ysp[ct - 2]),)
                                   if tb == 0 and ct >= 2 else ()), inc=True)
                sse[(ct, tb)] = em.now("S")
            sse[ct] = em.now("S")
            stg = PO.ap()[:, 4096 + (ct % 2) * 2048: 4096 + (ct % 2) * 2048 + 2048]
            E("V", lambda e, stg=stg: e.tensor_tensor(Mp, Mp, stg, ALU.max),
              (("S", sse[ct]), ("V", vinit)) +
              ((("V", vamax[ct - 1]),) if ct >= 1 else ()))
            E("V", lambda e, stg=stg: e.tensor_tensor(Mn, Mn, stg, ALU.min), inc=True)
            vamax[ct] = em.now("V")
            ysp[ct] = E("Y", lambda e, ct=ct, stg=stg: e.dma_start(hsil_d[ct], stg),
                        (("S", sse[ct]),), inc=True, tag="yhs%d" % (ct % 2))
        return {"ffnA_v": em.now("V"), "ffnA_t": em.now("T"),
                "ffnA_y": (("yhs0", em.now("yhs0")), ("yhs1", em.now("yhs1"))),
                "ffnA_s": em.now("S"), "amax_v": vamax[31]}

    def ph_ffnmid(L, ev):
        Mp = PB.ap()[:, 0:2048]
        Mn = PB.ap()[:, 2048:4096]
        vneg = E("V", lambda e: e.tensor_scalar(Mn, Mn, -1.0, None, ALU.mult),
                 (("V", ev["ffnA_v"]),), inc=True)
        E("V", lambda e: e.tensor_tensor(Mp, Mp, Mn, ALU.max), (("V", vneg),),
          inc=True)
        vam = em.now("V")
        A2C = SMF.ap()[:, 603:619]
        vred = 0
        for tt in range(16):
            ttr = E("T", lambda e, tt=tt: e.transpose(
                ps1.ap().bitcast(BF16)[:, (tt % 2) * 1024:(tt % 2) * 1024 + 128]
                if False else
                ps1[:, (tt % 2) * 512:(tt % 2) * 512 + 64].bitcast(BF16),
                Mp[:, tt * 128:(tt + 1) * 128], IDENT),
                (("V", vam),) if tt < 2 else (("V", vred),), inc=True)
            E("V", lambda e, tt=tt: e.tensor_reduce(
                A2C[:, tt:tt + 1],
                ps1[:, (tt % 2) * 512:(tt % 2) * 512 + 64].bitcast(BF16),
                axis=AX.X, op=ALU.max), (("T", ttr),), inc=True)
            vred = em.now("V")
        E("V", lambda e: e.tensor_scalar(A2C, A2C, 1e-5, None, ALU.max),
          (("V", em.now("V")),), inc=True)
        vclip = em.now("V")
        y1 = E("Y", lambda e: e.dma_start(
            row_d[4:5, 0:2048].rearrange("o (c p) -> (o p) c", p=128), A2C,
            allow_slow_non_contiguous=True),
            (("V", vclip)) + tuple(ev["ffnA_y"])
            if False else (("V", vclip),) + tuple(ev["ffnA_y"]),
            inc=True, tag="yrow")
        AROW = POf[0:1, 4096:6144]
        y2 = E("Y", lambda e: e.dma_start(AROW, row_d[4:5, 0:2048]),
               (("yrow", y1),), inc=True, tag="yrow")
        S2R = POf[0:1, 6144:8192]
        vrc = E("V", lambda e: e.reciprocal(S2R, AROW),
                (("yrow", y2), ("V", em.now("V"))), inc=True)
        E("V", lambda e: e.tensor_scalar(S2R, S2R, 127.0, None, ALU.mult),
          (("V", vrc),))
        E("V", lambda e: e.tensor_scalar(R0[:, 0:2048], AROW, ev["mcq1"], None,
                                         ALU.mult), inc=True)
        vrows = em.now("V")
        for tb in range(4):
            t1 = E("T", lambda e, tb=tb: e.matmul(
                ps1[:, 1024:1536], ONERF, S2R[:, tb * 512:(tb + 1) * 512],
                start=True, stop=True),
                (("V", vrows), ("V", ev["ffnA_v"])), inc=True)
            E("V", lambda e, tb=tb: e.tensor_copy(
                PDf[:, 2048 + tb * 512: 2048 + (tb + 1) * 512], ps1[:, 1024:1536]),
              (("T", t1),), inc=True)
            t2 = E("T", lambda e, tb=tb: e.matmul(
                ps1[:, 1536:2048], ONERF, R0[0:1, tb * 512:(tb + 1) * 512],
                start=True, stop=True), (("V", em.now("V")),), inc=True)
            E("V", lambda e, tb=tb: e.tensor_copy(
                PDf[:, tb * 512:(tb + 1) * 512], ps1[:, 1536:2048]),
              (("T", t2),), inc=True)
        return {"mid_v": em.now("V"), "mid_y": em.now("yrow")}

    def ph_ffnB(L, ev, xsrc, xdst):
        load_actset("natural_log_exp_and_others")
        w2 = dr["w2T0" if L == 0 else "w2T1"]
        vqw = {}
        wbufs = [SCR.ap()[:, 1024:2048], ROWS.ap()[:, 1024:2048],
                 PBf[:, 4096:5120], PBf[:, 5120:6144]]
        q1c, q2c, q3c = {}, {}, {}

        def w2_q2(cc):
            b = wbufs[cc % 4]
            E("V", lambda e, b=b: e.tensor_scalar(
                b, b, MAGIC, -1.0, ALU.subtract, ALU.max),
              (("V", q1c[cc]),), inc=True)
            q2c[cc] = em.now("V")

        def w2_q3(cc):
            b = wbufs[cc % 4]
            dst = (PO.ap()[:, cc * 1024:(cc + 1) * 1024] if cc < 16 else
                   PA.ap()[:, (cc - 16) * 1024:(cc - 15) * 1024])
            w = [("V", q2c[cc])]
            if cc == 0:
                w += [("V", ev["mid_v"]), ("yrow", ev["mid_y"])]
            if cc == 16:
                w.append(("T", ev["ffnA_t"]))
            E("V", lambda e, dst=dst, b=b: e.tensor_scalar(
                dst, b, 1.0, None, ALU.min), tuple(w), inc=True)
            q3c[cc] = em.now("V")
            vqw[cc] = q3c[cc]
        for cc in range(32):
            buf = WS.ap()[:, (cc % 2) * 1024:(cc % 2) * 1024 + 1024]
            yw2 = "yw2%d" % (cc % 2)
            gw = E("Y", lambda e, cc=cc, buf=buf: e.dma_start(
                buf, w2[cc * 128:(cc + 1) * 128, :]),
                (("V", q1c[cc - 2]),) if cc >= 2 else
                (("V", ev["ffnA_v"]), ("T", ev["ffnA_t"]), ("S", ev["ffnA_s"])),
                inc=True, tag=yw2)
            wA = [(yw2, gw)]
            if cc >= 4:
                wA.append(("V", q3c[cc - 4]))
            E("V", lambda e, buf=buf, b=wbufs[cc % 4]: e.tensor_scalar(
                b, buf, WSR[1], MAGIC, ALU.mult, ALU.add), tuple(wA), inc=True)
            q1c[cc] = em.now("V")
            if cc >= 1:
                w2_q2(cc - 1)
            if cc >= 2:
                w2_q3(cc - 2)
        w2_q2(31)
        w2_q3(30)
        w2_q3(31)
        vw2 = em.now("V")
        tmp = PBf[:, 0:4096]
        outb = PBf[:, 4096:8192]
        CPB = SCRb[:, 3072:3584]
        SQB = SCRb[:, 3584:4096]
        yhs = {}
        vhq = {}
        tmm = {}
        youts = 0
        vev_prev = 0
        for tb in range(4):
            hqb = PXb[:, (tb % 2) * 16384:(tb % 2) * 16384 + 16384]
            for cc in range(32):
                q_ = ("yhq%d" % (tb % 2), "ghq%d" % (tb % 2))[cc % 2]
                yhs[(tb, cc)] = E(("Y", "G")[cc % 2],
                                  lambda e, tb=tb, cc=cc, hqb=hqb: e.dma_start(
                    hqb[:, cc * 512:(cc + 1) * 512],
                    hsil_d[cc][:, tb * 512:(tb + 1) * 512]),
                    ((tuple(ev["ffnA_y"]) + tuple(ev["xland"])) if tb == 0
                     else ()) +
                    ((("T", tmm.get(tb - 2, 0)),) if tb >= 2 and cc <= 1 else ()),
                    inc=True, tag=q_)
            hq_full = (("yhq%d" % (tb % 2), em.now("yhq%d" % (tb % 2))),
                       ("ghq%d" % (tb % 2), em.now("ghq%d" % (tb % 2))))
            qslots = [SCR.ap()[:, 0:512], WS.ap()[:, 1024:1536],
                      WS.ap()[:, 1536:2048], ROWS.ap()[:, 1024:1536]]
            qs1 = {}
            qpend = []

            def hq_s2():
                cc_, sl = qpend.pop(0)
                E("V", lambda e, cc_=cc_, sl=sl, hqb=hqb: e.tensor_scalar(
                    hqb[:, cc_ * 512:(cc_ + 1) * 512], sl, MAGIC,
                    MAGIC, ALU.add, ALU.subtract), (("V", qs1[cc_]),), inc=True)
                vhq[(tb, cc_)] = em.now("V")
            for cc in range(32):
                sl = qslots[cc % 4]
                wq_ = ()
                if cc == 0:
                    wq_ = hq_full + (("V", ev["mid_v"]), ("V", vw2))
                    if tb >= 1:
                        wq_ += (("V", vhq[(tb - 1, 31)]),)
                elif cc >= 4:
                    wq_ = (("V", vhq.get((tb, cc - 4), 0)),)
                E("V", lambda e, cc=cc, sl=sl, hqb=hqb, tb=tb: e.tensor_tensor(
                    sl, hqb[:, cc * 512:(cc + 1) * 512],
                    PDf[:, 2048 + tb * 512: 2048 + (tb + 1) * 512], ALU.mult),
                  wq_, inc=True)
                qs1[cc] = em.now("V")
                qpend.append((cc, sl))
                if len(qpend) >= 4:
                    hq_s2()
            while qpend:
                hq_s2()
            v_evict = {}
            for Mt in range(8):
                yxr = "yxr%d" % (Mt % 2)
                yx = E("S", lambda e, Mt=Mt, tb=tb: e.dma_start(
                    WS.ap()[:, (Mt % 2) * 512:(Mt % 2) * 512 + 512],
                    xsrc[Mt * 128:(Mt + 1) * 128, tb * 512:(tb + 1) * 512]),
                    (("V", v_evict[Mt - 2]),) if Mt >= 2 else
                    ((("V", vw2),) + tuple(ev["xland"]) if tb == 0 and Mt == 0 else
                     ((("V", vev_prev),) if Mt == 0 else ())), inc=True, tag=yxr,
                    dma=True)
                for cc in range(32):
                    lhs = (PO.ap()[:, cc * 1024 + Mt * 128: cc * 1024 + (Mt + 1) * 128]
                           if cc < 16 else
                           PA.ap()[:, (cc - 16) * 1024 + Mt * 128:
                                   (cc - 16) * 1024 + (Mt + 1) * 128])
                    w = []
                    if Mt == 0:
                        w = [("V", vhq[(tb, cc)]), ("V", vqw[cc])]
                    elif cc == 0:
                        w = [("V", vhq[(tb, 31)]), ("V", vw2)]
                    if cc == 0 and Mt >= 4:
                        w.append(("V", v_evict[Mt - 4]))
                    E("T", lambda e, cc=cc, Mt=Mt, lhs=lhs, hqb=hqb: e.matmul(
                        ps0[:, (Mt % 4) * 512:(Mt % 4) * 512 + 512], lhs,
                        hqb[:, cc * 512:(cc + 1) * 512],
                        start=(cc == 0), stop=(cc == 31)), tuple(w), inc=(cc == 31))
                ttm = em.now("T")
                E("V", lambda e, Mt=Mt, tb=tb: e.scalar_tensor_tensor(
                    SCR.ap()[:, 512:1024], ps0[:, (Mt % 4) * 512:(Mt % 4) * 512 + 512],
                    1.0, PDf[:, tb * 512:(tb + 1) * 512], ALU.mult, ALU.mult),
                  (("T", ttm), ("V", em.now("V"))), inc=True)
                E("V", lambda e, Mt=Mt: e.tensor_tensor(
                    tmp[:, Mt * 512:(Mt + 1) * 512], SCR.ap()[:, 512:1024],
                    WS.ap()[:, (Mt % 2) * 512:(Mt % 2) * 512 + 512], ALU.add),
                  ((yxr, yx), ("V", em.now("V"))), inc=True)
                v_evict[Mt] = em.now("V")
            tmm[tb] = em.now("T")
            for Mt in range(8):
                tmps = tmp[:, Mt * 512:(Mt + 1) * 512]
                E("V", lambda e, tmps=tmps: e.tensor_copy(CPB, tmps),
                  (("T", em.now("T")),))
                E("V", lambda e, tmps=tmps: e.tensor_tensor(SQB, tmps, tmps,
                                                            ALU.mult), inc=True)
                vst = em.now("V")
                E("T", lambda e, Mt=Mt: e.matmul(
                    ps1[0:1, 0:512], ONESB, CPB, start=(Mt == 0), stop=(Mt == 7)),
                  (("V", vst), ("S", ev["ffnA_s"])))
                E("T", lambda e, Mt=Mt: e.matmul(
                    ps1[0:1, 512:1024], ONESB, SQB, start=(Mt == 0), stop=(Mt == 7)),
                  inc=True)
            tst = em.now("T")
            rmu = ROWS.ap()[0:1, 0:512]
            rsd = ROWS.ap()[0:1, 512:1024]
            vmu1 = E("V", lambda e, rmu=rmu: e.tensor_scalar(
                rmu, ps1[0:1, 0:512], 1.0 / 1024, None, ALU.mult), (("T", tst),),
                inc=True)
            E("V", lambda e, rmu=rmu, rsd=rsd: e.tensor_tensor(rsd, rmu, rmu,
                                                               ALU.mult),
              (("V", vmu1),), inc=True)
            E("V", lambda e, rsd=rsd: e.scalar_tensor_tensor(
                rsd, ps1[0:1, 512:1024], 1.0 / 1024, rsd, ALU.mult, ALU.subtract),
              (("V", em.now("V")),), inc=True)
            vvar = em.now("V")
            sl = E("S", lambda e, rsd=rsd: e.activation(rsd, rsd, AF.Ln, bias=EPSR),
                   (("V", vvar),), inc=True)
            sl = E("S", lambda e, rsd=rsd: e.activation(rsd, rsd, AF.Exp, scale=-0.5),
                   (("S", sl),), inc=True)
            t1 = E("T", lambda e, rmu=rmu: e.matmul(ps1[:, 1024:1536], ONERF, rmu,
                                                    start=True, stop=True),
                   (("S", sl),))
            t2 = E("T", lambda e, rsd=rsd: e.matmul(ps1[:, 1536:2048], ONERF, rsd,
                                                    start=True, stop=True), inc=True)
            tbb = em.now("T")
            E("V", lambda e: e.tensor_copy(SCR.ap()[:, 1024:1536], ps1[:, 1024:1536]),
              (("T", tbb),))
            E("V", lambda e: e.tensor_copy(PDf[:, 4096:4608], ps1[:, 1536:2048]),
              inc=True)
            vbb = em.now("V")
            for Mt in range(8):
                tmps = tmp[:, Mt * 512:(Mt + 1) * 512]
                E("V", lambda e, tmps=tmps: e.tensor_tensor(
                    tmps, tmps, SCR.ap()[:, 1024:1536], ALU.subtract),
                  (("V", vbb), ("yo1", youts)) if Mt == 0 else (), inc=True)
                E("V", lambda e, tmps=tmps, Mt=Mt: e.tensor_tensor(
                    outb[:, Mt * 512:(Mt + 1) * 512], tmps, PDf[:, 4096:4608],
                    ALU.mult), (("V", em.now("V")),), inc=True)
            vln = em.now("V")
            vev_prev = vln
            youts = E("Y", lambda e, tb=tb: e.dma_start(
                xdst[:, tb * 512:(tb + 1) * 512].rearrange("(mt p) t -> p mt t",
                                                           p=128),
                outb.rearrange("p (mt t) -> p mt t", t=512)),
                (("V", vln),), inc=True, tag="yo1")
        return {"ffnB_y": youts, "ffnB_v": em.now("V"), "ffnB_t": em.now("T"),
                "ffnB_s": em.now("S"), "xland": (("yo1", youts),)}

    evs = {}
    evs.update(ph_init())
    for L in range(2):
        ev_pre = dict(evs)
        if L == 0:
            evs.update(ph_dbuild(evs))
            xsrc = dr["xT"]
        else:
            evs.update(ph_tbuild(evs, pre=(("T", evs["ffnB_t"]),
                                           ("V", evs["ffnB_v"]))))
            xsrc = xh[0]
        pa_free = () if L == 0 else (("T", evs["ffnB_t"]),) + tuple(evs["xland"])
        if L == 0:
            dump("init_smf", SMF.ap()[:, 0:768], None)
            dump("init_smb", SMB.ap()[:, 0:1024], None)
            dump("dbuild_pd", PD.ap()[:, 0:2056], PD.ap()[:, 0:1025])
            dump("dbuild_stk", SMF.ap()[:, 474:490], SMF.ap()[0:1, 474:490])
            dump("dbuild_norc", SMF.ap()[:, 490:499])
            dump("dbuild_frep", SMF.ap()[:, 506:507])
        evs.update(ph_prep(L, xsrc, evs, pa_free=pa_free,
                           ps_free=tuple(evs.get("ps_free", ()))))
        if L == 0:
            dump("prep_rows", ROWS.ap()[:, 0:2048], ROWS.ap()[0:1, 0:2048])
            dump("prep_pxs", PX.ap()[0:32, 0:6144], PX.ap()[0:1, 0:6144])
            dump("prep_pa", PA.ap()[:, 0:2048])
        pbf = () if L == 0 else (("yo1", evs["ffnB_y"]),)
        evs.update(ph_qkv(L, evs, po_free=tuple(evs.get("po_free", ())) + pbf,
                          pxb_free=pbf, ps_free=tuple(evs.get("ps_free", ()))))
        if L == 0:
            dump("qkv_px", PX.ap()[:, 0:1024])
            dump("qkv_pxk", PX.ap()[:, 8192:9216])
            dump("qkv_pb", PB.ap()[:, 0:260])
        evs.update(ph_bounds(L, evs))
        if L == 0:
            dump("bounds_rows", ROWS.ap()[:, 0:2048], ROWS.ap()[8:24, 0:1024])
        wgen = ph_wmean_gen(L, evs) if stage > 2 * L + 1 else None
        evs.update(ph_attncore(L, evs, bg=wgen))
        if L == 0:
            dump("attn_po", PO.ap()[:, 0:2048])
            dump("attn_pa", PAf[:, 0:2048])
            if DBGHW:
                E("V", lambda e: e.tensor_copy(WS.ap()[:, 0:2048],
                                               ps0.ap()[:, 0:2048]),
                  tuple((k, v) for k, v in em.cnt.items() if v > 0), inc=True)
                dump("attn_ps0", WS.ap()[:, 0:2048])
                ps1v = (ps1.ap()[0:65, 0:1024] if DBG_SIM
                        else ps1.ap()[:, 0:2048])
                wsv = (WS.ap()[0:65, 0:1024] if DBG_SIM
                       else WS.ap()[:, 0:2048])
                E("V", lambda e, ps1v=ps1v, wsv=wsv: e.tensor_copy(wsv, ps1v),
                  (("ydbg", em.now("ydbg")),), inc=True)
                dump("attn_ps1", WS.ap()[:, 0:2048], WS.ap()[0:65, 0:1024])
            dump("attn_scr", SCR.ap()[0:64, 0:1024], SCR.ap()[0:64, 0:512])
        last = (stage == 2 * L + 1)
        pr = ph_proj(L, evs, xsrc, yT if last else xh[1])
        evs.update(pr)
        if L == 0:
            dump("proj_pa", PAf[0:32, 0:2048])
        if last:
            break
        evs.update(ph_wmean_fin(L))
        evs.update(ph_quantx(L, evs))
        evs.update(ph_ffnA(L, evs))
        evs.update(ph_ffnmid(L, evs))
        last = (stage == 2 * L + 2)
        fb = ph_ffnB(L, evs, xh[1], yT if last else xh[0])
        evs.update(fb)
        if last:
            break

    dma_tags = sorted({t for (eng, _, _, _, t) in em.steps if eng in ("Y", "G")})
    em.emit("Y", lambda e: e.nop(),
            tuple((t, em.now(t)) for t in dma_tags), False, tag="ynop")

    DBG.clear()
    DBG['em'] = em
    DBG.update(dict(PX=PX.ap(), PXb=PXb, PA=PA.ap(), PB=PB.ap(), PO=PO.ap(),
                    PD=PD.ap(), SMF=SMF.ap(), SMB=SMB.ap(), SCR=SCR.ap(),
                    ROWS=ROWS.ap(), WS=WS.ap(), SM2=SM2.ap(), ps0=ps0.ap(), ps1=ps1.ap()))
    all_tags = sorted({t for (_, _, _, _, t) in em.steps} |
                      {we for (_, _, ws, _, _) in em.steps for (we, _) in ws})
    sems = {t: es.enter_context(nc.semaphore(f"sem{t}")) for t in all_tags}
    eng_map = {"T": "tensor", "V": "vector", "S": "scalar", "Y": "sync", "G": "gpsimd"}
    with nc.Block() as block:
        def mk(engkey):
            def body(e):
                hw = dict.fromkeys(all_tags, 0)
                for eng, fn, waits, n, t in em.steps:
                    if eng != engkey:
                        continue
                    for (we, wv) in waits:
                        if isinstance(wv, str):
                            raise RuntimeError("unresolved wait " + wv)
                        if wv > hw[we]:
                            e.wait_ge(sems[we], wv)
                            hw[we] = wv
                    inst = fn(e)
                    if n:
                        inst.then_inc(sems[t], n)
            return body
        for k, name in eng_map.items():
            getattr(block, name)(mk(k))

    es.close()
    return nc


def kernel(**inp):
    f32 = np.float32
    x = np.asarray(inp["L0_attn_x"], f32)
    hh0, hh1 = _make_hh()
    sc_ = np.asarray(inp["L0_attn_spatial_centers"], f32)
    csT = np.tile(sc_[0].reshape(2, 1), (1, 1152)).astype(f32)
    csT[:, 1:1025] = sc_.T
    shared = {
        "wqkvT0": np.ascontiguousarray(np.asarray(inp["L0_attn_W_qkv"], f32).T),
        "wprojT0": np.ascontiguousarray(np.asarray(inp["L0_attn_W_proj"], f32).T),
        "bproj0": _col(inp["L0_attn_b_proj"], 8),
        "gamma0": _col(inp["L0_attn_ln_gamma"], 8),
        "beta0": _col(inp["L0_attn_ln_beta"], 8),
        "csT": csT,
        "w1T0": np.ascontiguousarray(np.asarray(inp["L0_ffn_w1"], f32).T),
        "cw0": np.concatenate([_col(np.asarray(inp["L0_ffn_conv_w"], f32)[:, 0, t], 32)
                               for t in range(3)], axis=1),
        "cb0": _col(inp["L0_ffn_conv_b"], 32),
        "w2T0": np.ascontiguousarray(np.asarray(inp["L0_ffn_w2"], f32).T),
        "wqkvT1": np.ascontiguousarray(np.asarray(inp["L1_attn_W_attn"], f32).T),
        "battn1": _col(np.asarray(inp["L1_attn_b_attn"], f32)[0:2048], 16),
        "bv1": np.ascontiguousarray(np.asarray(inp["L1_attn_b_attn"], f32)[2048:3072]
                                    .reshape(1, 1024)),
        "wprojT1": np.ascontiguousarray(np.asarray(inp["L1_attn_W_proj"], f32).T),
        "bproj1": _col(inp["L1_attn_b_proj"], 8),
        "relemb": np.asarray(inp["L1_attn_rel_emb"], f32),
        "w1T1": np.ascontiguousarray(np.asarray(inp["L1_ffn_w1"], f32).T),
        "cw1": np.concatenate([_col(np.asarray(inp["L1_ffn_conv_w"], f32)[:, 0, t], 32)
                               for t in range(3)], axis=1),
        "cb1": _col(inp["L1_ffn_conv_b"], 32),
        "w2T1": np.ascontiguousarray(np.asarray(inp["L1_ffn_w2"], f32).T),
        "ident": np.eye(128, dtype=ml_dtypes.bfloat16),
        "identf": np.eye(128, dtype=np.float32),
        "hh0": hh0, "hh1": hh1,
    }
    nc = build(bass.Bass("TRN2", target_bir_lowering=False,
                         detect_race_conditions=RACECHECK), stage=STAGE)
    in_maps = []
    for b in range(8):
        m = dict(shared)
        m["xT"] = np.ascontiguousarray(x[b].T)
        in_maps.append(m)
    res = run_bass_kernel_spmd(nc, in_maps, core_ids=list(range(8)))
    return np.stack([np.asarray(r["yT"], f32).T for r in res.results], axis=0)
